# version 07
# baseline (speedup 1.0000x reference)
import numpy as np
import ml_dtypes
import contextlib
import concourse.bass as bass
import concourse.mybir as mybir

F32 = mybir.dt.float32
BF16 = mybir.dt.bfloat16
I32 = mybir.dt.int32
AF = mybir.ActivationFunctionType
OP = mybir.AluOpType
AX = mybir.AxisListType
bf16 = ml_dtypes.bfloat16

H, DSSM, CONV, PROJ = 4096, 8192, 8448, 16704
NH, Q, S = 64, 128, 2048
NCH = S // Q
KCONV = 4
EPS = 1e-5
HB = S // 2
NZB = DSSM // 128
NXB = CONV // 128
NKB = H // 128
NGRP = 16
GW = 512


def build(debug=False):
    nc = bass.Bass("TRN2", target_bir_lowering=False)
    es = contextlib.ExitStack()

    def dram(name, shape, dt, kind):
        return nc.dram_tensor(name, shape, dt, kind=kind).ap()

    def sbuf(name, shape, dt):
        return es.enter_context(nc.sbuf_tensor(name, shape, dt))

    EI, EO = "ExternalInput", "ExternalOutput"
    SCR = EO if debug else "Internal"

    x_h = dram("x", [S, H], F32, EI)
    win_h = dram("w_in", [H, PROJ], F32, EI)
    convw_h = dram("conv_w_t", [128, NXB * KCONV], F32, EI)
    dtb_h = dram("dtb", [1, NH], F32, EI)
    alog_h = dram("alog", [1, NH], F32, EI)
    dpar_h = dram("dpar", [1, NH], F32, EI)
    nwt_h = dram("nwt", [128, NZB], F32, EI)
    wout_h = dram("w_out", [DSSM, H], F32, EI)
    wft_h = dram("wf_t", [H, H], F32, EI)
    biast_h = dram("bias_t", [128, 32], F32, EI)
    lat_h = dram("la_t", [H, 48], F32, EI)
    lbt_h = dram("lb_t", [48, H], F32, EI)
    lsi_h = dram("lsi", [1, 4], I32, EI)
    bval_h = dram("bval", [1, 1], F32, EI)

    xn_sp = dram("xn_sp", [S, H], BF16, SCR)
    zs_sp = dram("zs_sp", [DSSM, S], BF16, SCR)
    xs_sp = dram("xs_sp", [CONV, S], BF16, SCR)
    yr_sp = dram("yr_sp", [S, DSSM], BF16, SCR)
    at_sp = dram("at_sp", [H, S], BF16, SCR)
    x2_sp = dram("x2_sp", [S, H], F32, SCR)
    xn2_sp = dram("xn2_sp", [S, H], BF16, SCR)
    ff_sp = dram("ff_sp", [H, S], BF16, SCR)
    rawb = dram("rawb", [256, S], F32, SCR)
    out_h = dram("out", [S, H], F32, EO)

    ident_f = sbuf("ident_f", [128, 128], F32)
    ident_b = sbuf("ident_b", [128, 128], BF16)
    ones_row = sbuf("ones_row", [1, 128], F32)
    ones64 = sbuf("ones64", [64, 128], F32)
    convw_s = sbuf("convw_s", [128, NXB * KCONV], F32)
    dtb_r = sbuf("dtb_r", [1, NH], F32)
    alog_r = sbuf("alog_r", [1, NH], F32)
    dpar_r = sbuf("dpar_r", [1, NH], F32)
    nwt_s = sbuf("nwt_s", [128, NZB], F32)
    lsi_s = sbuf("lsi_s", [1, 4], I32)
    lsi_f = sbuf("lsi_f", [1, 4], F32)
    bval_s = sbuf("bval_s", [1, 1], F32)
    dtb_c = sbuf("dtb_c", [NH, 1], F32)
    a_c = sbuf("a_c", [NH, 1], F32)
    m48 = sbuf("m48", [48, 1], F32)
    mrow = sbuf("mrow", [1, 48], F32)
    g4 = sbuf("g4", [1, 4], F32)
    g4b = sbuf("g4b", [1, 4], F32)
    sc1 = sbuf("sc1", [1, 8], F32)
    dt_fm = sbuf("dt_fm", [NH, S], F32)
    eps_c = sbuf("eps_c", [128, 1], F32)
    warm_i = sbuf("warm_i", [128, 256], F32)
    warm_o = sbuf("warm_o", [128, 256], F32)

    sems = {}

    def sem(name):
        if name not in sems:
            class _S:
                pass
            s = _S()
            s.h = es.enter_context(nc.semaphore(name))
            s.n = 0
            sems[name] = s
        return sems[name]

    with nc.Block() as blk:
        @blk.gpsimd
        def _(e):
            e.dma_reset(range(152, 256))
            e.sem_clear(range(152, 256))
    nc.all_engine_barrier()

    ld0, v0, s0, t0_ = sem("ld0"), sem("v0"), sem("s0"), sem("t0")
    with nc.Block() as blk:
        @blk.sync
        def _(e):
            for dst, src in [(convw_s, convw_h), (dtb_r, dtb_h), (alog_r, alog_h),
                             (dpar_r, dpar_h), (nwt_s, nwt_h), (lsi_s, lsi_h),
                             (bval_s, bval_h)]:
                e.dma_start(dst[:, :], src[:, :]).then_inc(ld0.h, 16)
        ld0.n += 16 * 7

        @blk.gpsimd
        def _(e):
            e.memset(ident_f[:, :], 1.0)
            e.affine_select(ident_f[:, :], ident_f[:, :], [[-1, 128]],
                            OP.is_equal, 0.0, base=0, channel_multiplier=1)
            e.memset(ident_b[:, :], 1.0)
            e.affine_select(ident_b[:, :], ident_b[:, :], [[-1, 128]],
                            OP.is_equal, 0.0, base=0, channel_multiplier=1)
            e.memset(ones_row[:, :], 1.0)
            e.memset(eps_c[:, :], EPS)
            e.memset(warm_i[:, :], 0.5)
            e.memset(ones64[:, :], 1.0).then_inc(v0.h, 1)
        v0.n += 1

        @blk.vector
        def _(e):
            e.wait_ge(ld0.h, 16 * 7)
            e.wait_ge(v0.h, 1)

            def spacer():
                e.tensor_scalar(g4[0:1, 0:4], ones64[0:1, 0:4], 0.0, None, OP.mult)
            e.tensor_copy(lsi_f[:, :], lsi_s[:, :])
            spacer()
            e.tensor_scalar(sc1[0:1, 0:4], lsi_f[:, :], -1.0, 0.5, OP.mult, OP.add)
            spacer()
            e.tensor_scalar(sc1[0:1, 4:8], sc1[0:1, 0:4], bval_s[0:1, 0:1], None,
                            OP.add).then_inc(v0.h, 1)
        v0.n += 1

        @blk.scalar
        def _(e):
            e.wait_ge(v0.h, v0.n)
            for fn in (AF.Silu, AF.Square, AF.Abs, AF.Relu, AF.Sign, AF.Exp,
                       AF.Ln, AF.Copy):
                e.activation(mrow[0:1, 0:1], eps_c[0:1, 0:1], fn)
            e.activation(g4[:, :], sc1[0:1, 4:8], AF.Sign)
            e.activation(g4b[:, :], g4[:, :], AF.Copy, scale=0.5,
                         bias=0.5).then_inc(s0.h, 1)
        s0.n += 1

        @blk.vector
        def _(e):
            e.wait_ge(s0.h, s0.n)

            def spacer():
                e.tensor_scalar(g4[0:1, 0:4], ones64[0:1, 0:4], 0.0, None, OP.mult)
            e.tensor_scalar(sc1[0:1, 0:1], g4b[0:1, 3:4], 1.0, 3.0, OP.mult, OP.add)
            e.tensor_scalar(sc1[0:1, 2:3], g4b[0:1, 1:2], -1.0, 1.0, OP.mult, OP.add)
            e.tensor_scalar(sc1[0:1, 4:5], g4b[0:1, 1:2], 2.0, None, OP.mult)
            e.tensor_scalar(sc1[0:1, 5:6], g4b[0:1, 2:3], -1.0, 1.0, OP.mult, OP.add)
            e.tensor_tensor(sc1[0:1, 1:2], sc1[0:1, 0:1], g4b[0:1, 2:3], OP.mult)
            e.tensor_tensor(sc1[0:1, 3:4], sc1[0:1, 2:3], g4b[0:1, 0:1], OP.mult)
            spacer()
            e.tensor_tensor(sc1[0:1, 4:5], sc1[0:1, 4:5], sc1[0:1, 3:4], OP.add)
            spacer()
            e.tensor_tensor(sc1[0:1, 5:6], sc1[0:1, 5:6], sc1[0:1, 4:5], OP.mult)
            spacer()
            e.tensor_tensor(sc1[0:1, 6:7], sc1[0:1, 1:2], sc1[0:1, 5:6], OP.add)
            spacer()
            i = None
            for k in range(3):
                e.tensor_scalar(sc1[0:1, 7:8], sc1[0:1, 6:7], -(k + 2.0), None,
                                OP.add)
                spacer()
                e.tensor_tensor(sc1[0:1, 7:8], sc1[0:1, 7:8], sc1[0:1, 7:8], OP.mult)
                spacer()
                e.tensor_scalar(sc1[0:1, 7:8], sc1[0:1, 7:8], -1.0, 1.0, OP.mult,
                                OP.add)
                spacer()
                e.tensor_relu(sc1[0:1, 7:8], sc1[0:1, 7:8])
                spacer()
                i = e.tensor_scalar(mrow[0:1, k * 16:(k + 1) * 16],
                                    ones_row[0:1, 0:16], sc1[0:1, 7:8], None,
                                    OP.mult)
            i.then_inc(v0.h, 1)
        v0.n += 1

    nc.all_engine_barrier()

    with contextlib.ExitStack() as ph:
        ps0 = ph.enter_context(nc.psum_tensor("ps0", [128, 512], F32))
        with nc.Block() as blk:
            @blk.tensor
            def _(e):
                e.matmul(ps0[0:64, 0:1], dtb_r[0:1, 0:64], ones_row[0:1, 0:1],
                         start=True, stop=True)
                e.matmul(ps0[0:64, 1:2], alog_r[0:1, 0:64], ones_row[0:1, 0:1],
                         start=True, stop=True)
                e.matmul(ps0[0:48, 2:3], mrow[0:1, 0:48], ones_row[0:1, 0:1],
                         start=True, stop=True).then_inc(t0_.h, 1)
            t0_.n += 1

            @blk.vector
            def _(e):
                e.wait_ge(t0_.h, t0_.n)
                e.tensor_copy(dtb_c[0:64, 0:1], ps0[0:64, 0:1])
                e.tensor_copy(m48[0:48, 0:1], ps0[0:48, 2:3]).then_inc(v0.h, 1)
            v0.n += 1

            @blk.scalar
            def _(e):
                e.wait_ge(v0.h, v0.n)
                e.activation(a_c[0:64, 0:1], ps0[0:64, 1:2],
                             AF.Exp).then_inc(s0.h, 1)
            s0.n += 1
        nc.all_engine_barrier()

    ld1, v1, s1, sp1 = sem("ld1"), sem("v1"), sem("s1"), sem("sp1")
    with contextlib.ExitStack() as ph:
        def psb(name, shape, dt):
            return ph.enter_context(nc.sbuf_tensor(name, shape, dt))
        x_t = [psb(f"x_t{i}", [128, H], F32) for i in range(2)]
        sq_t = [psb(f"sq_t{i}", [128, H], F32) for i in range(2)]
        xn_t = [psb(f"xn_t{i}", [128, H], BF16) for i in range(2)]
        st_t = psb("st_t", [128, 8], F32)

        with nc.Block() as blk:
            @blk.sync
            def _(e):
                for tc in range(NCH):
                    if tc >= 2:
                        e.wait_ge(v1.h, 2 * (tc - 2) + 2)
                        e.wait_ge(s1.h, 2 * (tc - 2) + 1)
                    e.dma_start(x_t[tc % 2][:, :],
                                x_h[tc * 128:(tc + 1) * 128, :]).then_inc(ld1.h, 16)
            ld1.n += 16 * NCH

            @blk.scalar
            def _(e):
                for tc in range(NCH):
                    e.wait_ge(ld1.h, 16 * (tc + 1))
                    if tc >= 2:
                        e.wait_ge(v1.h, 2 * (tc - 2) + 1)
                    e.activation(sq_t[tc % 2][:, :], x_t[tc % 2][:, :],
                                 AF.Square).then_inc(s1.h, 1)
                    e.wait_ge(v1.h, 2 * tc + 1)
                    e.activation(st_t[:, 6:7], st_t[:, 4:5], AF.Ln, scale=-1.0,
                                 bias=eps_c[:, 0:1])
                    e.activation(st_t[:, 7:8], st_t[:, 6:7], AF.Exp,
                                 scale=-0.5).then_inc(s1.h, 1)
            s1.n += 2 * NCH

            @blk.vector
            def _(e):
                for tc in range(NCH):
                    e.wait_ge(ld1.h, 16 * (tc + 1))
                    e.reduce_sum(st_t[:, 0:1], x_t[tc % 2][:, :], axis=AX.X)
                    e.wait_ge(s1.h, 2 * tc + 1)
                    e.reduce_sum(st_t[:, 1:2], sq_t[tc % 2][:, :], axis=AX.X)
                    e.tensor_scalar(st_t[:, 2:3], st_t[:, 0:1], 1.0 / H, None,
                                    OP.mult)
                    e.tensor_scalar(st_t[:, 3:4], st_t[:, 1:2], 1.0 / H, None,
                                    OP.mult)
                    e.tensor_scalar(st_t[:, 0:1], st_t[:, 0:1], 0.0, None, OP.mult)
                    e.scalar_tensor_tensor(st_t[:, 4:5], st_t[:, 2:3],
                                           st_t[:, 2:3], st_t[:, 3:4],
                                           OP.mult, OP.subtract).then_inc(v1.h, 1)
                    e.wait_ge(s1.h, 2 * tc + 2)
                    if tc >= 2:
                        e.wait_ge(sp1.h, 16 * (tc - 1))
                    e.tensor_scalar(xn_t[tc % 2][:, :], x_t[tc % 2][:, :],
                                    st_t[:, 2:3], st_t[:, 7:8],
                                    OP.subtract, OP.mult).then_inc(v1.h, 1)
            v1.n += 2 * NCH

            @blk.gpsimd
            def _(e):
                for tc in range(NCH):
                    e.wait_ge(v1.h, 2 * tc + 2)
                    e.dma_start(xn_sp[tc * 128:(tc + 1) * 128, :],
                                xn_t[tc % 2][:, :]).then_inc(sp1.h, 16)
                e.wait_ge(sp1.h, 16 * NCH)
            sp1.n += 16 * NCH
        nc.all_engine_barrier()

    CB_ORDER = list(range(64, 131)) + list(range(64))
    ldT, ldW = sem("a_ldT"), sem("a_ldW")
    tdA, evA, cvA, siA = sem("a_td"), sem("a_ev"), sem("a_cv"), sem("a_si")
    spX, spZ = sem("a_spx"), sem("a_spz")
    with contextlib.ExitStack() as ph:
        def psb(name, shape, dt):
            return ph.enter_context(nc.sbuf_tensor(name, shape, dt))
        xn_fm = psb("xn_fm", [128, NKB * HB], BF16)
        wb = [psb(f"wb{i}", [128, NKB * 128], BF16) for i in range(2)]
        conv_in = psb("conv_in", [128, HB + 3], F32)
        scr1 = psb("scr1", [128, HB], F32)
        scr2 = [psb(f"scr2_{i}", [128, HB], F32) for i in range(2)]
        xs_t = [psb(f"xs_t{i}", [128, HB], BF16) for i in range(2)]
        zs_t = [psb(f"zs_t{i}", [128, HB], BF16) for i in range(2)]
        bsave = psb("bsave", [128, NXB * 3], F32)
        rawdbg_t = [psb(f"rawdbg{i}", [128, HB], F32) for i in range(2)] if debug else None
        pa_ps = [ph.enter_context(nc.psum_tensor(f"pa_ps{i}", [128, 1024], F32))
                 for i in range(2)]

        with nc.Block() as blk:
            @blk.vector
            def _(e):
                e.memset(conv_in[:, 0:3], 0.0)
        nc.all_engine_barrier()

        for hf in range(2):
            t0 = hf * HB
            b_ldT, b_ldW, b_td, b_ev = ldT.n, ldW.n, tdA.n, evA.n
            b_cv, b_si, b_spx, b_spz = cvA.n, siA.n, spX.n, spZ.n

            with nc.Block() as blk:
                @blk.sync
                def _(e):
                    for k in range(NKB):
                        e.dma_start(xn_fm[:, k * HB:(k + 1) * HB],
                                    xn_sp[t0:t0 + HB, k * 128:(k + 1) * 128],
                                    transpose=True).then_inc(ldT.h, 16)
                ldT.n += 16 * NKB

                @blk.gpsimd
                def _(e):
                    for i, cb in enumerate(CB_ORDER):
                        cw = 64 if cb == 130 else 128
                        if i >= 2:
                            e.wait_ge(tdA.h, b_td + i - 1)
                        e.dma_start(
                            wb[i % 2][:, 0:NKB * cw].rearrange(
                                "p (k c) -> p k c", k=NKB),
                            win_h[:, cb * 128:cb * 128 + cw].rearrange(
                                "(k p) c -> p k c", p=128),
                        ).then_inc(ldW.h, 16)
                ldW.n += 16 * len(CB_ORDER)

                @blk.tensor
                def _(e):
                    e.wait_ge(ldT.h, b_ldT + 16 * NKB)
                    for i, cb in enumerate(CB_ORDER):
                        cw = 64 if cb == 130 else 128
                        e.wait_ge(ldW.h, b_ldW + 16 * (i + 1))
                        if i >= 2:
                            e.wait_ge(evA.h, b_ev + i - 1)
                        ps = pa_ps[i % 2]
                        mi = None
                        for k in range(NKB):
                            for n in range(2):
                                mi = e.matmul(
                                    ps[0:cw, n * 512:(n + 1) * 512],
                                    wb[i % 2][:, k * cw:(k + 1) * cw],
                                    xn_fm[:, k * HB + n * 512:
                                          k * HB + (n + 1) * 512],
                                    start=(k == 0), stop=(k == NKB - 1),
                                )
                        mi.then_inc(tdA.h, 1)
                tdA.n += len(CB_ORDER)

                @blk.vector
                def _(e):
                    nxv = 0
                    for i, cb in enumerate(CB_ORDER):
                        ps = pa_ps[i % 2]
                        if 64 <= cb <= 129:
                            blkx = cb - 64
                            e.wait_ge(tdA.h, b_td + i + 1)
                            if hf == 1:
                                e.tensor_copy(conv_in[:, 0:3],
                                              bsave[:, blkx * 3:(blkx + 1) * 3])
                            e.tensor_copy(conv_in[:, 3:3 + HB],
                                          ps[:, 0:HB]).then_inc(evA.h, 1)
                            if debug and blkx < 2:
                                e.tensor_copy(rawdbg_t[blkx][:, :],
                                              conv_in[:, 3:3 + HB])
                            wc = [convw_s[:, blkx * 4 + k:blkx * 4 + k + 1]
                                  for k in range(4)]
                            e.tensor_scalar(scr1[:, :], conv_in[:, 0:HB], wc[0],
                                            None, OP.mult)
                            if nxv >= 2:
                                e.wait_ge(siA.h, b_si + nxv - 1)
                            e.scalar_tensor_tensor(scr2[nxv % 2][:, :],
                                                   conv_in[:, 1:1 + HB],
                                                   wc[1], scr1[:, :], OP.mult,
                                                   OP.add)
                            e.scalar_tensor_tensor(scr1[:, :], conv_in[:, 2:2 + HB],
                                                   wc[2], scr2[nxv % 2][:, :],
                                                   OP.mult, OP.add)
                            if hf == 0:
                                e.tensor_copy(bsave[:, blkx * 3:(blkx + 1) * 3],
                                              conv_in[:, HB:HB + 3])
                            e.scalar_tensor_tensor(scr2[nxv % 2][:, :],
                                                   conv_in[:, 3:3 + HB],
                                                   wc[3], scr1[:, :], OP.mult,
                                                   OP.add).then_inc(cvA.h, 1)
                            nxv += 1
                        elif cb == 130:
                            e.wait_ge(tdA.h, b_td + i + 1)
                            e.tensor_copy(dt_fm[0:64, t0:t0 + HB],
                                          ps[0:64, 0:HB]).then_inc(evA.h, 1)
                cvA.n += NXB

                @blk.scalar
                def _(e):
                    for _w in range(6):
                        e.activation(warm_o[:, :], warm_i[:, :], AF.Silu)
                    nxs = 0
                    nzs = 0
                    for i, cb in enumerate(CB_ORDER):
                        ps = pa_ps[i % 2]
                        if 64 <= cb <= 129:
                            e.wait_ge(cvA.h, b_cv + nxs + 1)
                            if nxs >= 2:
                                e.wait_ge(spX.h, b_spx + 16 * (nxs - 1))
                            e.activation(xs_t[nxs % 2][:, :], scr2[nxs % 2][:, :],
                                         AF.Silu).then_inc(siA.h, 1)
                            e.dma_start(
                                xs_sp[(cb - 64) * 128:(cb - 63) * 128, t0:t0 + HB],
                                xs_t[nxs % 2][:, :]).then_inc(spX.h, 16)
                            nxs += 1
                        elif cb <= 63:
                            e.wait_ge(tdA.h, b_td + i + 1)
                            if nzs >= 2:
                                e.wait_ge(spZ.h, b_spz + 16 * (nzs - 1))
                            e.activation(zs_t[nzs % 2][:, :], ps[:, 0:HB],
                                         AF.Silu).then_inc(evA.h, 1)
                            e.dma_start(
                                zs_sp[cb * 128:(cb + 1) * 128, t0:t0 + HB],
                                zs_t[nzs % 2][:, :]).then_inc(spZ.h, 16)
                            nzs += 1
                    e.wait_ge(spX.h, b_spx + 16 * NXB)
                    e.wait_ge(spZ.h, b_spz + 16 * NZB)
                    if debug:
                        e.dma_start(rawb[0:128, t0:t0 + HB],
                                    rawdbg_t[0][:, :]).then_inc(spZ.h, 16)
                        e.dma_start(rawb[128:256, t0:t0 + HB],
                                    rawdbg_t[1][:, :]).then_inc(spZ.h, 16)
                        e.wait_ge(spZ.h, b_spz + 16 * NZB + 32)
                siA.n += NXB
                spX.n += 16 * NXB
                spZ.n += 16 * NZB + (32 if debug else 0)
                evA.n += NXB + 1 + NZB

            nc.all_engine_barrier()

    es34 = contextlib.ExitStack()

    def sb34(name, shape, dt):
        return es34.enter_context(nc.sbuf_tensor(name, shape, dt))
    lndt_b = sb34("lndt_b", [NH, S], BF16)
    nA = sb34("nA", [NH, S], BF16)
    egi = sb34("egi", [NH, S], BF16)
    wst = sb34("wst", [NH, S], BF16)
    strict_up = sb34("strict_up", [128, 128], BF16)
    negtri = sb34("negtri", [128, 128], BF16)
    tri01 = sb34("tri01", [128, 128], F32)
    sel64 = sb34("sel64", [64, DSSM], BF16)
    d_i = sb34("d_i", [128, DSSM], BF16)

    v3, s3, g3_, t3_ = sem("v3"), sem("s3"), sem("g3"), sem("t3")
    with contextlib.ExitStack() as ph:
        def psb(name, shape, dt):
            return ph.enter_context(nc.sbuf_tensor(name, shape, dt))
        tmp64 = psb("tmp64", [NH, S], F32)
        ta = psb("ta", [NH, S], F32)
        pa_f = psb("pa_f", [NH, S], F32)
        d_tile = psb("d_tile", [128, NH], F32)
        ps3 = ph.enter_context(nc.psum_tensor("ps3", [128, 512], F32))

        with nc.Block() as blk:
            @blk.gpsimd
            def _(e):
                e.memset(strict_up[:, :], 1.0)
                e.affine_select(strict_up[:, :], strict_up[:, :], [[-1, 128]],
                                OP.is_gt, 0.0, base=0, channel_multiplier=1)
                e.memset(negtri[:, :], 0.0)
                e.affine_select(negtri[:, :], negtri[:, :], [[-1, 128]],
                                OP.is_gt, -1.0, base=0, channel_multiplier=1)
                e.memset(tri01[:, :], 0.0)
                e.affine_select(tri01[:, :], tri01[:, :], [[-1, 128]],
                                OP.is_gt, 1.0, base=0, channel_multiplier=1)
                e.memset(sel64[:, :], 1.0)
                e.affine_select(sel64[:, :], sel64[:, :], [[-1, 64], [0, 128]],
                                OP.is_equal, 0.0, base=0,
                                channel_multiplier=1).then_inc(g3_.h, 1)
            g3_.n += 1

            @blk.vector
            def _(e):
                e.tensor_scalar(tmp64[:, :], dt_fm[:, :], dtb_c[0:64, 0:1], None,
                                OP.add).then_inc(v3.h, 1)
            v3.n += 1

            @blk.scalar
            def _(e):
                for _w in range(6):
                    e.activation(warm_o[:, :], warm_i[:, :], AF.Exp)
                e.wait_ge(v3.h, v3.n)
                e.activation(ta[:, :], tmp64[:, :], AF.Abs)
                e.activation(ta[:, :], ta[:, :], AF.Exp, scale=-1.0)
                e.activation(ta[:, :], ta[:, :], AF.Ln, bias=1.0)
                e.activation(pa_f[:, :], tmp64[:, :], AF.Relu).then_inc(s3.h, 1)
            s3.n += 1

            @blk.vector
            def _(e):
                e.wait_ge(s3.h, s3.n)
                e.tensor_tensor(tmp64[:, :], pa_f[:, :], ta[:, :],
                                OP.add).then_inc(v3.h, 1)
            v3.n += 1

            @blk.scalar
            def _(e):
                e.wait_ge(v3.h, v3.n)
                e.activation(lndt_b[:, :], tmp64[:, :], AF.Ln).then_inc(s3.h, 1)
            s3.n += 1

            @blk.vector
            def _(e):
                e.wait_ge(s3.h, s3.n)
                e.tensor_scalar(nA[:, :], tmp64[:, :], a_c[0:64, 0:1], None,
                                OP.mult)
                i = None
                for c in range(NCH):
                    sl = slice(c * Q, (c + 1) * Q)
                    i = e.tensor_tensor_scan(pa_f[:, sl], ones64[:, :], nA[:, sl],
                                             0.0, OP.mult, OP.add)
                i.then_inc(v3.h, 1)
            v3.n += 1

            @blk.scalar
            def _(e):
                e.wait_ge(v3.h, v3.n)
                e.activation(egi[:, :], pa_f[:, :], AF.Exp,
                             scale=-1.0).then_inc(s3.h, 1)
            s3.n += 1

            @blk.vector
            def _(e):
                e.wait_ge(s3.h, s3.n)
                i = None
                for c in range(NCH):
                    sl = slice(c * Q, (c + 1) * Q)
                    i = e.tensor_scalar(ta[:, sl], pa_f[:, sl],
                                        pa_f[:, c * Q + Q - 1:c * Q + Q], None,
                                        OP.subtract)
                i.then_inc(v3.h, 1)
            v3.n += 1

            @blk.scalar
            def _(e):
                e.wait_ge(v3.h, v3.n)
                e.activation(pa_f[:, :], ta[:, :], AF.Exp).then_inc(s3.h, 1)
            s3.n += 1

            @blk.vector
            def _(e):
                e.wait_ge(s3.h, s3.n)
                e.tensor_tensor(wst[:, :], pa_f[:, :], tmp64[:, :],
                                OP.mult).then_inc(v3.h, 1)
            v3.n += 1

            @blk.tensor
            def _(e):
                e.matmul(ps3[0:128, 0:64], ones_row[0:1, 0:128], dpar_r[0:1, 0:64],
                         start=True, stop=True).then_inc(t3_.h, 1)
            t3_.n += 1

            @blk.vector
            def _(e):
                e.wait_ge(t3_.h, t3_.n)
                e.tensor_copy(d_tile[:, 0:64], ps3[0:128, 0:64])
                e.tensor_tensor(
                    d_i[:, :].rearrange("p (h q) -> p h q", h=64),
                    ident_b[:, :].unsqueeze(1).broadcast_to([128, 64, 128]),
                    d_tile[:, 0:64].unsqueeze(2).broadcast_to([128, 64, 128]),
                    OP.mult).then_inc(v3.h, 1)
            v3.n += 1
        nc.all_engine_barrier()

    sT, sV, sA, sV1, sP = sem("sc_t"), sem("sc_v"), sem("sc_a"), sem("sc_v1"), sem("sc_p")
    ldX, ldZ, ldB, spY = sem("sc_ldx"), sem("sc_ldz"), sem("sc_ldb"), sem("sc_spy")
    VPC, TPC, APC, PPC = 57, 71, 50, 32
    with contextlib.ExitStack() as ph:
        def psb(name, shape, dt):
            return ph.enter_context(nc.sbuf_tensor(name, shape, dt))
        S_bf = psb("S_bf", [128, DSSM], BF16)
        X_tm = [psb(f"X_tm{i}", [128, DSSM], BF16) for i in range(2)]
        Zs_tm = [psb(f"Zs_tm{i}", [128, DSSM // 2], BF16) for i in range(2)]
        yg = psb("yg", [128, DSSM], BF16)
        yrms_t = psb("yrms_t", [128, DSSM], BF16)
        B_c = [psb(f"B_c{i}", [128, 128], BF16) for i in range(2)]
        C_c = [psb(f"C_c{i}", [128, 128], BF16) for i in range(2)]
        B_T = psb("B_T", [128, 128], BF16)
        S_cbT = psb("S_cbT", [128, 128], BF16)
        nA_T = psb("nA_T", [128, 64], F32)
        egi_T = psb("egi_T", [128, 64], F32)
        w_T = psb("w_T", [128, 64], F32)
        eG_row = psb("eG_row", [1, 64], F32)
        eG_tile = psb("eG_tile", [128, 64], F32)
        rhs_g = [psb(f"rhs_g{i}", [128, GW], BF16) for i in range(4)]
        E_t = [psb(f"E_t{i}", [128, GW], BF16) for i in range(2)]
        M2t = [psb(f"M2t{i}", [128, GW], BF16) for i in range(2)]
        M2 = [psb(f"M2_{i}", [128, GW], BF16) for i in range(2)]
        xw = [psb(f"xw{i}", [128, GW], BF16) for i in range(2)]
        tbuf = [psb(f"tb{i}", [128, GW], BF16) for i in range(2)]
        t2b = [psb(f"t2b{i}", [128, GW], BF16) for i in range(2)]
        sq_scr = psb("sq_scr", [128, GW], F32)
        ss16 = psb("ss16", [128, 16], F32)
        ss_c = psb("ss_c", [128, 2], F32)
        PS_D = [ph.enter_context(nc.psum_tensor(f"psD{i}", [128, GW], F32))
                for i in range(2)]
        PS_I = [ph.enter_context(nc.psum_tensor(f"psI{i}", [128, GW], F32))
                for i in range(2)]
        PS_Y = [ph.enter_context(nc.psum_tensor(f"psY{i}", [128, GW], F32))
                for i in range(2)]
        PS_S = [ph.enter_context(nc.psum_tensor(f"psS{i}", [128, GW], F32))
                for i in range(2)]

        with nc.Block() as blk:
            @blk.gpsimd
            def _(e):
                e.memset(S_bf[:, :], 0.0)
        nc.all_engine_barrier()

        VORD = [("pre", k) for k in range(1, 8)]
        for _i in range(NGRP + 2):
            if _i < NGRP:
                VORD.append(("V5", _i))
            if 1 <= _i <= NGRP:
                VORD.append(("V3", _i - 1))
                VORD.append(("V4", _i - 1))
            if 2 <= _i <= NGRP + 1:
                VORD.append(("V7", _i - 2))
        VORD += [("ss", 0), ("yr", 0)]
        VPOS = {k: i + 1 for i, k in enumerate(VORD)}
        VPC = len(VORD)
        AORD = []
        for _i in range(NGRP + 2):
            if _i < NGRP:
                AORD.append(("A1", _i))
            if 1 <= _i <= NGRP:
                AORD.append(("A4", _i - 1))
            if 2 <= _i <= NGRP + 1:
                AORD.append(("A2", _i - 2))
        AORD += [("ln", 0), ("ex", 0)]
        APOS = {k: i + 1 for i, k in enumerate(AORD)}
        APC = len(AORD)
        TPC = 71
        PPC = NGRP

        def _norm(c, g):
            while g < 0:
                c -= 1
                g += NGRP
            return (c, g) if c >= 0 else None

        def tT(c, g, j):
            return c * TPC + 7 + 4 * g + j

        def tPre(c, k):
            return c * TPC + k

        def vI(c, kind, g=0):
            return c * VPC + VPOS[(kind, g)]

        def aI(c, kind, g=0):
            return c * APC + APOS[(kind, g)]

        def pI(c, g):
            return c * PPC + g + 1

        def v1i(c, g):
            return c * NGRP + g + 1

        with nc.Block() as blk:
            @blk.sync
            def _(e):
                e.dma_start(X_tm[0][:, :], xs_sp[0:DSSM, 0:128],
                            transpose=True).then_inc(ldX.h, 16)
                e.dma_start(B_c[0][:, :],
                            xs_sp[DSSM:DSSM + 128, 0:128]).then_inc(ldB.h, 16)
                e.dma_start(C_c[0][:, :],
                            xs_sp[DSSM + 128:DSSM + 256, 0:128]).then_inc(ldB.h, 16)
                for c in range(NCH):
                    sl = slice(c * Q, (c + 1) * Q)
                    if c >= 1:
                        e.wait_ge(sV.h, vI(c - 1, "yr"))
                        e.dma_start(yr_sp[(c - 1) * Q:c * Q, :],
                                    yrms_t[:, :]).then_inc(spY.h, 16)
                        e.wait_ge(sV.h, vI(c - 1, "V7", 7))
                    e.dma_start(Zs_tm[0][:, :], zs_sp[0:DSSM // 2, sl],
                                transpose=True).then_inc(ldZ.h, 16)
                    if c >= 1:
                        e.wait_ge(sV.h, vI(c - 1, "V7", 15))
                    e.dma_start(Zs_tm[1][:, :], zs_sp[DSSM // 2:DSSM, sl],
                                transpose=True).then_inc(ldZ.h, 16)
                    if c + 1 < NCH:
                        sl1 = slice((c + 1) * Q, (c + 2) * Q)
                        if c >= 1:
                            e.wait_ge(sT.h, tT(c - 1, 15, 4))
                            e.wait_ge(sV.h, vI(c - 1, "V5", 15))
                        e.dma_start(X_tm[(c + 1) % 2][:, :], xs_sp[0:DSSM, sl1],
                                    transpose=True).then_inc(ldX.h, 16)
                        e.dma_start(B_c[(c + 1) % 2][:, :],
                                    xs_sp[DSSM:DSSM + 128, sl1]).then_inc(ldB.h, 16)
                        e.dma_start(C_c[(c + 1) % 2][:, :],
                                    xs_sp[DSSM + 128:DSSM + 256, sl1]).then_inc(
                            ldB.h, 16)
                e.wait_ge(sV.h, vI(NCH - 1, "yr"))
                e.dma_start(yr_sp[(NCH - 1) * Q:NCH * Q, :],
                            yrms_t[:, :]).then_inc(spY.h, 16)
                e.wait_ge(spY.h, 16 * NCH)
            ldX.n += 16 * NCH
            ldZ.n += 32 * NCH
            ldB.n += 32 * NCH
            spY.n += 16 * NCH

            @blk.tensor
            def _(e):
                for c in range(NCH):
                    sl = slice(c * Q, (c + 1) * Q)
                    if c >= 1:
                        e.wait_ge(sA.h, aI(c - 1, "A1", 14))
                        e.wait_ge(sV.h, vI(c - 1, "V4", 14))
                    e.matmul(PS_D[0][0:128, 0:64], nA[:, sl], ident_b[0:64, 0:64],
                             start=True, stop=True).then_inc(sT.h, 1)
                    e.matmul(PS_D[0][0:128, 64:128], egi[:, sl],
                             ident_b[0:64, 0:64], start=True,
                             stop=True).then_inc(sT.h, 1)
                    e.matmul(PS_D[0][0:128, 128:192], wst[:, sl],
                             ident_b[0:64, 0:64], start=True,
                             stop=True).then_inc(sT.h, 1)
                    e.matmul(PS_D[0][0:1, 192:256],
                             egi[:, c * Q + Q - 1:c * Q + Q],
                             ident_b[0:64, 0:64], start=True,
                             stop=True).then_inc(sT.h, 1)
                    e.wait_ge(sV.h, c * VPC + 4)
                    e.matmul(PS_D[0][0:128, 256:320], ones_row[0:1, 0:128],
                             eG_row[0:1, 0:64], start=True,
                             stop=True).then_inc(sT.h, 1)
                    e.wait_ge(ldB.h, 32 * (c + 1))
                    e.matmul(PS_I[0][0:128, 0:128], B_c[c % 2][:, :],
                             C_c[c % 2][:, :], start=True,
                             stop=True).then_inc(sT.h, 1)
                    e.matmul(PS_I[0][0:128, 128:256], B_c[c % 2][:, :],
                             ident_b[:, :], start=True, stop=True).then_inc(sT.h, 1)
                    for g in range(NGRP):
                        p_ = g % 2
                        gsl = slice(g * GW, (g + 1) * GW)
                        pv = _norm(c, g - 2)
                        e.wait_ge(sV1.h, v1i(c, g))
                        if pv:
                            e.wait_ge(sA.h, aI(pv[0], "A1", pv[1]))
                        e.matmul(PS_D[p_][:, :], strict_up[:, :],
                                 rhs_g[g % 4][:, :], start=True, stop=False)
                        e.matmul(PS_D[p_][:, :], lndt_b[:, sl],
                                 sel64[:, g * GW:(g + 1) * GW],
                                 start=False, stop=True).then_inc(sT.h, 1)
                        if pv:
                            e.wait_ge(sV.h, vI(pv[0], "V4", pv[1]))
                        if c >= 1:
                            e.wait_ge(sA.h, aI(c - 1, "A4", g))
                        e.matmul(PS_I[p_][:, :], C_c[c % 2][:, :], S_bf[:, gsl],
                                 start=True, stop=True).then_inc(sT.h, 1)
                        e.wait_ge(sV.h, vI(c, "V4", g))
                        e.matmul(PS_Y[p_][:, :], ident_b[:, :], tbuf[p_][:, :],
                                 start=True, stop=False, skip_group_check=True)
                        mi = None
                        for hh in range(4):
                            mi = e.matmul(PS_Y[p_][:, hh * 128:(hh + 1) * 128],
                                          M2[p_][:, hh * 128:(hh + 1) * 128],
                                          X_tm[c % 2][:, (4 * g + hh) * 128:
                                                      (4 * g + hh + 1) * 128],
                                          start=False, stop=(hh == 3),
                                          skip_group_check=True)
                        mi.then_inc(sT.h, 1)
                        if pv:
                            e.wait_ge(sA.h, aI(pv[0], "A4", pv[1]))
                        e.wait_ge(sV.h, vI(c, "V5", g))
                        e.wait_ge(sP.h, pI(c, g))
                        e.matmul(PS_S[p_][:, :], B_T[:, :], xw[p_][:, :],
                                 start=True, stop=False)
                        e.matmul(PS_S[p_][:, :], ident_b[:, :], t2b[p_][:, :],
                                 start=False, stop=True,
                                 skip_group_check=True).then_inc(sT.h, 1)
            sT.n += TPC * NCH

            @blk.vector
            def _(e):
                def emit_v1(c, g):
                    pv4 = _norm(c, g - 4)
                    if pv4:
                        e.wait_ge(sT.h, tT(pv4[0], pv4[1], 1))
                    hsl = slice(4 * g, 4 * g + 4)
                    e.tensor_tensor(
                        rhs_g[g % 4][:, :].rearrange("p (h q) -> p h q", h=4),
                        negtri[:, :].unsqueeze(1).broadcast_to([128, 4, 128]),
                        nA_T[:, hsl].unsqueeze(2).broadcast_to([128, 4, 128]),
                        OP.mult).then_inc(sV1.h, 1)

                for c in range(NCH):
                    e.wait_ge(sT.h, tPre(c, 4))
                    e.tensor_copy(nA_T[:, :], PS_D[0][0:128, 0:64]).then_inc(sV.h, 1)
                    e.tensor_copy(egi_T[:, :],
                                  PS_D[0][0:128, 64:128]).then_inc(sV.h, 1)
                    e.tensor_copy(w_T[:, :],
                                  PS_D[0][0:128, 128:192]).then_inc(sV.h, 1)
                    e.tensor_copy(eG_row[0:1, :],
                                  PS_D[0][0:1, 192:256]).then_inc(sV.h, 1)
                    e.wait_ge(sT.h, tPre(c, 5))
                    e.tensor_copy(eG_tile[:, :],
                                  PS_D[0][0:128, 256:320]).then_inc(sV.h, 1)
                    e.wait_ge(sT.h, tPre(c, 6))
                    e.tensor_tensor(S_cbT[:, :], PS_I[0][0:128, 0:128],
                                    tri01[:, :], OP.mult).then_inc(sV.h, 1)
                    e.wait_ge(sT.h, tPre(c, 7))
                    e.tensor_copy(B_T[:, :],
                                  PS_I[0][0:128, 128:256]).then_inc(sV.h, 1)
                    emit_v1(c, 0)
                    emit_v1(c, 1)
                    for i in range(NGRP + 2):
                        if i < NGRP:
                            g = i
                            p_ = g % 2
                            gsl = slice(g * GW, (g + 1) * GW)
                            hsl = slice(4 * g, 4 * g + 4)
                            pv = _norm(c, g - 2)
                            e.wait_ge(ldX.h, 16 * (c + 1))
                            if pv:
                                e.wait_ge(sT.h, tT(pv[0], pv[1], 4))
                            e.tensor_tensor(
                                xw[p_][:, :].rearrange("p (h q) -> p h q", h=4),
                                X_tm[c % 2][:, gsl].rearrange(
                                    "p (h q) -> p h q", h=4),
                                w_T[:, hsl].unsqueeze(2).broadcast_to(
                                    [128, 4, 128]),
                                OP.mult).then_inc(sV.h, 1)
                        if 1 <= i <= NGRP:
                            g = i - 1
                            p_ = g % 2
                            gsl = slice(g * GW, (g + 1) * GW)
                            hsl = slice(4 * g, 4 * g + 4)
                            pv = _norm(c, g - 2)
                            e.wait_ge(sA.h, aI(c, "A1", g))
                            if pv:
                                e.wait_ge(sT.h, tT(pv[0], pv[1], 3))
                            e.tensor_tensor(
                                M2t[p_][:, :].rearrange("p (h q) -> p h q", h=4),
                                E_t[p_][:, :].rearrange("p (h q) -> p h q", h=4),
                                S_cbT[:, :].unsqueeze(1).broadcast_to(
                                    [128, 4, 128]),
                                OP.mult)
                            e.tensor_tensor(M2[p_][:, :], M2t[p_][:, :],
                                            d_i[:, gsl], OP.add).then_inc(sV.h, 1)
                            e.wait_ge(sT.h, tT(c, g, 2))
                            e.tensor_tensor(
                                tbuf[p_][:, :].rearrange("p (h q) -> p h q", h=4),
                                PS_I[p_][:, :].rearrange("p (h q) -> p h q", h=4),
                                egi_T[:, hsl].unsqueeze(2).broadcast_to(
                                    [128, 4, 128]),
                                OP.mult).then_inc(sV.h, 1)
                        if 2 <= i <= NGRP + 1:
                            g = i - 2
                            p_ = g % 2
                            gsl = slice(g * GW, (g + 1) * GW)
                            e.wait_ge(sT.h, tT(c, g, 3))
                            e.wait_ge(ldZ.h, 32 * c + (16 if g < 8 else 32))
                            e.tensor_tensor(yg[:, gsl], PS_Y[p_][:, :],
                                            Zs_tm[g // 8][:, (g % 8) * GW:
                                                          (g % 8 + 1) * GW],
                                            OP.mult).then_inc(sV.h, 1)
                        if i + 2 < NGRP:
                            emit_v1(c, i + 2)
                    e.wait_ge(sA.h, aI(c, "A2", 15))
                    e.reduce_sum(ss_c[:, 0:1], ss16[:, :],
                                 axis=AX.X).then_inc(sV.h, 1)
                    e.wait_ge(sA.h, aI(c, "ex"))
                    if c >= 1:
                        e.wait_ge(spY.h, 16 * c)
                    e.tensor_scalar(yrms_t[:, :], yg[:, :], ss_c[:, 1:2], None,
                                    OP.mult).then_inc(sV.h, 1)
            sV.n += VPC * NCH
            sV1.n += NGRP * NCH

            @blk.gpsimd
            def _(e):
                for c in range(NCH):
                    for g in range(NGRP):
                        p_ = g % 2
                        gsl = slice(g * GW, (g + 1) * GW)
                        hsl = slice(4 * g, 4 * g + 4)
                        pv = _norm(c, g - 2)
                        if c >= 1:
                            e.wait_ge(sA.h, aI(c - 1, "A4", g))
                        if pv:
                            e.wait_ge(sT.h, tT(pv[0], pv[1], 4))
                        e.wait_ge(sV.h, c * VPC + 5)
                        e.tensor_tensor(
                            t2b[p_][:, :].rearrange("p (h q) -> p h q", h=4),
                            S_bf[:, gsl].rearrange("p (h q) -> p h q", h=4),
                            eG_tile[:, hsl].unsqueeze(2).broadcast_to(
                                [128, 4, 128]),
                            OP.mult).then_inc(sP.h, 1)
            sP.n += PPC * NCH

            @blk.scalar
            def _(e):
                for c in range(NCH):
                    for i in range(NGRP + 2):
                        if i < NGRP:
                            g = i
                            p_ = g % 2
                            pv = _norm(c, g - 2)
                            e.wait_ge(sT.h, tT(c, g, 1))
                            if pv:
                                e.wait_ge(sV.h, vI(pv[0], "V3", pv[1]))
                            e.activation(E_t[p_][:, :], PS_D[p_][:, :],
                                         AF.Exp).then_inc(sA.h, 1)
                        if 1 <= i <= NGRP:
                            g = i - 1
                            p_ = g % 2
                            gsl = slice(g * GW, (g + 1) * GW)
                            e.wait_ge(sT.h, tT(c, g, 4))
                            e.activation(S_bf[:, gsl], PS_S[p_][:, :],
                                         AF.Copy).then_inc(sA.h, 1)
                        if 2 <= i <= NGRP + 1:
                            g = i - 2
                            gsl = slice(g * GW, (g + 1) * GW)
                            e.wait_ge(sV.h, vI(c, "V7", g))
                            if g == 0 and c >= 1:
                                e.wait_ge(sV.h, vI(c - 1, "ss"))
                            e.activation(sq_scr[:, :], yg[:, gsl], AF.Square,
                                         accum_out=ss16[:, g:g + 1]).then_inc(
                                sA.h, 1)
                    e.wait_ge(sV.h, vI(c, "ss"))
                    e.activation(ss_c[:, 1:2], ss_c[:, 0:1], AF.Ln,
                                 scale=1.0 / DSSM,
                                 bias=eps_c[:, 0:1]).then_inc(sA.h, 1)
                    e.activation(ss_c[:, 1:2], ss_c[:, 1:2], AF.Exp,
                                 scale=-0.5).then_inc(sA.h, 1)
            sA.n += APC * NCH
        nc.all_engine_barrier()
    es34.close()

    ldYT, ldWo = sem("c_ldyt"), sem("c_ldwo")
    tdC, evC, spC, vC = sem("c_td"), sem("c_ev"), sem("c_sp"), sem("c_v")
    with contextlib.ExitStack() as ph:
        def psb(name, shape, dt):
            return ph.enter_context(nc.sbuf_tensor(name, shape, dt))
        yk = psb("yk", [128, NZB * HB], BF16)
        ystg = [psb(f"ystg{i}", [128, HB], BF16) for i in range(2)]
        wo = [psb(f"wo{i}", [128, NZB * 128], BF16) for i in range(2)]
        at_t = [psb(f"at_t{i}", [128, HB], BF16) for i in range(2)]
        psC = [ph.enter_context(nc.psum_tensor(f"psC{i}", [128, HB], F32))
               for i in range(2)]

        for hf in range(2):
            t0 = hf * HB
            b_ldyt, b_ldwo, b_td, b_ev, b_sp, b_v = (ldYT.n, ldWo.n, tdC.n,
                                                     evC.n, spC.n, vC.n)
            with nc.Block() as blk:
                @blk.sync
                def _(e):
                    for k in range(NZB):
                        if k >= 2:
                            e.wait_ge(vC.h, b_v + k - 1)
                        e.dma_start(ystg[k % 2][:, :],
                                    yr_sp[t0:t0 + HB, k * 128:(k + 1) * 128],
                                    transpose=True).then_inc(ldYT.h, 16)
                ldYT.n += 16 * NZB

                @blk.vector
                def _(e):
                    for k in range(NZB):
                        e.wait_ge(ldYT.h, b_ldyt + 16 * (k + 1))
                        e.tensor_scalar(yk[:, k * HB:(k + 1) * HB],
                                        ystg[k % 2][:, :], nwt_s[:, k:k + 1],
                                        None, OP.mult).then_inc(vC.h, 1)
                vC.n += NZB

                @blk.gpsimd
                def _(e):
                    for ob in range(32):
                        if ob >= 2:
                            e.wait_ge(tdC.h, b_td + ob - 1)
                        e.dma_start(
                            wo[ob % 2][:, :].rearrange("p (k c) -> p k c", k=NZB),
                            wout_h[:, ob * 128:(ob + 1) * 128].rearrange(
                                "(k p) c -> p k c", p=128),
                        ).then_inc(ldWo.h, 16)
                ldWo.n += 16 * 32

                @blk.tensor
                def _(e):
                    for ob in range(32):
                        e.wait_ge(ldWo.h, b_ldwo + 16 * (ob + 1))
                        if ob >= 2:
                            e.wait_ge(evC.h, b_ev + ob - 1)
                        mi = None
                        for k in range(NZB):
                            if ob == 0:
                                e.wait_ge(vC.h, b_v + k + 1)
                            for n in range(2):
                                mi = e.matmul(
                                    psC[ob % 2][:, n * 512:(n + 1) * 512],
                                    wo[ob % 2][:, k * 128:(k + 1) * 128],
                                    yk[:, k * HB + n * 512:k * HB + (n + 1) * 512],
                                    start=(k == 0), stop=(k == NZB - 1))
                        mi.then_inc(tdC.h, 1)
                tdC.n += 32

                @blk.vector
                def _(e):
                    for ob in range(32):
                        e.wait_ge(tdC.h, b_td + ob + 1)
                        if ob >= 2:
                            e.wait_ge(spC.h, b_sp + 16 * (ob - 1))
                        e.tensor_copy(at_t[ob % 2][:, :],
                                      psC[ob % 2][:, :]).then_inc(evC.h, 1)
                vC.n += 0

                @blk.scalar
                def _(e):
                    for ob in range(32):
                        e.wait_ge(evC.h, b_ev + ob + 1)
                        e.dma_start(at_sp[ob * 128:(ob + 1) * 128, t0:t0 + HB],
                                    at_t[ob % 2][:, :]).then_inc(spC.h, 16)
                    e.wait_ge(spC.h, b_sp + 16 * 32)
                evC.n += 32
                spC.n += 16 * 32
            nc.all_engine_barrier()

    ldE, vE, sE, spE = sem("e_ld"), sem("e_v"), sem("e_s"), sem("e_sp")
    ldF, vF, sF, stO = sem("f_ld"), sem("f_v"), sem("f_s"), sem("f_out")
    ldXT, ldWf = sem("d_ldxt"), sem("d_ldwf")
    tdD, evD, spD, vD = sem("d_td"), sem("d_ev"), sem("d_sp"), sem("d_v")
    with contextlib.ExitStack() as ph:
        def psb(name, shape, dt):
            return ph.enter_context(nc.sbuf_tensor(name, shape, dt))
        tm_t = [psb(f"tm_t{i}", [128, H], BF16) for i in range(2)]
        xr_t = [psb(f"xr_t{i}", [128, H], F32) for i in range(2)]
        res_t = [psb(f"res_t{i}", [128, H], F32) for i in range(2)]
        nres_t = [psb(f"nres_t{i}", [128, H], BF16) for i in range(2)]
        st_e = psb("st_e", [128, 8], F32)
        xk2 = psb("xk2", [128, NKB * HB], BF16)
        wf = [psb(f"wf{i}", [128, NKB * 128], BF16) for i in range(2)]
        la_s = psb("la_s", [128, NKB * 48], BF16)
        lb_s = psb("lb_s", [48, H], BF16)
        bias_s = psb("bias_s", [128, 32], F32)
        u48 = psb("u48", [48, HB], BF16)
        ff_t = [psb(f"ff_t{i}", [128, HB], BF16) for i in range(2)]
        psU = ph.enter_context(nc.psum_tensor("psU", [48, HB], F32))
        psD = [ph.enter_context(nc.psum_tensor(f"psDD{i}", [128, HB], F32))
               for i in range(3)]

        def e1_sync(e, tc):
            if tc == 0:
                e.wait_ge(spC.h, 16 * 32)
            if tc == 8:
                e.wait_ge(spC.h, 16 * 64)
            if tc >= 2:
                e.wait_ge(vE.h, 4 * (tc - 2) + 4)
                e.wait_ge(sE.h, 4 * (tc - 2) + 3)
            e.dma_start(tm_t[tc % 2][:, :], at_sp[0:H, tc * 128:(tc + 1) * 128],
                        transpose=True).then_inc(ldE.h, 16)
            e.dma_start(xr_t[tc % 2][:, :],
                        x_h[tc * 128:(tc + 1) * 128, :]).then_inc(ldE.h, 16)

        def e1_scalar(e, tc):
            e.wait_ge(ldE.h, 32 * (tc + 1))
            if tc >= 2:
                e.wait_ge(vE.h, 4 * (tc - 2) + 3)
            e.activation(res_t[tc % 2][:, :], tm_t[tc % 2][:, :],
                         AF.Square).then_inc(sE.h, 1)
            e.wait_ge(vE.h, 4 * tc + 1)
            e.activation(st_e[:, 6:7], st_e[:, 4:5], AF.Ln, scale=-1.0,
                         bias=eps_c[:, 0:1])
            e.activation(st_e[:, 7:8], st_e[:, 6:7], AF.Exp,
                         scale=-0.5).then_inc(sE.h, 1)
            e.wait_ge(vE.h, 4 * tc + 2)
            e.activation(xr_t[tc % 2][:, :], res_t[tc % 2][:, :],
                         AF.Square).then_inc(sE.h, 1)
            e.wait_ge(vE.h, 4 * tc + 3)
            e.activation(st_e[:, 6:7], st_e[:, 4:5], AF.Ln, scale=-1.0,
                         bias=eps_c[:, 0:1])
            e.activation(st_e[:, 7:8], st_e[:, 6:7], AF.Exp,
                         scale=-0.5).then_inc(sE.h, 1)

        def e1_vector(e, tc):
            e.wait_ge(ldE.h, 32 * (tc + 1))
            e.reduce_sum(st_e[:, 0:1], tm_t[tc % 2][:, :], axis=AX.X)
            e.wait_ge(sE.h, 4 * tc + 1)
            e.reduce_sum(st_e[:, 1:2], res_t[tc % 2][:, :], axis=AX.X)
            e.tensor_scalar(st_e[:, 2:3], st_e[:, 0:1], 1.0 / H, None, OP.mult)
            e.tensor_scalar(st_e[:, 3:4], st_e[:, 1:2], 1.0 / H, None, OP.mult)
            e.tensor_scalar(st_e[:, 0:1], st_e[:, 0:1], 0.0, None, OP.mult)
            e.scalar_tensor_tensor(st_e[:, 4:5], st_e[:, 2:3], st_e[:, 2:3],
                                   st_e[:, 3:4], OP.mult,
                                   OP.subtract).then_inc(vE.h, 1)
            e.wait_ge(sE.h, 4 * tc + 2)
            if tc >= 2:
                e.wait_ge(spE.h, 32 * (tc - 2) + 16)
            e.tensor_scalar(res_t[tc % 2][:, :], tm_t[tc % 2][:, :],
                            st_e[:, 2:3], st_e[:, 7:8], OP.subtract, OP.mult)
            e.tensor_tensor(res_t[tc % 2][:, :], res_t[tc % 2][:, :],
                            xr_t[tc % 2][:, :], OP.add).then_inc(vE.h, 1)
            e.wait_ge(sE.h, 4 * tc + 3)
            e.reduce_sum(st_e[:, 0:1], res_t[tc % 2][:, :], axis=AX.X)
            e.reduce_sum(st_e[:, 1:2], xr_t[tc % 2][:, :], axis=AX.X)
            e.tensor_scalar(st_e[:, 2:3], st_e[:, 0:1], 1.0 / H, None, OP.mult)
            e.tensor_scalar(st_e[:, 3:4], st_e[:, 1:2], 1.0 / H, None, OP.mult)
            e.tensor_scalar(st_e[:, 0:1], st_e[:, 0:1], 0.0, None, OP.mult)
            e.scalar_tensor_tensor(st_e[:, 4:5], st_e[:, 2:3], st_e[:, 2:3],
                                   st_e[:, 3:4], OP.mult,
                                   OP.subtract).then_inc(vE.h, 1)
            e.wait_ge(sE.h, 4 * tc + 4)
            if tc >= 2:
                e.wait_ge(spE.h, 32 * (tc - 2) + 32)
            e.tensor_scalar(nres_t[tc % 2][:, :], res_t[tc % 2][:, :],
                            st_e[:, 2:3], st_e[:, 7:8], OP.subtract,
                            OP.mult).then_inc(vE.h, 1)

        def e1_gpsimd(e, tc):
            e.wait_ge(vE.h, 4 * tc + 2)
            e.dma_start(x2_sp[tc * 128:(tc + 1) * 128, :],
                        res_t[tc % 2][:, :]).then_inc(spE.h, 16)
            e.wait_ge(vE.h, 4 * tc + 4)
            e.dma_start(xn2_sp[tc * 128:(tc + 1) * 128, :],
                        nres_t[tc % 2][:, :]).then_inc(spE.h, 16)

        def e2_sync(e, tc):
            if tc == 0:
                e.wait_ge(spD.h, 16 * 32)
            if tc == 8:
                e.wait_ge(spD.h, 16 * 64)
            if tc >= 2:
                e.wait_ge(vF.h, 3 * (tc - 2) + 3)
                e.wait_ge(sF.h, 2 * (tc - 2) + 1)
            e.dma_start(tm_t[tc % 2][:, :], ff_sp[0:H, tc * 128:(tc + 1) * 128],
                        transpose=True).then_inc(ldF.h, 16)
            e.dma_start(xr_t[tc % 2][:, :],
                        x2_sp[tc * 128:(tc + 1) * 128, :]).then_inc(ldF.h, 16)

        def e2_scalar(e, tc):
            e.wait_ge(ldF.h, 32 * (tc + 1))
            if tc >= 2:
                e.wait_ge(vF.h, 3 * (tc - 2) + 1)
            e.activation(res_t[tc % 2][:, :], tm_t[tc % 2][:, :],
                         AF.Square).then_inc(sF.h, 1)
            e.wait_ge(vF.h, 3 * tc + 1)
            e.activation(st_e[:, 6:7], st_e[:, 4:5], AF.Ln, scale=-1.0,
                         bias=eps_c[:, 0:1])
            e.activation(st_e[:, 7:8], st_e[:, 6:7], AF.Exp,
                         scale=-0.5).then_inc(sF.h, 1)

        def e2_vector(e, tc):
            e.wait_ge(ldF.h, 32 * (tc + 1))
            e.reduce_sum(st_e[:, 0:1], tm_t[tc % 2][:, :], axis=AX.X)
            e.wait_ge(sF.h, 2 * tc + 1)
            e.reduce_sum(st_e[:, 1:2], res_t[tc % 2][:, :], axis=AX.X)
            e.tensor_scalar(st_e[:, 2:3], st_e[:, 0:1], 1.0 / H, None, OP.mult)
            e.tensor_scalar(st_e[:, 3:4], st_e[:, 1:2], 1.0 / H, None, OP.mult)
            e.tensor_scalar(st_e[:, 0:1], st_e[:, 0:1], 0.0, None, OP.mult)
            e.scalar_tensor_tensor(st_e[:, 4:5], st_e[:, 2:3], st_e[:, 2:3],
                                   st_e[:, 3:4], OP.mult,
                                   OP.subtract).then_inc(vF.h, 1)
            e.wait_ge(sF.h, 2 * tc + 2)
            if tc >= 2:
                e.wait_ge(stO.h, 16 * (tc - 1))
            e.tensor_scalar(res_t[tc % 2][:, :], tm_t[tc % 2][:, :],
                            st_e[:, 2:3], st_e[:, 7:8], OP.subtract,
                            OP.mult).then_inc(vF.h, 1)
            e.tensor_tensor(res_t[tc % 2][:, :], res_t[tc % 2][:, :],
                            xr_t[tc % 2][:, :], OP.add).then_inc(vF.h, 1)

        def e2_gpsimd(e, tc):
            e.wait_ge(vF.h, 3 * tc + 3)
            e.dma_start(out_h[tc * 128:(tc + 1) * 128, :],
                        res_t[tc % 2][:, :]).then_inc(stO.h, 16)

        with nc.Block() as blk:
            @blk.sync
            def _(e):
                for tc in range(8):
                    e1_sync(e, tc)

            @blk.scalar
            def _(e):
                for tc in range(8):
                    e1_scalar(e, tc)

            @blk.vector
            def _(e):
                for tc in range(8):
                    e1_vector(e, tc)

            @blk.gpsimd
            def _(e):
                for tc in range(8):
                    e1_gpsimd(e, tc)
        ldE.n += 32 * 8
        sE.n += 4 * 8
        vE.n += 4 * 8
        spE.n += 32 * 8
        nc.all_engine_barrier()

        for hf in range(2):
            t0 = hf * HB
            b_xt, b_wf, b_td, b_ev, b_sp = ldXT.n, ldWf.n, tdD.n, evD.n, spD.n
            epi_emit = (lambda fns, e, j: fns(e, 8 + j)) if hf == 0 else                        (lambda fns, e, j: fns(e, j))
            e_s, e_sc, e_v, e_g = (e1_sync, e1_scalar, e1_vector, e1_gpsimd)                 if hf == 0 else (e2_sync, e2_scalar, e2_vector, e2_gpsimd)
            with nc.Block() as blk:
                @blk.sync
                def _(e):
                    if hf == 0:
                        e.wait_ge(spE.h, 32 * 8)
                    if hf == 1:
                        e.wait_ge(tdD.h, b_td)
                        e.wait_ge(spE.h, 32 * 16)
                    for k in range(NKB):
                        e.dma_start(xk2[:, k * HB:(k + 1) * HB],
                                    xn2_sp[t0:t0 + HB, k * 128:(k + 1) * 128],
                                    transpose=True).then_inc(ldXT.h, 16)
                    for j in range(8):
                        epi_emit(e_s, e, j)
                ldXT.n += 16 * NKB
                ldE.n += (32 * 8 if hf == 0 else 0)
                ldF.n += (32 * 8 if hf == 1 else 0)

                @blk.gpsimd
                def _(e):
                    if hf == 0:
                        e.dma_start(bias_s[:, :], biast_h[:, :]).then_inc(
                            ldWf.h, 16)
                        e.dma_start(la_s[:, :].rearrange("p (k r) -> p k r",
                                                         k=NKB),
                                    lat_h[:, :].rearrange("(k p) r -> p k r",
                                                          p=128)).then_inc(
                            ldWf.h, 16)
                        e.dma_start(lb_s[:, :], lbt_h[:, :]).then_inc(ldWf.h, 16)
                    for ob in range(32):
                        if ob >= 2:
                            e.wait_ge(tdD.h, b_td + 1 + ob - 1)
                        e.dma_start(
                            wf[ob % 2][:, :].rearrange("p (k c) -> p k c", k=NKB),
                            wft_h[:, ob * 128:(ob + 1) * 128].rearrange(
                                "(k p) c -> p k c", p=128),
                        ).then_inc(ldWf.h, 16)
                        if ob % 4 == 3:
                            epi_emit(e_g, e, ob // 4)
                    if hf == 1:
                        e.wait_ge(spE.h, 32 * 16)
                ldWf.n += 16 * 32 + (48 if hf == 0 else 0)
                spE.n += (32 * 8 if hf == 0 else 0)
                stO.n += (16 * 8 if hf == 1 else 0)

                @blk.tensor
                def _(e):
                    if hf == 0:
                        e.wait_ge(ldWf.h, 16 * 3)
                    mi = None
                    for k in range(NKB):
                        e.wait_ge(ldXT.h, b_xt + 16 * (k + 1))
                        for n in range(2):
                            mi = e.matmul(
                                psU[0:48, n * 512:(n + 1) * 512],
                                la_s[:, k * 48:(k + 1) * 48],
                                xk2[:, k * HB + n * 512:k * HB + (n + 1) * 512],
                                start=(k == 0), stop=(k == NKB - 1))
                    mi.then_inc(tdD.h, 1)
                    for ob in range(32):
                        e.wait_ge(ldWf.h, b_wf + (48 if hf == 0 else 0)
                                  + 16 * (ob + 1))
                        if ob == 0:
                            e.wait_ge(vD.h, hf + 1)
                        if ob >= 3:
                            e.wait_ge(evD.h, b_ev + ob - 2)
                        for k in range(NKB):
                            for n in range(2):
                                e.matmul(psD[ob % 3][:, n * 512:(n + 1) * 512],
                                         wf[ob % 2][:, k * 128:(k + 1) * 128],
                                         xk2[:, k * HB + n * 512:
                                             k * HB + (n + 1) * 512],
                                         start=(k == 0), stop=False)
                        mi = None
                        for n in range(2):
                            mi = e.matmul(psD[ob % 3][:, n * 512:(n + 1) * 512],
                                          lb_s[0:48, ob * 128:(ob + 1) * 128],
                                          u48[0:48, n * 512:(n + 1) * 512],
                                          start=False, stop=True)
                        mi.then_inc(tdD.h, 1)
                tdD.n += 33

                @blk.vector
                def _(e):
                    if hf == 0:
                        e.wait_ge(ldWf.h, 16 * 3)
                    e.wait_ge(tdD.h, b_td + 1)
                    e.tensor_scalar(u48[0:48, :], psU[0:48, :], m48[0:48, 0:1],
                                    None, OP.mult).then_inc(vD.h, 1)
                    for ob in range(32):
                        e.wait_ge(tdD.h, b_td + 1 + ob + 1)
                        if ob >= 2:
                            e.wait_ge(spD.h, b_sp + 16 * (ob - 1))
                        e.tensor_scalar(ff_t[ob % 2][:, :], psD[ob % 3][:, :],
                                        bias_s[:, ob:ob + 1], None,
                                        OP.add).then_inc(evD.h, 1)
                        if ob % 4 == 3:
                            epi_emit(e_v, e, ob // 4)
                vD.n += 1
                evD.n += 32
                vE.n += (4 * 8 if hf == 0 else 0)
                vF.n += (3 * 8 if hf == 1 else 0)

                @blk.scalar
                def _(e):
                    for ob in range(32):
                        e.wait_ge(evD.h, b_ev + ob + 1)
                        e.dma_start(ff_sp[ob * 128:(ob + 1) * 128, t0:t0 + HB],
                                    ff_t[ob % 2][:, :]).then_inc(spD.h, 16)
                        if ob % 4 == 3:
                            epi_emit(e_sc, e, ob // 4)
                    e.wait_ge(spD.h, b_sp + 16 * 32)
                spD.n += 16 * 32
                sE.n += (4 * 8 if hf == 0 else 0)
                sF.n += (2 * 8 if hf == 1 else 0)
            nc.all_engine_barrier()

        with nc.Block() as blk:
            @blk.sync
            def _(e):
                for tc in range(8, 16):
                    e2_sync(e, tc)

            @blk.scalar
            def _(e):
                for tc in range(8, 16):
                    e2_scalar(e, tc)
                e.wait_ge(stO.h, 16 * NCH)

            @blk.vector
            def _(e):
                for tc in range(8, 16):
                    e2_vector(e, tc)
                e.wait_ge(stO.h, 16 * NCH)

            @blk.gpsimd
            def _(e):
                for tc in range(8, 16):
                    e2_gpsimd(e, tc)
                e.wait_ge(stO.h, 16 * NCH)
        ldF.n += 32 * 8
        sF.n += 2 * 8
        vF.n += 3 * 8
        stO.n += 16 * 8

    es.close()
    return nc


def make_in_maps(ins, cores=range(8)):
    x = np.asarray(ins["L0_attn_hidden_states"])
    W_in = np.ascontiguousarray(np.asarray(ins["L0_attn_W_in"], dtype=np.float32))
    conv_w = np.asarray(ins["L0_attn_conv_w"], dtype=np.float32)
    convw_t = np.ascontiguousarray(conv_w.reshape(NXB, 128, KCONV)
                                   .transpose(1, 0, 2).reshape(128, NXB * KCONV))
    dtb = np.ascontiguousarray(np.asarray(ins["L0_attn_dt_bias"],
                                          dtype=np.float32).reshape(1, NH))
    alog = np.ascontiguousarray(np.asarray(ins["L0_attn_A_log"],
                                           dtype=np.float32).reshape(1, NH))
    dpar = np.ascontiguousarray(np.asarray(ins["L0_attn_D_param"],
                                           dtype=np.float32).reshape(1, NH))
    nwt = np.ascontiguousarray(np.asarray(ins["L0_attn_norm_weight"],
                                          dtype=np.float32).reshape(NZB, 128).T)
    w_out = np.ascontiguousarray(np.asarray(ins["L0_attn_W_out"],
                                            dtype=np.float32))
    wf_t = np.ascontiguousarray(np.asarray(ins["L0_ffn_weight"],
                                           dtype=np.float32).T)
    bias_t = np.ascontiguousarray(np.asarray(ins["L0_ffn_bias"],
                                             dtype=np.float32).reshape(32, 128).T)
    la_t = np.ascontiguousarray(np.asarray(ins["L0_ffn_lora_A"], dtype=np.float32)
                                .transpose(2, 0, 1).reshape(H, 48))
    lb_t = np.ascontiguousarray(np.asarray(ins["L0_ffn_lora_B"], dtype=np.float32)
                                .transpose(0, 2, 1).reshape(48, H))
    lsi = np.ascontiguousarray(
        np.asarray(ins["L0_ffn_lora_start_indices"]).astype(np.int32).reshape(1, 4))
    maps = []
    for b in cores:
        maps.append({
            "x": np.ascontiguousarray(x[b]).astype(np.float32),
            "w_in": W_in, "conv_w_t": convw_t, "dtb": dtb, "alog": alog,
            "dpar": dpar, "nwt": nwt, "w_out": w_out, "wf_t": wf_t,
            "bias_t": bias_t, "la_t": la_t, "lb_t": lb_t, "lsi": lsi,
            "bval": np.array([[float(b)]], dtype=np.float32),
        })
    return maps


_NC_CACHE = {}


def kernel(**inputs):
    ins = inputs
    if "prog" not in _NC_CACHE:
        _NC_CACHE["prog"] = build(debug=False)
    nc = _NC_CACHE["prog"]
    maps = make_in_maps(ins, cores=range(8))
    from concourse.bass_utils import run_bass_kernel_spmd
    res = run_bass_kernel_spmd(nc, maps, core_ids=list(range(8)))
    out = np.stack([np.asarray(res.results[b]["out"]) for b in range(8)], 0)
    return out.astype(np.float32)


# version 08
# speedup vs baseline: 1.1064x; 1.1064x over previous
import numpy as np
import ml_dtypes
import contextlib
import concourse.bass as bass
import concourse.mybir as mybir

F32 = mybir.dt.float32
BF16 = mybir.dt.bfloat16
I32 = mybir.dt.int32
AF = mybir.ActivationFunctionType
OP = mybir.AluOpType
AX = mybir.AxisListType
bf16 = ml_dtypes.bfloat16

H, DSSM, CONV, PROJ = 4096, 8192, 8448, 16704
NH, Q, S = 64, 128, 2048
NCH = S // Q
KCONV = 4
EPS = 1e-5
HB = S // 2
NZB = DSSM // 128
NXB = CONV // 128
NKB = H // 128
NGRP = 16
GW = 512


def build(debug=False):
    nc = bass.Bass("TRN2", target_bir_lowering=False)
    es = contextlib.ExitStack()

    def dram(name, shape, dt, kind):
        return nc.dram_tensor(name, shape, dt, kind=kind).ap()

    def sbuf(name, shape, dt):
        return es.enter_context(nc.sbuf_tensor(name, shape, dt))

    EI, EO = "ExternalInput", "ExternalOutput"
    SCR = EO if debug else "Internal"

    x_h = dram("x", [S, H], F32, EI)
    win_h = dram("w_in", [H, PROJ], F32, EI)
    convw_h = dram("conv_w_t", [128, NXB * KCONV], F32, EI)
    dtb_h = dram("dtb", [1, NH], F32, EI)
    alog_h = dram("alog", [1, NH], F32, EI)
    dpar_h = dram("dpar", [1, NH], F32, EI)
    nwt_h = dram("nwt", [128, NZB], F32, EI)
    wout_h = dram("w_out", [DSSM, H], F32, EI)
    wft_h = dram("wf_t", [H, H], F32, EI)
    biast_h = dram("bias_t", [128, 32], F32, EI)
    lat_h = dram("la_t", [H, 48], F32, EI)
    lbt_h = dram("lb_t", [48, H], F32, EI)
    lsi_h = dram("lsi", [1, 4], I32, EI)
    bval_h = dram("bval", [1, 1], F32, EI)

    xn_sp = dram("xn_sp", [S, H], BF16, SCR)
    zs_sp = dram("zs_sp", [DSSM, S], BF16, SCR)
    xs_sp = dram("xs_sp", [CONV, S], BF16, SCR)
    yr_sp = dram("yr_sp", [S, DSSM], BF16, SCR)
    at_sp = dram("at_sp", [H, S], BF16, SCR)
    x2_sp = dram("x2_sp", [S, H], F32, SCR)
    xn2_sp = dram("xn2_sp", [S, H], BF16, SCR)
    ff_sp = dram("ff_sp", [H, S], BF16, SCR)
    rawb = dram("rawb", [256, S], F32, SCR)
    out_h = dram("out", [S, H], F32, EO)

    ident_f = sbuf("ident_f", [128, 128], F32)
    ident_b = sbuf("ident_b", [128, 128], BF16)
    ones_row = sbuf("ones_row", [1, 128], F32)
    ones64 = sbuf("ones64", [64, 128], F32)
    convw_s = sbuf("convw_s", [128, NXB * KCONV], F32)
    dtb_r = sbuf("dtb_r", [1, NH], F32)
    alog_r = sbuf("alog_r", [1, NH], F32)
    dpar_r = sbuf("dpar_r", [1, NH], F32)
    nwt_s = sbuf("nwt_s", [128, NZB], F32)
    lsi_s = sbuf("lsi_s", [1, 4], I32)
    lsi_f = sbuf("lsi_f", [1, 4], F32)
    bval_s = sbuf("bval_s", [1, 1], F32)
    dtb_c = sbuf("dtb_c", [NH, 1], F32)
    a_c = sbuf("a_c", [NH, 1], F32)
    m48 = sbuf("m48", [48, 1], F32)
    mrow = sbuf("mrow", [1, 48], F32)
    g4 = sbuf("g4", [1, 4], F32)
    g4b = sbuf("g4b", [1, 4], F32)
    sc1 = sbuf("sc1", [1, 8], F32)
    dt_fm = sbuf("dt_fm", [NH, S], F32)
    eps_c = sbuf("eps_c", [128, 1], F32)
    warm_i = sbuf("warm_i", [128, 256], F32)
    warm_o = sbuf("warm_o", [128, 256], F32)

    sems = {}

    def sem(name):
        if name not in sems:
            class _S:
                pass
            s = _S()
            s.h = es.enter_context(nc.semaphore(name))
            s.n = 0
            sems[name] = s
        return sems[name]

    with nc.Block() as blk:
        @blk.gpsimd
        def _(e):
            e.dma_reset(range(152, 256))
            e.sem_clear(range(152, 256))
    nc.all_engine_barrier()

    ld0, v0, s0, t0_ = sem("ld0"), sem("v0"), sem("s0"), sem("t0")
    with nc.Block() as blk:
        @blk.sync
        def _(e):
            for dst, src in [(convw_s, convw_h), (dtb_r, dtb_h), (alog_r, alog_h),
                             (dpar_r, dpar_h), (nwt_s, nwt_h), (lsi_s, lsi_h),
                             (bval_s, bval_h)]:
                e.dma_start(dst[:, :], src[:, :]).then_inc(ld0.h, 16)
        ld0.n += 16 * 7

        @blk.gpsimd
        def _(e):
            e.memset(ident_f[:, :], 1.0)
            e.affine_select(ident_f[:, :], ident_f[:, :], [[-1, 128]],
                            OP.is_equal, 0.0, base=0, channel_multiplier=1)
            e.memset(ident_b[:, :], 1.0)
            e.affine_select(ident_b[:, :], ident_b[:, :], [[-1, 128]],
                            OP.is_equal, 0.0, base=0, channel_multiplier=1)
            e.memset(ones_row[:, :], 1.0)
            e.memset(eps_c[:, :], EPS)
            e.memset(warm_i[:, :], 0.5)
            e.memset(ones64[:, :], 1.0).then_inc(v0.h, 1)
        v0.n += 1

        @blk.vector
        def _(e):
            e.wait_ge(ld0.h, 16 * 7)
            e.wait_ge(v0.h, 1)

            def spacer():
                e.tensor_scalar(g4[0:1, 0:4], ones64[0:1, 0:4], 0.0, None, OP.mult)
            e.tensor_copy(lsi_f[:, :], lsi_s[:, :])
            spacer()
            e.tensor_scalar(sc1[0:1, 0:4], lsi_f[:, :], -1.0, 0.5, OP.mult, OP.add)
            spacer()
            e.tensor_scalar(sc1[0:1, 4:8], sc1[0:1, 0:4], bval_s[0:1, 0:1], None,
                            OP.add).then_inc(v0.h, 1)
        v0.n += 1

        @blk.scalar
        def _(e):
            e.wait_ge(v0.h, v0.n)
            for fn in (AF.Silu, AF.Square, AF.Abs, AF.Relu, AF.Sign, AF.Exp,
                       AF.Ln, AF.Copy):
                e.activation(mrow[0:1, 0:1], eps_c[0:1, 0:1], fn)
            e.activation(g4[:, :], sc1[0:1, 4:8], AF.Sign)
            e.activation(g4b[:, :], g4[:, :], AF.Copy, scale=0.5,
                         bias=0.5).then_inc(s0.h, 1)
        s0.n += 1

        @blk.vector
        def _(e):
            e.wait_ge(s0.h, s0.n)

            def spacer():
                e.tensor_scalar(g4[0:1, 0:4], ones64[0:1, 0:4], 0.0, None, OP.mult)
            e.tensor_scalar(sc1[0:1, 0:1], g4b[0:1, 3:4], 1.0, 3.0, OP.mult, OP.add)
            e.tensor_scalar(sc1[0:1, 2:3], g4b[0:1, 1:2], -1.0, 1.0, OP.mult, OP.add)
            e.tensor_scalar(sc1[0:1, 4:5], g4b[0:1, 1:2], 2.0, None, OP.mult)
            e.tensor_scalar(sc1[0:1, 5:6], g4b[0:1, 2:3], -1.0, 1.0, OP.mult, OP.add)
            e.tensor_tensor(sc1[0:1, 1:2], sc1[0:1, 0:1], g4b[0:1, 2:3], OP.mult)
            e.tensor_tensor(sc1[0:1, 3:4], sc1[0:1, 2:3], g4b[0:1, 0:1], OP.mult)
            spacer()
            e.tensor_tensor(sc1[0:1, 4:5], sc1[0:1, 4:5], sc1[0:1, 3:4], OP.add)
            spacer()
            e.tensor_tensor(sc1[0:1, 5:6], sc1[0:1, 5:6], sc1[0:1, 4:5], OP.mult)
            spacer()
            e.tensor_tensor(sc1[0:1, 6:7], sc1[0:1, 1:2], sc1[0:1, 5:6], OP.add)
            spacer()
            i = None
            for k in range(3):
                e.tensor_scalar(sc1[0:1, 7:8], sc1[0:1, 6:7], -(k + 2.0), None,
                                OP.add)
                spacer()
                e.tensor_tensor(sc1[0:1, 7:8], sc1[0:1, 7:8], sc1[0:1, 7:8], OP.mult)
                spacer()
                e.tensor_scalar(sc1[0:1, 7:8], sc1[0:1, 7:8], -1.0, 1.0, OP.mult,
                                OP.add)
                spacer()
                e.tensor_relu(sc1[0:1, 7:8], sc1[0:1, 7:8])
                spacer()
                i = e.tensor_scalar(mrow[0:1, k * 16:(k + 1) * 16],
                                    ones_row[0:1, 0:16], sc1[0:1, 7:8], None,
                                    OP.mult)
            i.then_inc(v0.h, 1)
        v0.n += 1

    nc.all_engine_barrier()

    with contextlib.ExitStack() as ph:
        ps0 = ph.enter_context(nc.psum_tensor("ps0", [128, 512], F32))
        with nc.Block() as blk:
            @blk.tensor
            def _(e):
                e.matmul(ps0[0:64, 0:1], dtb_r[0:1, 0:64], ones_row[0:1, 0:1],
                         start=True, stop=True)
                e.matmul(ps0[0:64, 1:2], alog_r[0:1, 0:64], ones_row[0:1, 0:1],
                         start=True, stop=True)
                e.matmul(ps0[0:48, 2:3], mrow[0:1, 0:48], ones_row[0:1, 0:1],
                         start=True, stop=True).then_inc(t0_.h, 1)
            t0_.n += 1

            @blk.vector
            def _(e):
                e.wait_ge(t0_.h, t0_.n)
                e.tensor_copy(dtb_c[0:64, 0:1], ps0[0:64, 0:1])
                e.tensor_copy(m48[0:48, 0:1], ps0[0:48, 2:3]).then_inc(v0.h, 1)
            v0.n += 1

            @blk.scalar
            def _(e):
                e.wait_ge(v0.h, v0.n)
                e.activation(a_c[0:64, 0:1], ps0[0:64, 1:2],
                             AF.Exp).then_inc(s0.h, 1)
            s0.n += 1
        nc.all_engine_barrier()

    ld1, v1, s1, sp1 = sem("ld1"), sem("v1"), sem("s1"), sem("sp1")
    with contextlib.ExitStack() as ph:
        def psb(name, shape, dt):
            return ph.enter_context(nc.sbuf_tensor(name, shape, dt))
        x_t = [psb(f"x_t{i}", [128, H], F32) for i in range(2)]
        sq_t = [psb(f"sq_t{i}", [128, H], F32) for i in range(2)]
        xn_t = [psb(f"xn_t{i}", [128, H], BF16) for i in range(2)]
        st_t = psb("st_t", [128, 8], F32)

        with nc.Block() as blk:
            @blk.sync
            def _(e):
                for tc in range(NCH):
                    if tc >= 2:
                        e.wait_ge(v1.h, 2 * (tc - 2) + 2)
                        e.wait_ge(s1.h, 2 * (tc - 2) + 1)
                    e.dma_start(x_t[tc % 2][:, :],
                                x_h[tc * 128:(tc + 1) * 128, :]).then_inc(ld1.h, 16)
            ld1.n += 16 * NCH

            @blk.scalar
            def _(e):
                for tc in range(NCH):
                    e.wait_ge(ld1.h, 16 * (tc + 1))
                    if tc >= 2:
                        e.wait_ge(v1.h, 2 * (tc - 2) + 1)
                    e.activation(sq_t[tc % 2][:, :], x_t[tc % 2][:, :],
                                 AF.Square).then_inc(s1.h, 1)
                    e.wait_ge(v1.h, 2 * tc + 1)
                    e.activation(st_t[:, 6:7], st_t[:, 4:5], AF.Ln, scale=-1.0,
                                 bias=eps_c[:, 0:1])
                    e.activation(st_t[:, 7:8], st_t[:, 6:7], AF.Exp,
                                 scale=-0.5).then_inc(s1.h, 1)
            s1.n += 2 * NCH

            @blk.vector
            def _(e):
                for tc in range(NCH):
                    e.wait_ge(ld1.h, 16 * (tc + 1))
                    e.reduce_sum(st_t[:, 0:1], x_t[tc % 2][:, :], axis=AX.X)
                    e.wait_ge(s1.h, 2 * tc + 1)
                    e.reduce_sum(st_t[:, 1:2], sq_t[tc % 2][:, :], axis=AX.X)
                    e.tensor_scalar(st_t[:, 2:3], st_t[:, 0:1], 1.0 / H, None,
                                    OP.mult)
                    e.tensor_scalar(st_t[:, 3:4], st_t[:, 1:2], 1.0 / H, None,
                                    OP.mult)
                    e.tensor_scalar(st_t[:, 0:1], st_t[:, 0:1], 0.0, None, OP.mult)
                    e.scalar_tensor_tensor(st_t[:, 4:5], st_t[:, 2:3],
                                           st_t[:, 2:3], st_t[:, 3:4],
                                           OP.mult, OP.subtract).then_inc(v1.h, 1)
                    e.wait_ge(s1.h, 2 * tc + 2)
                    if tc >= 2:
                        e.wait_ge(sp1.h, 16 * (tc - 1))
                    e.tensor_scalar(xn_t[tc % 2][:, :], x_t[tc % 2][:, :],
                                    st_t[:, 2:3], st_t[:, 7:8],
                                    OP.subtract, OP.mult).then_inc(v1.h, 1)
            v1.n += 2 * NCH

            @blk.gpsimd
            def _(e):
                for tc in range(NCH):
                    e.wait_ge(v1.h, 2 * tc + 2)
                    e.dma_start(xn_sp[tc * 128:(tc + 1) * 128, :],
                                xn_t[tc % 2][:, :]).then_inc(sp1.h, 16)
                e.wait_ge(sp1.h, 16 * NCH)
            sp1.n += 16 * NCH
        nc.all_engine_barrier()

    CB_ORDER = list(range(64, 131)) + list(range(64))
    ldT, ldW = sem("a_ldT"), sem("a_ldW")
    tdA, evA, cvA, siA = sem("a_td"), sem("a_ev"), sem("a_cv"), sem("a_si")
    spX, spZ = sem("a_spx"), sem("a_spz")
    with contextlib.ExitStack() as ph:
        def psb(name, shape, dt):
            return ph.enter_context(nc.sbuf_tensor(name, shape, dt))
        xn_fm = psb("xn_fm", [128, NKB * HB], BF16)
        wb = [psb(f"wb{i}", [128, NKB * 128], BF16) for i in range(2)]
        conv_in = psb("conv_in", [128, HB + 3], F32)
        scr1 = psb("scr1", [128, HB], F32)
        scr2 = [psb(f"scr2_{i}", [128, HB], F32) for i in range(2)]
        xs_t = [psb(f"xs_t{i}", [128, HB], BF16) for i in range(2)]
        zs_t = [psb(f"zs_t{i}", [128, HB], BF16) for i in range(2)]
        bsave = psb("bsave", [128, NXB * 3], F32)
        rawdbg_t = [psb(f"rawdbg{i}", [128, HB], F32) for i in range(2)] if debug else None
        pa_ps = [ph.enter_context(nc.psum_tensor(f"pa_ps{i}", [128, 1024], F32))
                 for i in range(2)]

        with nc.Block() as blk:
            @blk.vector
            def _(e):
                e.memset(conv_in[:, 0:3], 0.0)
        nc.all_engine_barrier()

        for hf in range(2):
            t0 = hf * HB
            b_ldT, b_ldW, b_td, b_ev = ldT.n, ldW.n, tdA.n, evA.n
            b_cv, b_si, b_spx, b_spz = cvA.n, siA.n, spX.n, spZ.n

            with nc.Block() as blk:
                @blk.sync
                def _(e):
                    for k in range(NKB):
                        e.dma_start(xn_fm[:, k * HB:(k + 1) * HB],
                                    xn_sp[t0:t0 + HB, k * 128:(k + 1) * 128],
                                    transpose=True).then_inc(ldT.h, 16)
                ldT.n += 16 * NKB

                @blk.gpsimd
                def _(e):
                    for i, cb in enumerate(CB_ORDER):
                        cw = 64 if cb == 130 else 128
                        if i >= 2:
                            e.wait_ge(tdA.h, b_td + i - 1)
                        e.dma_start(
                            wb[i % 2][:, 0:NKB * cw].rearrange(
                                "p (k c) -> p k c", k=NKB),
                            win_h[:, cb * 128:cb * 128 + cw].rearrange(
                                "(k p) c -> p k c", p=128),
                        ).then_inc(ldW.h, 16)
                ldW.n += 16 * len(CB_ORDER)

                @blk.tensor
                def _(e):
                    e.wait_ge(ldT.h, b_ldT + 16 * NKB)
                    for i, cb in enumerate(CB_ORDER):
                        cw = 64 if cb == 130 else 128
                        e.wait_ge(ldW.h, b_ldW + 16 * (i + 1))
                        if i >= 2:
                            e.wait_ge(evA.h, b_ev + i - 1)
                        ps = pa_ps[i % 2]
                        mi = None
                        for k in range(NKB):
                            for n in range(2):
                                mi = e.matmul(
                                    ps[0:cw, n * 512:(n + 1) * 512],
                                    wb[i % 2][:, k * cw:(k + 1) * cw],
                                    xn_fm[:, k * HB + n * 512:
                                          k * HB + (n + 1) * 512],
                                    start=(k == 0), stop=(k == NKB - 1),
                                )
                        mi.then_inc(tdA.h, 1)
                tdA.n += len(CB_ORDER)

                @blk.vector
                def _(e):
                    nxv = 0
                    for i, cb in enumerate(CB_ORDER):
                        ps = pa_ps[i % 2]
                        if 64 <= cb <= 129:
                            blkx = cb - 64
                            e.wait_ge(tdA.h, b_td + i + 1)
                            if hf == 1:
                                e.tensor_copy(conv_in[:, 0:3],
                                              bsave[:, blkx * 3:(blkx + 1) * 3])
                            e.tensor_copy(conv_in[:, 3:3 + HB],
                                          ps[:, 0:HB]).then_inc(evA.h, 1)
                            if debug and blkx < 2:
                                e.tensor_copy(rawdbg_t[blkx][:, :],
                                              conv_in[:, 3:3 + HB])
                            wc = [convw_s[:, blkx * 4 + k:blkx * 4 + k + 1]
                                  for k in range(4)]
                            e.tensor_scalar(scr1[:, :], conv_in[:, 0:HB], wc[0],
                                            None, OP.mult)
                            if nxv >= 2:
                                e.wait_ge(siA.h, b_si + nxv - 1)
                            e.scalar_tensor_tensor(scr2[nxv % 2][:, :],
                                                   conv_in[:, 1:1 + HB],
                                                   wc[1], scr1[:, :], OP.mult,
                                                   OP.add)
                            e.scalar_tensor_tensor(scr1[:, :], conv_in[:, 2:2 + HB],
                                                   wc[2], scr2[nxv % 2][:, :],
                                                   OP.mult, OP.add)
                            if hf == 0:
                                e.tensor_copy(bsave[:, blkx * 3:(blkx + 1) * 3],
                                              conv_in[:, HB:HB + 3])
                            e.scalar_tensor_tensor(scr2[nxv % 2][:, :],
                                                   conv_in[:, 3:3 + HB],
                                                   wc[3], scr1[:, :], OP.mult,
                                                   OP.add).then_inc(cvA.h, 1)
                            nxv += 1
                        elif cb == 130:
                            e.wait_ge(tdA.h, b_td + i + 1)
                            e.tensor_copy(dt_fm[0:64, t0:t0 + HB],
                                          ps[0:64, 0:HB]).then_inc(evA.h, 1)
                cvA.n += NXB

                @blk.scalar
                def _(e):
                    for _w in range(6):
                        e.activation(warm_o[:, :], warm_i[:, :], AF.Silu)
                    nxs = 0
                    nzs = 0
                    for i, cb in enumerate(CB_ORDER):
                        ps = pa_ps[i % 2]
                        if 64 <= cb <= 129:
                            e.wait_ge(cvA.h, b_cv + nxs + 1)
                            if nxs >= 2:
                                e.wait_ge(spX.h, b_spx + 16 * (nxs - 1))
                            e.activation(xs_t[nxs % 2][:, :], scr2[nxs % 2][:, :],
                                         AF.Silu).then_inc(siA.h, 1)
                            e.dma_start(
                                xs_sp[(cb - 64) * 128:(cb - 63) * 128, t0:t0 + HB],
                                xs_t[nxs % 2][:, :]).then_inc(spX.h, 16)
                            nxs += 1
                        elif cb <= 63:
                            e.wait_ge(tdA.h, b_td + i + 1)
                            if nzs >= 2:
                                e.wait_ge(spZ.h, b_spz + 16 * (nzs - 1))
                            e.activation(zs_t[nzs % 2][:, :], ps[:, 0:HB],
                                         AF.Silu).then_inc(evA.h, 1)
                            e.dma_start(
                                zs_sp[cb * 128:(cb + 1) * 128, t0:t0 + HB],
                                zs_t[nzs % 2][:, :]).then_inc(spZ.h, 16)
                            nzs += 1
                    e.wait_ge(spX.h, b_spx + 16 * NXB)
                    e.wait_ge(spZ.h, b_spz + 16 * NZB)
                    if debug:
                        e.dma_start(rawb[0:128, t0:t0 + HB],
                                    rawdbg_t[0][:, :]).then_inc(spZ.h, 16)
                        e.dma_start(rawb[128:256, t0:t0 + HB],
                                    rawdbg_t[1][:, :]).then_inc(spZ.h, 16)
                        e.wait_ge(spZ.h, b_spz + 16 * NZB + 32)
                siA.n += NXB
                spX.n += 16 * NXB
                spZ.n += 16 * NZB + (32 if debug else 0)
                evA.n += NXB + 1 + NZB

            nc.all_engine_barrier()

    es34 = contextlib.ExitStack()

    def sb34(name, shape, dt):
        return es34.enter_context(nc.sbuf_tensor(name, shape, dt))
    lndt_b = sb34("lndt_b", [NH, S], BF16)
    nA = sb34("nA", [NH, S], BF16)
    egi = sb34("egi", [NH, S], BF16)
    wst = sb34("wst", [NH, S], BF16)
    strict_up = sb34("strict_up", [128, 128], BF16)
    negtri = sb34("negtri", [128, 128], BF16)
    tri01 = sb34("tri01", [128, 128], F32)
    sel64 = sb34("sel64", [64, DSSM], BF16)
    d_i = sb34("d_i", [128, DSSM], BF16)

    v3, s3, g3_, t3_ = sem("v3"), sem("s3"), sem("g3"), sem("t3")
    with contextlib.ExitStack() as ph:
        def psb(name, shape, dt):
            return ph.enter_context(nc.sbuf_tensor(name, shape, dt))
        tmp64 = psb("tmp64", [NH, S], F32)
        ta = psb("ta", [NH, S], F32)
        pa_f = psb("pa_f", [NH, S], F32)
        d_tile = psb("d_tile", [128, NH], F32)
        ps3 = ph.enter_context(nc.psum_tensor("ps3", [128, 512], F32))

        with nc.Block() as blk:
            @blk.gpsimd
            def _(e):
                e.memset(strict_up[:, :], 1.0)
                e.affine_select(strict_up[:, :], strict_up[:, :], [[-1, 128]],
                                OP.is_gt, 0.0, base=0, channel_multiplier=1)
                e.memset(negtri[:, :], 0.0)
                e.affine_select(negtri[:, :], negtri[:, :], [[-1, 128]],
                                OP.is_gt, -1.0, base=0, channel_multiplier=1)
                e.memset(tri01[:, :], 0.0)
                e.affine_select(tri01[:, :], tri01[:, :], [[-1, 128]],
                                OP.is_gt, 1.0, base=0, channel_multiplier=1)
                e.memset(sel64[:, :], 1.0)
                e.affine_select(sel64[:, :], sel64[:, :], [[-1, 64], [0, 128]],
                                OP.is_equal, 0.0, base=0,
                                channel_multiplier=1).then_inc(g3_.h, 1)
            g3_.n += 1

            @blk.vector
            def _(e):
                e.tensor_scalar(tmp64[:, :], dt_fm[:, :], dtb_c[0:64, 0:1], None,
                                OP.add).then_inc(v3.h, 1)
            v3.n += 1

            @blk.scalar
            def _(e):
                for _w in range(6):
                    e.activation(warm_o[:, :], warm_i[:, :], AF.Exp)
                e.wait_ge(v3.h, v3.n)
                e.activation(ta[:, :], tmp64[:, :], AF.Abs)
                e.activation(ta[:, :], ta[:, :], AF.Exp, scale=-1.0)
                e.activation(ta[:, :], ta[:, :], AF.Ln, bias=1.0)
                e.activation(pa_f[:, :], tmp64[:, :], AF.Relu).then_inc(s3.h, 1)
            s3.n += 1

            @blk.vector
            def _(e):
                e.wait_ge(s3.h, s3.n)
                e.tensor_tensor(tmp64[:, :], pa_f[:, :], ta[:, :],
                                OP.add).then_inc(v3.h, 1)
            v3.n += 1

            @blk.scalar
            def _(e):
                e.wait_ge(v3.h, v3.n)
                e.activation(lndt_b[:, :], tmp64[:, :], AF.Ln).then_inc(s3.h, 1)
            s3.n += 1

            @blk.vector
            def _(e):
                e.wait_ge(s3.h, s3.n)
                e.tensor_scalar(nA[:, :], tmp64[:, :], a_c[0:64, 0:1], None,
                                OP.mult)
                i = None
                for c in range(NCH):
                    sl = slice(c * Q, (c + 1) * Q)
                    i = e.tensor_tensor_scan(pa_f[:, sl], ones64[:, :], nA[:, sl],
                                             0.0, OP.mult, OP.add)
                i.then_inc(v3.h, 1)
            v3.n += 1

            @blk.scalar
            def _(e):
                e.wait_ge(v3.h, v3.n)
                e.activation(egi[:, :], pa_f[:, :], AF.Exp,
                             scale=-1.0).then_inc(s3.h, 1)
            s3.n += 1

            @blk.vector
            def _(e):
                e.wait_ge(s3.h, s3.n)
                i = None
                for c in range(NCH):
                    sl = slice(c * Q, (c + 1) * Q)
                    i = e.tensor_scalar(ta[:, sl], pa_f[:, sl],
                                        pa_f[:, c * Q + Q - 1:c * Q + Q], None,
                                        OP.subtract)
                i.then_inc(v3.h, 1)
            v3.n += 1

            @blk.scalar
            def _(e):
                e.wait_ge(v3.h, v3.n)
                e.activation(pa_f[:, :], ta[:, :], AF.Exp).then_inc(s3.h, 1)
            s3.n += 1

            @blk.vector
            def _(e):
                e.wait_ge(s3.h, s3.n)
                e.tensor_tensor(wst[:, :], pa_f[:, :], tmp64[:, :],
                                OP.mult).then_inc(v3.h, 1)
            v3.n += 1

            @blk.tensor
            def _(e):
                e.matmul(ps3[0:128, 0:64], ones_row[0:1, 0:128], dpar_r[0:1, 0:64],
                         start=True, stop=True).then_inc(t3_.h, 1)
            t3_.n += 1

            @blk.vector
            def _(e):
                e.wait_ge(t3_.h, t3_.n)
                e.tensor_copy(d_tile[:, 0:64], ps3[0:128, 0:64])
                e.tensor_tensor(
                    d_i[:, :].rearrange("p (h q) -> p h q", h=64),
                    ident_b[:, :].unsqueeze(1).broadcast_to([128, 64, 128]),
                    d_tile[:, 0:64].unsqueeze(2).broadcast_to([128, 64, 128]),
                    OP.mult).then_inc(v3.h, 1)
            v3.n += 1
        nc.all_engine_barrier()

    sT, sV, sA, sV1, sP = sem("sc_t"), sem("sc_v"), sem("sc_a"), sem("sc_v1"), sem("sc_p")
    ldX, ldZ, ldB, spY = sem("sc_ldx"), sem("sc_ldz"), sem("sc_ldb"), sem("sc_spy")
    VPC, TPC, APC, PPC = 57, 71, 50, 32
    with contextlib.ExitStack() as ph:
        def psb(name, shape, dt):
            return ph.enter_context(nc.sbuf_tensor(name, shape, dt))
        S_bf = psb("S_bf", [128, DSSM], BF16)
        X_tm = [psb(f"X_tm{i}", [128, DSSM], BF16) for i in range(2)]
        Zs_tm = [psb(f"Zs_tm{i}", [128, DSSM // 2], BF16) for i in range(2)]
        yg = psb("yg", [128, DSSM], BF16)
        yrms_t = psb("yrms_t", [128, DSSM], BF16)
        B_c = [psb(f"B_c{i}", [128, 128], BF16) for i in range(2)]
        C_c = [psb(f"C_c{i}", [128, 128], BF16) for i in range(2)]
        B_T = psb("B_T", [128, 128], BF16)
        S_cbT = psb("S_cbT", [128, 128], BF16)
        nA_T = psb("nA_T", [128, 64], F32)
        egi_T = psb("egi_T", [128, 64], F32)
        w_T = psb("w_T", [128, 64], F32)
        eG_row = psb("eG_row", [1, 64], F32)
        eG_tile = psb("eG_tile", [128, 64], F32)
        rhs_g = [psb(f"rhs_g{i}", [128, GW], BF16) for i in range(4)]
        E_t = [psb(f"E_t{i}", [128, GW], BF16) for i in range(2)]
        M2t = [psb(f"M2t{i}", [128, GW], BF16) for i in range(2)]
        M2 = [psb(f"M2_{i}", [128, GW], BF16) for i in range(2)]
        xw = [psb(f"xw{i}", [128, GW], BF16) for i in range(2)]
        tbuf = [psb(f"tb{i}", [128, GW], BF16) for i in range(2)]
        t2b = [psb(f"t2b{i}", [128, GW], BF16) for i in range(2)]
        sq_scr = psb("sq_scr", [128, GW], F32)
        ss16 = psb("ss16", [128, 16], F32)
        ss_c = psb("ss_c", [128, 2], F32)
        PS_D = [ph.enter_context(nc.psum_tensor(f"psD{i}", [128, GW], F32))
                for i in range(2)]
        PS_I = [ph.enter_context(nc.psum_tensor(f"psI{i}", [128, GW], F32))
                for i in range(2)]
        PS_Y = [ph.enter_context(nc.psum_tensor(f"psY{i}", [128, GW], F32))
                for i in range(2)]
        PS_S = [ph.enter_context(nc.psum_tensor(f"psS{i}", [128, GW], F32))
                for i in range(2)]

        with nc.Block() as blk:
            @blk.gpsimd
            def _(e):
                e.memset(S_bf[:, :], 0.0)
        nc.all_engine_barrier()

        VORD = [("pre", k) for k in range(1, 8)]
        for _i in range(NGRP + 2):
            if _i < NGRP:
                VORD.append(("V5", _i))
            if 1 <= _i <= NGRP:
                VORD.append(("V3", _i - 1))
                VORD.append(("V4", _i - 1))
            if 2 <= _i <= NGRP + 1:
                VORD.append(("V7", _i - 2))
        VORD += [("ss", 0), ("yr", 0)]
        VPOS = {k: i + 1 for i, k in enumerate(VORD)}
        VPC = len(VORD)
        AORD = []
        for _i in range(NGRP + 2):
            if _i < NGRP:
                AORD.append(("A1", _i))
            if 1 <= _i <= NGRP:
                AORD.append(("A4", _i - 1))
            if 2 <= _i <= NGRP + 1:
                AORD.append(("A2", _i - 2))
        AORD += [("ln", 0), ("ex", 0)]
        APOS = {k: i + 1 for i, k in enumerate(AORD)}
        APC = len(AORD)
        TPC = 71
        PPC = NGRP

        def _norm(c, g):
            while g < 0:
                c -= 1
                g += NGRP
            return (c, g) if c >= 0 else None

        def tT(c, g, j):
            return c * TPC + 7 + 4 * g + j

        def tPre(c, k):
            return c * TPC + k

        def vI(c, kind, g=0):
            return c * VPC + VPOS[(kind, g)]

        def aI(c, kind, g=0):
            return c * APC + APOS[(kind, g)]

        def pI(c, g):
            return c * PPC + g + 1

        def v1i(c, g):
            return c * NGRP + g + 1

        with nc.Block() as blk:
            @blk.sync
            def _(e):
                e.dma_start(X_tm[0][:, :], xs_sp[0:DSSM, 0:128],
                            transpose=True).then_inc(ldX.h, 16)
                e.dma_start(B_c[0][:, :],
                            xs_sp[DSSM:DSSM + 128, 0:128]).then_inc(ldB.h, 16)
                e.dma_start(C_c[0][:, :],
                            xs_sp[DSSM + 128:DSSM + 256, 0:128]).then_inc(ldB.h, 16)
                for c in range(NCH):
                    sl = slice(c * Q, (c + 1) * Q)
                    if c >= 1:
                        e.wait_ge(sV.h, vI(c - 1, "yr"))
                        e.dma_start(yr_sp[(c - 1) * Q:c * Q, :],
                                    yrms_t[:, :]).then_inc(spY.h, 16)
                        e.wait_ge(sV.h, vI(c - 1, "V7", 7))
                    e.dma_start(Zs_tm[0][:, :], zs_sp[0:DSSM // 2, sl],
                                transpose=True).then_inc(ldZ.h, 16)
                    if c >= 1:
                        e.wait_ge(sV.h, vI(c - 1, "V7", 15))
                    e.dma_start(Zs_tm[1][:, :], zs_sp[DSSM // 2:DSSM, sl],
                                transpose=True).then_inc(ldZ.h, 16)
                    if c + 1 < NCH:
                        sl1 = slice((c + 1) * Q, (c + 2) * Q)
                        if c >= 1:
                            e.wait_ge(sT.h, tT(c - 1, 15, 4))
                            e.wait_ge(sV.h, vI(c - 1, "V5", 15))
                        e.dma_start(X_tm[(c + 1) % 2][:, :], xs_sp[0:DSSM, sl1],
                                    transpose=True).then_inc(ldX.h, 16)
                        e.dma_start(B_c[(c + 1) % 2][:, :],
                                    xs_sp[DSSM:DSSM + 128, sl1]).then_inc(ldB.h, 16)
                        e.dma_start(C_c[(c + 1) % 2][:, :],
                                    xs_sp[DSSM + 128:DSSM + 256, sl1]).then_inc(
                            ldB.h, 16)
                e.wait_ge(sV.h, vI(NCH - 1, "yr"))
                e.dma_start(yr_sp[(NCH - 1) * Q:NCH * Q, :],
                            yrms_t[:, :]).then_inc(spY.h, 16)
                e.wait_ge(spY.h, 16 * NCH)
            ldX.n += 16 * NCH
            ldZ.n += 32 * NCH
            ldB.n += 32 * NCH
            spY.n += 16 * NCH

            @blk.tensor
            def _(e):
                for c in range(NCH):
                    sl = slice(c * Q, (c + 1) * Q)
                    if c >= 1:
                        e.wait_ge(sA.h, aI(c - 1, "A1", 14))
                        e.wait_ge(sV.h, vI(c - 1, "V4", 14))
                    e.matmul(PS_D[0][0:128, 0:64], nA[:, sl], ident_b[0:64, 0:64],
                             start=True, stop=True).then_inc(sT.h, 1)
                    e.matmul(PS_D[0][0:128, 64:128], egi[:, sl],
                             ident_b[0:64, 0:64], start=True,
                             stop=True).then_inc(sT.h, 1)
                    e.matmul(PS_D[0][0:128, 128:192], wst[:, sl],
                             ident_b[0:64, 0:64], start=True,
                             stop=True).then_inc(sT.h, 1)
                    e.matmul(PS_D[0][0:1, 192:256],
                             egi[:, c * Q + Q - 1:c * Q + Q],
                             ident_b[0:64, 0:64], start=True,
                             stop=True).then_inc(sT.h, 1)
                    e.wait_ge(sV.h, c * VPC + 4)
                    e.matmul(PS_D[0][0:128, 256:320], ones_row[0:1, 0:128],
                             eG_row[0:1, 0:64], start=True,
                             stop=True).then_inc(sT.h, 1)
                    e.wait_ge(ldB.h, 32 * (c + 1))
                    e.matmul(PS_I[0][0:128, 0:128], B_c[c % 2][:, :],
                             C_c[c % 2][:, :], start=True,
                             stop=True).then_inc(sT.h, 1)
                    e.matmul(PS_I[0][0:128, 128:256], B_c[c % 2][:, :],
                             ident_b[:, :], start=True, stop=True).then_inc(sT.h, 1)
                    for g in range(NGRP):
                        p_ = g % 2
                        gsl = slice(g * GW, (g + 1) * GW)
                        pv = _norm(c, g - 2)
                        e.wait_ge(sV1.h, v1i(c, g))
                        if pv:
                            e.wait_ge(sA.h, aI(pv[0], "A1", pv[1]))
                        e.matmul(PS_D[p_][:, :], strict_up[:, :],
                                 rhs_g[g % 4][:, :], start=True, stop=False)
                        e.matmul(PS_D[p_][:, :], lndt_b[:, sl],
                                 sel64[:, g * GW:(g + 1) * GW],
                                 start=False, stop=True).then_inc(sT.h, 1)
                        if pv:
                            e.wait_ge(sV.h, vI(pv[0], "V4", pv[1]))
                        if c >= 1:
                            e.wait_ge(sA.h, aI(c - 1, "A4", g))
                        e.matmul(PS_I[p_][:, :], C_c[c % 2][:, :], S_bf[:, gsl],
                                 start=True, stop=True).then_inc(sT.h, 1)
                        e.wait_ge(sV.h, vI(c, "V4", g))
                        e.matmul(PS_Y[p_][:, :], ident_b[:, :], tbuf[p_][:, :],
                                 start=True, stop=False, skip_group_check=True)
                        mi = None
                        for hh in range(4):
                            mi = e.matmul(PS_Y[p_][:, hh * 128:(hh + 1) * 128],
                                          M2[p_][:, hh * 128:(hh + 1) * 128],
                                          X_tm[c % 2][:, (4 * g + hh) * 128:
                                                      (4 * g + hh + 1) * 128],
                                          start=False, stop=(hh == 3),
                                          skip_group_check=True)
                        mi.then_inc(sT.h, 1)
                        if pv:
                            e.wait_ge(sA.h, aI(pv[0], "A4", pv[1]))
                        e.wait_ge(sV.h, vI(c, "V5", g))
                        e.wait_ge(sP.h, pI(c, g))
                        e.matmul(PS_S[p_][:, :], B_T[:, :], xw[p_][:, :],
                                 start=True, stop=False)
                        e.matmul(PS_S[p_][:, :], ident_b[:, :], t2b[p_][:, :],
                                 start=False, stop=True,
                                 skip_group_check=True).then_inc(sT.h, 1)
            sT.n += TPC * NCH

            @blk.vector
            def _(e):
                def emit_v1(c, g):
                    pv4 = _norm(c, g - 4)
                    if pv4:
                        e.wait_ge(sT.h, tT(pv4[0], pv4[1], 1))
                    hsl = slice(4 * g, 4 * g + 4)
                    e.tensor_tensor(
                        rhs_g[g % 4][:, :].rearrange("p (h q) -> p h q", h=4),
                        negtri[:, :].unsqueeze(1).broadcast_to([128, 4, 128]),
                        nA_T[:, hsl].unsqueeze(2).broadcast_to([128, 4, 128]),
                        OP.mult).then_inc(sV1.h, 1)

                for c in range(NCH):
                    e.wait_ge(sT.h, tPre(c, 4))
                    e.tensor_copy(nA_T[:, :], PS_D[0][0:128, 0:64]).then_inc(sV.h, 1)
                    e.tensor_copy(egi_T[:, :],
                                  PS_D[0][0:128, 64:128]).then_inc(sV.h, 1)
                    e.tensor_copy(w_T[:, :],
                                  PS_D[0][0:128, 128:192]).then_inc(sV.h, 1)
                    e.tensor_copy(eG_row[0:1, :],
                                  PS_D[0][0:1, 192:256]).then_inc(sV.h, 1)
                    e.wait_ge(sT.h, tPre(c, 5))
                    e.tensor_copy(eG_tile[:, :],
                                  PS_D[0][0:128, 256:320]).then_inc(sV.h, 1)
                    e.wait_ge(sT.h, tPre(c, 6))
                    e.tensor_tensor(S_cbT[:, :], PS_I[0][0:128, 0:128],
                                    tri01[:, :], OP.mult).then_inc(sV.h, 1)
                    e.wait_ge(sT.h, tPre(c, 7))
                    e.tensor_copy(B_T[:, :],
                                  PS_I[0][0:128, 128:256]).then_inc(sV.h, 1)
                    emit_v1(c, 0)
                    emit_v1(c, 1)
                    for i in range(NGRP + 2):
                        if i < NGRP:
                            g = i
                            p_ = g % 2
                            gsl = slice(g * GW, (g + 1) * GW)
                            hsl = slice(4 * g, 4 * g + 4)
                            pv = _norm(c, g - 2)
                            e.wait_ge(ldX.h, 16 * (c + 1))
                            if pv:
                                e.wait_ge(sT.h, tT(pv[0], pv[1], 4))
                            e.tensor_tensor(
                                xw[p_][:, :].rearrange("p (h q) -> p h q", h=4),
                                X_tm[c % 2][:, gsl].rearrange(
                                    "p (h q) -> p h q", h=4),
                                w_T[:, hsl].unsqueeze(2).broadcast_to(
                                    [128, 4, 128]),
                                OP.mult).then_inc(sV.h, 1)
                        if 1 <= i <= NGRP:
                            g = i - 1
                            p_ = g % 2
                            gsl = slice(g * GW, (g + 1) * GW)
                            hsl = slice(4 * g, 4 * g + 4)
                            pv = _norm(c, g - 2)
                            e.wait_ge(sA.h, aI(c, "A1", g))
                            if pv:
                                e.wait_ge(sT.h, tT(pv[0], pv[1], 3))
                            e.tensor_tensor(
                                M2t[p_][:, :].rearrange("p (h q) -> p h q", h=4),
                                E_t[p_][:, :].rearrange("p (h q) -> p h q", h=4),
                                S_cbT[:, :].unsqueeze(1).broadcast_to(
                                    [128, 4, 128]),
                                OP.mult)
                            e.tensor_tensor(M2[p_][:, :], M2t[p_][:, :],
                                            d_i[:, gsl], OP.add).then_inc(sV.h, 1)
                            e.wait_ge(sT.h, tT(c, g, 2))
                            e.tensor_tensor(
                                tbuf[p_][:, :].rearrange("p (h q) -> p h q", h=4),
                                PS_I[p_][:, :].rearrange("p (h q) -> p h q", h=4),
                                egi_T[:, hsl].unsqueeze(2).broadcast_to(
                                    [128, 4, 128]),
                                OP.mult).then_inc(sV.h, 1)
                        if 2 <= i <= NGRP + 1:
                            g = i - 2
                            p_ = g % 2
                            gsl = slice(g * GW, (g + 1) * GW)
                            e.wait_ge(sT.h, tT(c, g, 3))
                            e.wait_ge(ldZ.h, 32 * c + (16 if g < 8 else 32))
                            e.tensor_tensor(yg[:, gsl], PS_Y[p_][:, :],
                                            Zs_tm[g // 8][:, (g % 8) * GW:
                                                          (g % 8 + 1) * GW],
                                            OP.mult).then_inc(sV.h, 1)
                        if i + 2 < NGRP:
                            emit_v1(c, i + 2)
                    e.wait_ge(sA.h, aI(c, "A2", 15))
                    e.reduce_sum(ss_c[:, 0:1], ss16[:, :],
                                 axis=AX.X).then_inc(sV.h, 1)
                    e.wait_ge(sA.h, aI(c, "ex"))
                    if c >= 1:
                        e.wait_ge(spY.h, 16 * c)
                    e.tensor_scalar(yrms_t[:, :], yg[:, :], ss_c[:, 1:2], None,
                                    OP.mult).then_inc(sV.h, 1)
            sV.n += VPC * NCH
            sV1.n += NGRP * NCH

            @blk.gpsimd
            def _(e):
                for c in range(NCH):
                    for g in range(NGRP):
                        p_ = g % 2
                        gsl = slice(g * GW, (g + 1) * GW)
                        hsl = slice(4 * g, 4 * g + 4)
                        pv = _norm(c, g - 2)
                        if c >= 1:
                            e.wait_ge(sA.h, aI(c - 1, "A4", g))
                        if pv:
                            e.wait_ge(sT.h, tT(pv[0], pv[1], 4))
                        e.wait_ge(sV.h, c * VPC + 5)
                        e.tensor_tensor(
                            t2b[p_][:, :].rearrange("p (h q) -> p h q", h=4),
                            S_bf[:, gsl].rearrange("p (h q) -> p h q", h=4),
                            eG_tile[:, hsl].unsqueeze(2).broadcast_to(
                                [128, 4, 128]),
                            OP.mult).then_inc(sP.h, 1)
            sP.n += PPC * NCH

            @blk.scalar
            def _(e):
                for c in range(NCH):
                    for i in range(NGRP + 2):
                        if i < NGRP:
                            g = i
                            p_ = g % 2
                            pv = _norm(c, g - 2)
                            e.wait_ge(sT.h, tT(c, g, 1))
                            if pv:
                                e.wait_ge(sV.h, vI(pv[0], "V3", pv[1]))
                            e.activation(E_t[p_][:, :], PS_D[p_][:, :],
                                         AF.Exp).then_inc(sA.h, 1)
                        if 1 <= i <= NGRP:
                            g = i - 1
                            p_ = g % 2
                            gsl = slice(g * GW, (g + 1) * GW)
                            e.wait_ge(sT.h, tT(c, g, 4))
                            e.activation(S_bf[:, gsl], PS_S[p_][:, :],
                                         AF.Copy).then_inc(sA.h, 1)
                        if 2 <= i <= NGRP + 1:
                            g = i - 2
                            gsl = slice(g * GW, (g + 1) * GW)
                            e.wait_ge(sV.h, vI(c, "V7", g))
                            if g == 0 and c >= 1:
                                e.wait_ge(sV.h, vI(c - 1, "ss"))
                            e.activation(sq_scr[:, :], yg[:, gsl], AF.Square,
                                         accum_out=ss16[:, g:g + 1]).then_inc(
                                sA.h, 1)
                    e.wait_ge(sV.h, vI(c, "ss"))
                    e.activation(ss_c[:, 1:2], ss_c[:, 0:1], AF.Ln,
                                 scale=1.0 / DSSM,
                                 bias=eps_c[:, 0:1]).then_inc(sA.h, 1)
                    e.activation(ss_c[:, 1:2], ss_c[:, 1:2], AF.Exp,
                                 scale=-0.5).then_inc(sA.h, 1)
            sA.n += APC * NCH
        nc.all_engine_barrier()
    es34.close()

    ldYT, ldWo = sem("c_ldyt"), sem("c_ldwo")
    tdC, evC, spC, vC = sem("c_td"), sem("c_ev"), sem("c_sp"), sem("c_v")
    with contextlib.ExitStack() as ph:
        def psb(name, shape, dt):
            return ph.enter_context(nc.sbuf_tensor(name, shape, dt))
        yk = psb("yk", [128, NZB * HB], BF16)
        ystg = [psb(f"ystg{i}", [128, HB], BF16) for i in range(2)]
        wo = [psb(f"wo{i}", [128, NZB * 128], BF16) for i in range(2)]
        at_t = [psb(f"at_t{i}", [128, HB], BF16) for i in range(2)]
        psC = [ph.enter_context(nc.psum_tensor(f"psC{i}", [128, HB], F32))
               for i in range(2)]

        for hf in range(2):
            t0 = hf * HB
            b_ldyt, b_ldwo, b_td, b_ev, b_sp, b_v = (ldYT.n, ldWo.n, tdC.n,
                                                     evC.n, spC.n, vC.n)
            with nc.Block() as blk:
                @blk.sync
                def _(e):
                    for k in range(NZB):
                        if k >= 2:
                            e.wait_ge(vC.h, b_v + k - 1)
                        e.dma_start(ystg[k % 2][:, :],
                                    yr_sp[t0:t0 + HB, k * 128:(k + 1) * 128],
                                    transpose=True).then_inc(ldYT.h, 16)
                ldYT.n += 16 * NZB

                @blk.vector
                def _(e):
                    for k in range(NZB):
                        e.wait_ge(ldYT.h, b_ldyt + 16 * (k + 1))
                        e.tensor_scalar(yk[:, k * HB:(k + 1) * HB],
                                        ystg[k % 2][:, :], nwt_s[:, k:k + 1],
                                        None, OP.mult).then_inc(vC.h, 1)
                vC.n += NZB

                @blk.gpsimd
                def _(e):
                    for ob in range(32):
                        if ob >= 2:
                            e.wait_ge(tdC.h, b_td + ob - 1)
                        e.dma_start(
                            wo[ob % 2][:, :].rearrange("p (k c) -> p k c", k=NZB),
                            wout_h[:, ob * 128:(ob + 1) * 128].rearrange(
                                "(k p) c -> p k c", p=128),
                        ).then_inc(ldWo.h, 16)
                ldWo.n += 16 * 32

                @blk.tensor
                def _(e):
                    for ob in range(32):
                        e.wait_ge(ldWo.h, b_ldwo + 16 * (ob + 1))
                        if ob >= 2:
                            e.wait_ge(evC.h, b_ev + ob - 1)
                        mi = None
                        for k in range(NZB):
                            if ob == 0:
                                e.wait_ge(vC.h, b_v + k + 1)
                            for n in range(2):
                                mi = e.matmul(
                                    psC[ob % 2][:, n * 512:(n + 1) * 512],
                                    wo[ob % 2][:, k * 128:(k + 1) * 128],
                                    yk[:, k * HB + n * 512:k * HB + (n + 1) * 512],
                                    start=(k == 0), stop=(k == NZB - 1))
                        mi.then_inc(tdC.h, 1)
                tdC.n += 32

                @blk.vector
                def _(e):
                    for ob in range(32):
                        e.wait_ge(tdC.h, b_td + ob + 1)
                        if ob >= 2:
                            e.wait_ge(spC.h, b_sp + 16 * (ob - 1))
                        e.tensor_copy(at_t[ob % 2][:, :],
                                      psC[ob % 2][:, :]).then_inc(evC.h, 1)
                vC.n += 0

                @blk.scalar
                def _(e):
                    for ob in range(32):
                        e.wait_ge(evC.h, b_ev + ob + 1)
                        e.dma_start(at_sp[ob * 128:(ob + 1) * 128, t0:t0 + HB],
                                    at_t[ob % 2][:, :]).then_inc(spC.h, 16)
                    e.wait_ge(spC.h, b_sp + 16 * 32)
                evC.n += 32
                spC.n += 16 * 32
            nc.all_engine_barrier()

    ldE, vE, sE, spE = sem("e_ld"), sem("e_v"), sem("e_s"), sem("e_sp")
    ldF, vF, sF, stO = sem("f_ld"), sem("f_v"), sem("f_s"), sem("f_out")
    ldXT, ldWf = sem("d_ldxt"), sem("d_ldwf")
    tdD, evD, spD, vD = sem("d_td"), sem("d_ev"), sem("d_sp"), sem("d_v")
    with contextlib.ExitStack() as ph:
        def psb(name, shape, dt):
            return ph.enter_context(nc.sbuf_tensor(name, shape, dt))
        tm_t = [psb(f"tm_t{i}", [128, H], BF16) for i in range(2)]
        xr_t = [psb(f"xr_t{i}", [128, H], F32) for i in range(2)]
        res_t = [psb(f"res_t{i}", [128, H], F32) for i in range(2)]
        nres_t = [psb(f"nres_t{i}", [128, H], BF16) for i in range(2)]
        st_e = psb("st_e", [128, 8], F32)
        xk2 = psb("xk2", [128, NKB * HB], BF16)
        wf = [psb(f"wf{i}", [128, NKB * 128], BF16) for i in range(2)]
        la_s = psb("la_s", [128, NKB * 48], BF16)
        lb_s = psb("lb_s", [48, H], BF16)
        bias_s = psb("bias_s", [128, 32], F32)
        u48 = psb("u48", [48, HB], BF16)
        ff_t = [psb(f"ff_t{i}", [128, HB], BF16) for i in range(2)]
        psU = ph.enter_context(nc.psum_tensor("psU", [48, HB], F32))
        psD = [ph.enter_context(nc.psum_tensor(f"psDD{i}", [128, HB], F32))
               for i in range(3)]

        def e1_sync(e, tc):
            if tc == 0:
                e.wait_ge(spC.h, 16 * 32)
            if tc == 8:
                e.wait_ge(spC.h, 16 * 64)
            if tc >= 2:
                e.wait_ge(vE.h, 4 * (tc - 2) + 4)
                e.wait_ge(sE.h, 4 * (tc - 2) + 3)
            e.dma_start(tm_t[tc % 2][:, :], at_sp[0:H, tc * 128:(tc + 1) * 128],
                        transpose=True).then_inc(ldE.h, 16)
            e.dma_start(xr_t[tc % 2][:, :],
                        x_h[tc * 128:(tc + 1) * 128, :]).then_inc(ldE.h, 16)

        def e1_scalar_p(e, tc, p):
            if p == 0:
                e.wait_ge(ldE.h, 32 * (tc + 1))
                if tc >= 2:
                    e.wait_ge(vE.h, 4 * (tc - 2) + 3)
                e.activation(res_t[tc % 2][:, :], tm_t[tc % 2][:, :],
                             AF.Square).then_inc(sE.h, 1)
            elif p == 1:
                e.wait_ge(vE.h, 4 * tc + 1)
                e.activation(st_e[:, 6:7], st_e[:, 4:5], AF.Ln, scale=-1.0,
                             bias=eps_c[:, 0:1])
                e.activation(st_e[:, 7:8], st_e[:, 6:7], AF.Exp,
                             scale=-0.5).then_inc(sE.h, 1)
            elif p == 2:
                e.wait_ge(vE.h, 4 * tc + 2)
                e.activation(xr_t[tc % 2][:, :], res_t[tc % 2][:, :],
                             AF.Square).then_inc(sE.h, 1)
            else:
                e.wait_ge(vE.h, 4 * tc + 3)
                e.activation(st_e[:, 6:7], st_e[:, 4:5], AF.Ln, scale=-1.0,
                             bias=eps_c[:, 0:1])
                e.activation(st_e[:, 7:8], st_e[:, 6:7], AF.Exp,
                             scale=-0.5).then_inc(sE.h, 1)

        def e1_scalar(e, tc):
            for p in range(4):
                e1_scalar_p(e, tc, p)

        def e1_vector_p(e, tc, p):
            if p == 0:
                e.wait_ge(ldE.h, 32 * (tc + 1))
                e.reduce_sum(st_e[:, 0:1], tm_t[tc % 2][:, :], axis=AX.X)
                e.wait_ge(sE.h, 4 * tc + 1)
                e.reduce_sum(st_e[:, 1:2], res_t[tc % 2][:, :], axis=AX.X)
                e.tensor_scalar(st_e[:, 2:3], st_e[:, 0:1], 1.0 / H, None, OP.mult)
                e.tensor_scalar(st_e[:, 3:4], st_e[:, 1:2], 1.0 / H, None, OP.mult)
                e.tensor_scalar(st_e[:, 0:1], st_e[:, 0:1], 0.0, None, OP.mult)
                e.scalar_tensor_tensor(st_e[:, 4:5], st_e[:, 2:3], st_e[:, 2:3],
                                       st_e[:, 3:4], OP.mult,
                                       OP.subtract).then_inc(vE.h, 1)
            elif p == 1:
                e.wait_ge(sE.h, 4 * tc + 2)
                if tc >= 2:
                    e.wait_ge(spE.h, 32 * (tc - 2) + 16)
                e.tensor_scalar(res_t[tc % 2][:, :], tm_t[tc % 2][:, :],
                                st_e[:, 2:3], st_e[:, 7:8], OP.subtract, OP.mult)
                e.tensor_tensor(res_t[tc % 2][:, :], res_t[tc % 2][:, :],
                                xr_t[tc % 2][:, :], OP.add).then_inc(vE.h, 1)
            elif p == 2:
                e.wait_ge(sE.h, 4 * tc + 3)
                e.reduce_sum(st_e[:, 0:1], res_t[tc % 2][:, :], axis=AX.X)
                e.reduce_sum(st_e[:, 1:2], xr_t[tc % 2][:, :], axis=AX.X)
                e.tensor_scalar(st_e[:, 2:3], st_e[:, 0:1], 1.0 / H, None, OP.mult)
                e.tensor_scalar(st_e[:, 3:4], st_e[:, 1:2], 1.0 / H, None, OP.mult)
                e.tensor_scalar(st_e[:, 0:1], st_e[:, 0:1], 0.0, None, OP.mult)
                e.scalar_tensor_tensor(st_e[:, 4:5], st_e[:, 2:3], st_e[:, 2:3],
                                       st_e[:, 3:4], OP.mult,
                                       OP.subtract).then_inc(vE.h, 1)
            else:
                e.wait_ge(sE.h, 4 * tc + 4)
                if tc >= 2:
                    e.wait_ge(spE.h, 32 * (tc - 2) + 32)
                e.tensor_scalar(nres_t[tc % 2][:, :], res_t[tc % 2][:, :],
                                st_e[:, 2:3], st_e[:, 7:8], OP.subtract,
                                OP.mult).then_inc(vE.h, 1)

        def e1_vector(e, tc):
            for p in range(4):
                e1_vector_p(e, tc, p)

        def e1_gpsimd(e, tc):
            e.wait_ge(vE.h, 4 * tc + 2)
            e.dma_start(x2_sp[tc * 128:(tc + 1) * 128, :],
                        res_t[tc % 2][:, :]).then_inc(spE.h, 16)
            e.wait_ge(vE.h, 4 * tc + 4)
            e.dma_start(xn2_sp[tc * 128:(tc + 1) * 128, :],
                        nres_t[tc % 2][:, :]).then_inc(spE.h, 16)

        def e2_sync(e, tc):
            if tc == 0:
                e.wait_ge(spD.h, 16 * 32)
            if tc == 8:
                e.wait_ge(spD.h, 16 * 64)
            if tc >= 2:
                e.wait_ge(vF.h, 3 * (tc - 2) + 3)
                e.wait_ge(sF.h, 2 * (tc - 2) + 1)
            e.dma_start(tm_t[tc % 2][:, :], ff_sp[0:H, tc * 128:(tc + 1) * 128],
                        transpose=True).then_inc(ldF.h, 16)
            e.dma_start(xr_t[tc % 2][:, :],
                        x2_sp[tc * 128:(tc + 1) * 128, :]).then_inc(ldF.h, 16)

        def e2_scalar_p(e, tc, p):
            if p == 0:
                e.wait_ge(ldF.h, 32 * (tc + 1))
                if tc >= 2:
                    e.wait_ge(vF.h, 3 * (tc - 2) + 1)
                e.activation(res_t[tc % 2][:, :], tm_t[tc % 2][:, :],
                             AF.Square).then_inc(sF.h, 1)
            elif p == 1:
                e.wait_ge(vF.h, 3 * tc + 1)
                e.activation(st_e[:, 6:7], st_e[:, 4:5], AF.Ln, scale=-1.0,
                             bias=eps_c[:, 0:1])
                e.activation(st_e[:, 7:8], st_e[:, 6:7], AF.Exp,
                             scale=-0.5).then_inc(sF.h, 1)

        def e2_scalar(e, tc):
            for p in range(2):
                e2_scalar_p(e, tc, p)

        def e2_vector_p(e, tc, p):
            if p == 0:
                e.wait_ge(ldF.h, 32 * (tc + 1))
                e.reduce_sum(st_e[:, 0:1], tm_t[tc % 2][:, :], axis=AX.X)
                e.wait_ge(sF.h, 2 * tc + 1)
                e.reduce_sum(st_e[:, 1:2], res_t[tc % 2][:, :], axis=AX.X)
                e.tensor_scalar(st_e[:, 2:3], st_e[:, 0:1], 1.0 / H, None, OP.mult)
                e.tensor_scalar(st_e[:, 3:4], st_e[:, 1:2], 1.0 / H, None, OP.mult)
                e.tensor_scalar(st_e[:, 0:1], st_e[:, 0:1], 0.0, None, OP.mult)
                e.scalar_tensor_tensor(st_e[:, 4:5], st_e[:, 2:3], st_e[:, 2:3],
                                       st_e[:, 3:4], OP.mult,
                                       OP.subtract).then_inc(vF.h, 1)
            elif p == 1:
                e.wait_ge(sF.h, 2 * tc + 2)
                if tc >= 2:
                    e.wait_ge(stO.h, 16 * (tc - 1))
                e.tensor_scalar(res_t[tc % 2][:, :], tm_t[tc % 2][:, :],
                                st_e[:, 2:3], st_e[:, 7:8], OP.subtract,
                                OP.mult).then_inc(vF.h, 1)
            else:
                e.tensor_tensor(res_t[tc % 2][:, :], res_t[tc % 2][:, :],
                                xr_t[tc % 2][:, :], OP.add).then_inc(vF.h, 1)

        def e2_vector(e, tc):
            for p in range(3):
                e2_vector_p(e, tc, p)

        def e2_gpsimd(e, tc):
            e.wait_ge(vF.h, 3 * tc + 3)
            e.dma_start(out_h[tc * 128:(tc + 1) * 128, :],
                        res_t[tc % 2][:, :]).then_inc(stO.h, 16)

        with nc.Block() as blk:
            @blk.sync
            def _(e):
                for tc in range(8):
                    e1_sync(e, tc)

            @blk.scalar
            def _(e):
                for tc in range(8):
                    e1_scalar(e, tc)

            @blk.vector
            def _(e):
                for tc in range(8):
                    e1_vector(e, tc)

            @blk.gpsimd
            def _(e):
                for tc in range(8):
                    e1_gpsimd(e, tc)
        ldE.n += 32 * 8
        sE.n += 4 * 8
        vE.n += 4 * 8
        spE.n += 32 * 8
        nc.all_engine_barrier()

        for hf in range(2):
            t0 = hf * HB
            b_xt, b_wf, b_td, b_ev, b_sp = ldXT.n, ldWf.n, tdD.n, evD.n, spD.n
            epi_emit = (lambda fns, e, j: fns(e, 8 + j)) if hf == 0 else                        (lambda fns, e, j: fns(e, j))
            e_s, e_sc, e_v, e_g = (e1_sync, e1_scalar, e1_vector, e1_gpsimd)                 if hf == 0 else (e2_sync, e2_scalar, e2_vector, e2_gpsimd)
            with nc.Block() as blk:
                @blk.sync
                def _(e):
                    if hf == 0:
                        e.wait_ge(spE.h, 32 * 8)
                    if hf == 1:
                        e.wait_ge(tdD.h, b_td)
                        e.wait_ge(spE.h, 32 * 16)
                    for k in range(NKB):
                        e.dma_start(xk2[:, k * HB:(k + 1) * HB],
                                    xn2_sp[t0:t0 + HB, k * 128:(k + 1) * 128],
                                    transpose=True).then_inc(ldXT.h, 16)
                    for j in range(8):
                        epi_emit(e_s, e, j)
                ldXT.n += 16 * NKB
                ldE.n += (32 * 8 if hf == 0 else 0)
                ldF.n += (32 * 8 if hf == 1 else 0)

                @blk.gpsimd
                def _(e):
                    if hf == 0:
                        e.dma_start(bias_s[:, :], biast_h[:, :]).then_inc(
                            ldWf.h, 16)
                        e.dma_start(la_s[:, :].rearrange("p (k r) -> p k r",
                                                         k=NKB),
                                    lat_h[:, :].rearrange("(k p) r -> p k r",
                                                          p=128)).then_inc(
                            ldWf.h, 16)
                        e.dma_start(lb_s[:, :], lbt_h[:, :]).then_inc(ldWf.h, 16)
                    for ob in range(32):
                        if ob >= 2:
                            e.wait_ge(tdD.h, b_td + 1 + ob - 1)
                        e.dma_start(
                            wf[ob % 2][:, :].rearrange("p (k c) -> p k c", k=NKB),
                            wft_h[:, ob * 128:(ob + 1) * 128].rearrange(
                                "(k p) c -> p k c", p=128),
                        ).then_inc(ldWf.h, 16)
                        if ob % 4 == 3:
                            epi_emit(e_g, e, ob // 4)
                    if hf == 1:
                        e.wait_ge(spE.h, 32 * 16)
                ldWf.n += 16 * 32 + (48 if hf == 0 else 0)
                spE.n += (32 * 8 if hf == 0 else 0)
                stO.n += (16 * 8 if hf == 1 else 0)

                @blk.tensor
                def _(e):
                    if hf == 0:
                        e.wait_ge(ldWf.h, 16 * 3)
                    mi = None
                    for k in range(NKB):
                        e.wait_ge(ldXT.h, b_xt + 16 * (k + 1))
                        for n in range(2):
                            mi = e.matmul(
                                psU[0:48, n * 512:(n + 1) * 512],
                                la_s[:, k * 48:(k + 1) * 48],
                                xk2[:, k * HB + n * 512:k * HB + (n + 1) * 512],
                                start=(k == 0), stop=(k == NKB - 1))
                    mi.then_inc(tdD.h, 1)
                    for ob in range(32):
                        e.wait_ge(ldWf.h, b_wf + (48 if hf == 0 else 0)
                                  + 16 * (ob + 1))
                        if ob == 0:
                            e.wait_ge(vD.h, hf + 1)
                        if ob >= 3:
                            e.wait_ge(evD.h, b_ev + ob - 2)
                        for k in range(NKB):
                            for n in range(2):
                                e.matmul(psD[ob % 3][:, n * 512:(n + 1) * 512],
                                         wf[ob % 2][:, k * 128:(k + 1) * 128],
                                         xk2[:, k * HB + n * 512:
                                             k * HB + (n + 1) * 512],
                                         start=(k == 0), stop=False)
                        mi = None
                        for n in range(2):
                            mi = e.matmul(psD[ob % 3][:, n * 512:(n + 1) * 512],
                                          lb_s[0:48, ob * 128:(ob + 1) * 128],
                                          u48[0:48, n * 512:(n + 1) * 512],
                                          start=False, stop=True)
                        mi.then_inc(tdD.h, 1)
                tdD.n += 33

                @blk.vector
                def _(e):
                    if hf == 0:
                        e.wait_ge(ldWf.h, 16 * 3)
                    e.wait_ge(tdD.h, b_td + 1)
                    e.tensor_scalar(u48[0:48, :], psU[0:48, :], m48[0:48, 0:1],
                                    None, OP.mult).then_inc(vD.h, 1)
                    for ob in range(32):
                        e.wait_ge(tdD.h, b_td + 1 + ob + 1)
                        if ob >= 2:
                            e.wait_ge(spD.h, b_sp + 16 * (ob - 1))
                        e.tensor_scalar(ff_t[ob % 2][:, :], psD[ob % 3][:, :],
                                        bias_s[:, ob:ob + 1], None,
                                        OP.add).then_inc(evD.h, 1)
                        if hf == 0:
                            e1_vector_p(e, 8 + ob // 4, ob % 4)
                        elif ob % 4 < 3:
                            e2_vector_p(e, ob // 4, ob % 4)
                vD.n += 1
                evD.n += 32
                vE.n += (4 * 8 if hf == 0 else 0)
                vF.n += (3 * 8 if hf == 1 else 0)

                @blk.scalar
                def _(e):
                    for ob in range(32):
                        e.wait_ge(evD.h, b_ev + ob + 1)
                        e.dma_start(ff_sp[ob * 128:(ob + 1) * 128, t0:t0 + HB],
                                    ff_t[ob % 2][:, :]).then_inc(spD.h, 16)
                        if hf == 0:
                            e1_scalar_p(e, 8 + ob // 4, ob % 4)
                        elif ob % 4 < 2:
                            e2_scalar_p(e, ob // 4, ob % 4)
                    e.wait_ge(spD.h, b_sp + 16 * 32)
                spD.n += 16 * 32
                sE.n += (4 * 8 if hf == 0 else 0)
                sF.n += (2 * 8 if hf == 1 else 0)
            nc.all_engine_barrier()

        with nc.Block() as blk:
            @blk.sync
            def _(e):
                for tc in range(8, 16):
                    e2_sync(e, tc)

            @blk.scalar
            def _(e):
                for tc in range(8, 16):
                    e2_scalar(e, tc)
                e.wait_ge(stO.h, 16 * NCH)

            @blk.vector
            def _(e):
                for tc in range(8, 16):
                    e2_vector(e, tc)
                e.wait_ge(stO.h, 16 * NCH)

            @blk.gpsimd
            def _(e):
                for tc in range(8, 16):
                    e2_gpsimd(e, tc)
                e.wait_ge(stO.h, 16 * NCH)
        ldF.n += 32 * 8
        sF.n += 2 * 8
        vF.n += 3 * 8
        stO.n += 16 * 8

    es.close()
    return nc


def make_in_maps(ins, cores=range(8)):
    x = np.asarray(ins["L0_attn_hidden_states"])
    W_in = np.ascontiguousarray(np.asarray(ins["L0_attn_W_in"], dtype=np.float32))
    conv_w = np.asarray(ins["L0_attn_conv_w"], dtype=np.float32)
    convw_t = np.ascontiguousarray(conv_w.reshape(NXB, 128, KCONV)
                                   .transpose(1, 0, 2).reshape(128, NXB * KCONV))
    dtb = np.ascontiguousarray(np.asarray(ins["L0_attn_dt_bias"],
                                          dtype=np.float32).reshape(1, NH))
    alog = np.ascontiguousarray(np.asarray(ins["L0_attn_A_log"],
                                           dtype=np.float32).reshape(1, NH))
    dpar = np.ascontiguousarray(np.asarray(ins["L0_attn_D_param"],
                                           dtype=np.float32).reshape(1, NH))
    nwt = np.ascontiguousarray(np.asarray(ins["L0_attn_norm_weight"],
                                          dtype=np.float32).reshape(NZB, 128).T)
    w_out = np.ascontiguousarray(np.asarray(ins["L0_attn_W_out"],
                                            dtype=np.float32))
    wf_t = np.ascontiguousarray(np.asarray(ins["L0_ffn_weight"],
                                           dtype=np.float32).T)
    bias_t = np.ascontiguousarray(np.asarray(ins["L0_ffn_bias"],
                                             dtype=np.float32).reshape(32, 128).T)
    la_t = np.ascontiguousarray(np.asarray(ins["L0_ffn_lora_A"], dtype=np.float32)
                                .transpose(2, 0, 1).reshape(H, 48))
    lb_t = np.ascontiguousarray(np.asarray(ins["L0_ffn_lora_B"], dtype=np.float32)
                                .transpose(0, 2, 1).reshape(48, H))
    lsi = np.ascontiguousarray(
        np.asarray(ins["L0_ffn_lora_start_indices"]).astype(np.int32).reshape(1, 4))
    maps = []
    for b in cores:
        maps.append({
            "x": np.ascontiguousarray(x[b]).astype(np.float32),
            "w_in": W_in, "conv_w_t": convw_t, "dtb": dtb, "alog": alog,
            "dpar": dpar, "nwt": nwt, "w_out": w_out, "wf_t": wf_t,
            "bias_t": bias_t, "la_t": la_t, "lb_t": lb_t, "lsi": lsi,
            "bval": np.array([[float(b)]], dtype=np.float32),
        })
    return maps


_NC_CACHE = {}


def kernel(**inputs):
    ins = inputs
    if "prog" not in _NC_CACHE:
        _NC_CACHE["prog"] = build(debug=False)
    nc = _NC_CACHE["prog"]
    maps = make_in_maps(ins, cores=range(8))
    from concourse.bass_utils import run_bass_kernel_spmd
    res = run_bass_kernel_spmd(nc, maps, core_ids=list(range(8)))
    out = np.stack([np.asarray(res.results[b]["out"]) for b in range(8)], 0)
    return out.astype(np.float32)


# version 09
# speedup vs baseline: 1.1362x; 1.0270x over previous
import numpy as np
import ml_dtypes
import contextlib
import concourse.bass as bass
import concourse.mybir as mybir

F32 = mybir.dt.float32
BF16 = mybir.dt.bfloat16
I32 = mybir.dt.int32
AF = mybir.ActivationFunctionType
OP = mybir.AluOpType
AX = mybir.AxisListType
bf16 = ml_dtypes.bfloat16

H, DSSM, CONV, PROJ = 4096, 8192, 8448, 16704
NH, Q, S = 64, 128, 2048
NCH = S // Q
KCONV = 4
EPS = 1e-5
HB = S // 2
NZB = DSSM // 128
NXB = CONV // 128
NKB = H // 128
NGRP = 16
GW = 512


def build(debug=False):
    nc = bass.Bass("TRN2", target_bir_lowering=False)
    es = contextlib.ExitStack()

    def dram(name, shape, dt, kind):
        return nc.dram_tensor(name, shape, dt, kind=kind).ap()

    def sbuf(name, shape, dt):
        return es.enter_context(nc.sbuf_tensor(name, shape, dt))

    EI, EO = "ExternalInput", "ExternalOutput"
    SCR = EO if debug else "Internal"

    x_h = dram("x", [S, H], F32, EI)
    win_h = dram("w_in", [H, PROJ], F32, EI)
    convw_h = dram("conv_w_t", [128, NXB * KCONV], F32, EI)
    dtb_h = dram("dtb", [1, NH], F32, EI)
    alog_h = dram("alog", [1, NH], F32, EI)
    dpar_h = dram("dpar", [1, NH], F32, EI)
    nwt_h = dram("nwt", [128, NZB], F32, EI)
    wout_h = dram("w_out", [DSSM, H], F32, EI)
    wft_h = dram("wf_t", [H, H], F32, EI)
    biast_h = dram("bias_t", [128, 32], F32, EI)
    lat_h = dram("la_t", [H, 48], F32, EI)
    lbt_h = dram("lb_t", [48, H], F32, EI)
    lsi_h = dram("lsi", [1, 4], I32, EI)
    bval_h = dram("bval", [1, 1], F32, EI)

    xn_sp = dram("xn_sp", [S, H], BF16, SCR)
    zs_sp = dram("zs_sp", [DSSM, S], BF16, SCR)
    xs_sp = dram("xs_sp", [CONV, S], BF16, SCR)
    yr_sp = dram("yr_sp", [S, DSSM], BF16, SCR)
    at_sp = dram("at_sp", [H, S], BF16, SCR)
    x2_sp = dram("x2_sp", [S, H], F32, SCR)
    xn2_sp = dram("xn2_sp", [S, H], BF16, SCR)
    ff_sp = dram("ff_sp", [H, S], BF16, SCR)
    rawb = dram("rawb", [256, S], F32, SCR)
    out_h = dram("out", [S, H], F32, EO)

    ident_f = sbuf("ident_f", [128, 128], F32)
    ident_b = sbuf("ident_b", [128, 128], BF16)
    ones_row = sbuf("ones_row", [1, 128], F32)
    ones64 = sbuf("ones64", [64, 128], F32)
    convw_s = sbuf("convw_s", [128, NXB * KCONV], F32)
    dtb_r = sbuf("dtb_r", [1, NH], F32)
    alog_r = sbuf("alog_r", [1, NH], F32)
    dpar_r = sbuf("dpar_r", [1, NH], F32)
    nwt_s = sbuf("nwt_s", [128, NZB], F32)
    lsi_s = sbuf("lsi_s", [1, 4], I32)
    lsi_f = sbuf("lsi_f", [1, 4], F32)
    bval_s = sbuf("bval_s", [1, 1], F32)
    dtb_c = sbuf("dtb_c", [NH, 1], F32)
    a_c = sbuf("a_c", [NH, 1], F32)
    m48 = sbuf("m48", [48, 1], F32)
    mrow = sbuf("mrow", [1, 48], F32)
    g4 = sbuf("g4", [1, 4], F32)
    g4b = sbuf("g4b", [1, 4], F32)
    sc1 = sbuf("sc1", [1, 8], F32)
    dt_fm = sbuf("dt_fm", [NH, S], F32)
    eps_c = sbuf("eps_c", [128, 1], F32)
    warm_i = sbuf("warm_i", [128, 256], F32)
    warm_o = sbuf("warm_o", [128, 256], F32)

    sems = {}

    def sem(name):
        if name not in sems:
            class _S:
                pass
            s = _S()
            s.h = es.enter_context(nc.semaphore(name))
            s.n = 0
            sems[name] = s
        return sems[name]

    with nc.Block() as blk:
        @blk.gpsimd
        def _(e):
            e.dma_reset(range(152, 256))
            e.sem_clear(range(152, 256))
    nc.all_engine_barrier()

    ld0, v0, s0, t0_ = sem("ld0"), sem("v0"), sem("s0"), sem("t0")
    with nc.Block() as blk:
        @blk.sync
        def _(e):
            for dst, src in [(convw_s, convw_h), (dtb_r, dtb_h), (alog_r, alog_h),
                             (dpar_r, dpar_h), (nwt_s, nwt_h), (lsi_s, lsi_h),
                             (bval_s, bval_h)]:
                e.dma_start(dst[:, :], src[:, :]).then_inc(ld0.h, 16)
        ld0.n += 16 * 7

        @blk.gpsimd
        def _(e):
            e.memset(ident_f[:, :], 1.0)
            e.affine_select(ident_f[:, :], ident_f[:, :], [[-1, 128]],
                            OP.is_equal, 0.0, base=0, channel_multiplier=1)
            e.memset(ident_b[:, :], 1.0)
            e.affine_select(ident_b[:, :], ident_b[:, :], [[-1, 128]],
                            OP.is_equal, 0.0, base=0, channel_multiplier=1)
            e.memset(ones_row[:, :], 1.0)
            e.memset(eps_c[:, :], EPS)
            e.memset(warm_i[:, :], 0.5)
            e.memset(ones64[:, :], 1.0).then_inc(v0.h, 1)
        v0.n += 1

        @blk.vector
        def _(e):
            e.wait_ge(ld0.h, 16 * 7)
            e.wait_ge(v0.h, 1)

            def spacer():
                e.tensor_scalar(g4[0:1, 0:4], ones64[0:1, 0:4], 0.0, None, OP.mult)
            e.tensor_copy(lsi_f[:, :], lsi_s[:, :])
            spacer()
            e.tensor_scalar(sc1[0:1, 0:4], lsi_f[:, :], -1.0, 0.5, OP.mult, OP.add)
            spacer()
            e.tensor_scalar(sc1[0:1, 4:8], sc1[0:1, 0:4], bval_s[0:1, 0:1], None,
                            OP.add).then_inc(v0.h, 1)
        v0.n += 1

        @blk.scalar
        def _(e):
            e.wait_ge(v0.h, v0.n)
            for fn in (AF.Silu, AF.Square, AF.Abs, AF.Relu, AF.Sign, AF.Exp,
                       AF.Ln, AF.Copy):
                e.activation(mrow[0:1, 0:1], eps_c[0:1, 0:1], fn)
            e.activation(g4[:, :], sc1[0:1, 4:8], AF.Sign)
            e.activation(g4b[:, :], g4[:, :], AF.Copy, scale=0.5,
                         bias=0.5).then_inc(s0.h, 1)
        s0.n += 1

        @blk.vector
        def _(e):
            e.wait_ge(s0.h, s0.n)

            def spacer():
                e.tensor_scalar(g4[0:1, 0:4], ones64[0:1, 0:4], 0.0, None, OP.mult)
            e.tensor_scalar(sc1[0:1, 0:1], g4b[0:1, 3:4], 1.0, 3.0, OP.mult, OP.add)
            e.tensor_scalar(sc1[0:1, 2:3], g4b[0:1, 1:2], -1.0, 1.0, OP.mult, OP.add)
            e.tensor_scalar(sc1[0:1, 4:5], g4b[0:1, 1:2], 2.0, None, OP.mult)
            e.tensor_scalar(sc1[0:1, 5:6], g4b[0:1, 2:3], -1.0, 1.0, OP.mult, OP.add)
            e.tensor_tensor(sc1[0:1, 1:2], sc1[0:1, 0:1], g4b[0:1, 2:3], OP.mult)
            e.tensor_tensor(sc1[0:1, 3:4], sc1[0:1, 2:3], g4b[0:1, 0:1], OP.mult)
            spacer()
            e.tensor_tensor(sc1[0:1, 4:5], sc1[0:1, 4:5], sc1[0:1, 3:4], OP.add)
            spacer()
            e.tensor_tensor(sc1[0:1, 5:6], sc1[0:1, 5:6], sc1[0:1, 4:5], OP.mult)
            spacer()
            e.tensor_tensor(sc1[0:1, 6:7], sc1[0:1, 1:2], sc1[0:1, 5:6], OP.add)
            spacer()
            i = None
            for k in range(3):
                e.tensor_scalar(sc1[0:1, 7:8], sc1[0:1, 6:7], -(k + 2.0), None,
                                OP.add)
                spacer()
                e.tensor_tensor(sc1[0:1, 7:8], sc1[0:1, 7:8], sc1[0:1, 7:8], OP.mult)
                spacer()
                e.tensor_scalar(sc1[0:1, 7:8], sc1[0:1, 7:8], -1.0, 1.0, OP.mult,
                                OP.add)
                spacer()
                e.tensor_relu(sc1[0:1, 7:8], sc1[0:1, 7:8])
                spacer()
                i = e.tensor_scalar(mrow[0:1, k * 16:(k + 1) * 16],
                                    ones_row[0:1, 0:16], sc1[0:1, 7:8], None,
                                    OP.mult)
            i.then_inc(v0.h, 1)
        v0.n += 1

    nc.all_engine_barrier()

    with contextlib.ExitStack() as ph:
        ps0 = ph.enter_context(nc.psum_tensor("ps0", [128, 512], F32))
        with nc.Block() as blk:
            @blk.tensor
            def _(e):
                e.matmul(ps0[0:64, 0:1], dtb_r[0:1, 0:64], ones_row[0:1, 0:1],
                         start=True, stop=True)
                e.matmul(ps0[0:64, 1:2], alog_r[0:1, 0:64], ones_row[0:1, 0:1],
                         start=True, stop=True)
                e.matmul(ps0[0:48, 2:3], mrow[0:1, 0:48], ones_row[0:1, 0:1],
                         start=True, stop=True).then_inc(t0_.h, 1)
            t0_.n += 1

            @blk.vector
            def _(e):
                e.wait_ge(t0_.h, t0_.n)
                e.tensor_copy(dtb_c[0:64, 0:1], ps0[0:64, 0:1])
                e.tensor_copy(m48[0:48, 0:1], ps0[0:48, 2:3]).then_inc(v0.h, 1)
            v0.n += 1

            @blk.scalar
            def _(e):
                e.wait_ge(v0.h, v0.n)
                e.activation(a_c[0:64, 0:1], ps0[0:64, 1:2],
                             AF.Exp).then_inc(s0.h, 1)
            s0.n += 1
        nc.all_engine_barrier()

    ld1, v1, s1, sp1 = sem("ld1"), sem("v1"), sem("s1"), sem("sp1")
    with contextlib.ExitStack() as ph:
        def psb(name, shape, dt):
            return ph.enter_context(nc.sbuf_tensor(name, shape, dt))
        x_t = [psb(f"x_t{i}", [128, H], F32) for i in range(2)]
        sq_t = [psb(f"sq_t{i}", [128, H], F32) for i in range(2)]
        xn_t = [psb(f"xn_t{i}", [128, H], BF16) for i in range(2)]
        st_t = psb("st_t", [128, 8], F32)

        with nc.Block() as blk:
            @blk.sync
            def _(e):
                for tc in range(NCH):
                    if tc >= 2:
                        e.wait_ge(v1.h, 2 * (tc - 2) + 2)
                        e.wait_ge(s1.h, 2 * (tc - 2) + 1)
                    e.dma_start(x_t[tc % 2][:, :],
                                x_h[tc * 128:(tc + 1) * 128, :]).then_inc(ld1.h, 16)
            ld1.n += 16 * NCH

            @blk.scalar
            def _(e):
                for tc in range(NCH):
                    e.wait_ge(ld1.h, 16 * (tc + 1))
                    if tc >= 2:
                        e.wait_ge(v1.h, 2 * (tc - 2) + 1)
                    e.activation(sq_t[tc % 2][:, :], x_t[tc % 2][:, :],
                                 AF.Square).then_inc(s1.h, 1)
                    e.wait_ge(v1.h, 2 * tc + 1)
                    e.activation(st_t[:, 6:7], st_t[:, 4:5], AF.Ln, scale=-1.0,
                                 bias=eps_c[:, 0:1])
                    e.activation(st_t[:, 7:8], st_t[:, 6:7], AF.Exp,
                                 scale=-0.5).then_inc(s1.h, 1)
            s1.n += 2 * NCH

            @blk.vector
            def _(e):
                for tc in range(NCH):
                    e.wait_ge(ld1.h, 16 * (tc + 1))
                    e.reduce_sum(st_t[:, 0:1], x_t[tc % 2][:, :], axis=AX.X)
                    e.wait_ge(s1.h, 2 * tc + 1)
                    e.reduce_sum(st_t[:, 1:2], sq_t[tc % 2][:, :], axis=AX.X)
                    e.tensor_scalar(st_t[:, 2:3], st_t[:, 0:1], 1.0 / H, None,
                                    OP.mult)
                    e.tensor_scalar(st_t[:, 3:4], st_t[:, 1:2], 1.0 / H, None,
                                    OP.mult)
                    e.tensor_scalar(st_t[:, 0:1], st_t[:, 0:1], 0.0, None, OP.mult)
                    e.scalar_tensor_tensor(st_t[:, 4:5], st_t[:, 2:3],
                                           st_t[:, 2:3], st_t[:, 3:4],
                                           OP.mult, OP.subtract).then_inc(v1.h, 1)
                    e.wait_ge(s1.h, 2 * tc + 2)
                    if tc >= 2:
                        e.wait_ge(sp1.h, 16 * (tc - 1))
                    e.tensor_scalar(xn_t[tc % 2][:, :], x_t[tc % 2][:, :],
                                    st_t[:, 2:3], st_t[:, 7:8],
                                    OP.subtract, OP.mult).then_inc(v1.h, 1)
            v1.n += 2 * NCH

            @blk.gpsimd
            def _(e):
                for tc in range(NCH):
                    e.wait_ge(v1.h, 2 * tc + 2)
                    e.dma_start(xn_sp[tc * 128:(tc + 1) * 128, :],
                                xn_t[tc % 2][:, :]).then_inc(sp1.h, 16)
                e.wait_ge(sp1.h, 16 * NCH)
            sp1.n += 16 * NCH
        nc.all_engine_barrier()

    CB_ORDER = list(range(64, 131)) + list(range(64))
    ldT, ldW = sem("a_ldT"), sem("a_ldW")
    tdA, evA, cvA, siA = sem("a_td"), sem("a_ev"), sem("a_cv"), sem("a_si")
    spX, spZ = sem("a_spx"), sem("a_spz")
    with contextlib.ExitStack() as ph:
        def psb(name, shape, dt):
            return ph.enter_context(nc.sbuf_tensor(name, shape, dt))
        xn_fm = psb("xn_fm", [128, NKB * HB], BF16)
        wb = [psb(f"wb{i}", [128, NKB * 128], BF16) for i in range(2)]
        conv_in = psb("conv_in", [128, HB + 3], F32)
        scr1 = psb("scr1", [128, HB], F32)
        scr2 = [psb(f"scr2_{i}", [128, HB], F32) for i in range(2)]
        xs_t = [psb(f"xs_t{i}", [128, HB], BF16) for i in range(2)]
        zs_t = [psb(f"zs_t{i}", [128, HB], BF16) for i in range(2)]
        bsave = psb("bsave", [128, NXB * 3], F32)
        rawdbg_t = [psb(f"rawdbg{i}", [128, HB], F32) for i in range(2)] if debug else None
        pa_ps = [ph.enter_context(nc.psum_tensor(f"pa_ps{i}", [128, 1024], F32))
                 for i in range(2)]

        with nc.Block() as blk:
            @blk.vector
            def _(e):
                e.memset(conv_in[:, 0:3], 0.0)
        nc.all_engine_barrier()

        for hf in range(2):
            t0 = hf * HB
            b_ldT, b_ldW, b_td, b_ev = ldT.n, ldW.n, tdA.n, evA.n
            b_cv, b_si, b_spx, b_spz = cvA.n, siA.n, spX.n, spZ.n

            with nc.Block() as blk:
                @blk.sync
                def _(e):
                    for k in range(NKB):
                        e.dma_start(xn_fm[:, k * HB:(k + 1) * HB],
                                    xn_sp[t0:t0 + HB, k * 128:(k + 1) * 128],
                                    transpose=True).then_inc(ldT.h, 16)
                ldT.n += 16 * NKB

                @blk.gpsimd
                def _(e):
                    for i, cb in enumerate(CB_ORDER):
                        cw = 64 if cb == 130 else 128
                        if i >= 2:
                            e.wait_ge(tdA.h, b_td + i - 1)
                        e.dma_start(
                            wb[i % 2][:, 0:NKB * cw].rearrange(
                                "p (k c) -> p k c", k=NKB),
                            win_h[:, cb * 128:cb * 128 + cw].rearrange(
                                "(k p) c -> p k c", p=128),
                        ).then_inc(ldW.h, 16)
                ldW.n += 16 * len(CB_ORDER)

                @blk.tensor
                def _(e):
                    e.wait_ge(ldT.h, b_ldT + 16 * NKB)
                    for i, cb in enumerate(CB_ORDER):
                        cw = 64 if cb == 130 else 128
                        e.wait_ge(ldW.h, b_ldW + 16 * (i + 1))
                        if i >= 2:
                            e.wait_ge(evA.h, b_ev + i - 1)
                        ps = pa_ps[i % 2]
                        mi = None
                        for k in range(NKB):
                            for n in range(2):
                                mi = e.matmul(
                                    ps[0:cw, n * 512:(n + 1) * 512],
                                    wb[i % 2][:, k * cw:(k + 1) * cw],
                                    xn_fm[:, k * HB + n * 512:
                                          k * HB + (n + 1) * 512],
                                    start=(k == 0), stop=(k == NKB - 1),
                                )
                        mi.then_inc(tdA.h, 1)
                tdA.n += len(CB_ORDER)

                @blk.vector
                def _(e):
                    nxv = 0
                    for i, cb in enumerate(CB_ORDER):
                        ps = pa_ps[i % 2]
                        if 64 <= cb <= 129:
                            blkx = cb - 64
                            e.wait_ge(tdA.h, b_td + i + 1)
                            if hf == 1:
                                e.tensor_copy(conv_in[:, 0:3],
                                              bsave[:, blkx * 3:(blkx + 1) * 3])
                            e.tensor_copy(conv_in[:, 3:3 + HB],
                                          ps[:, 0:HB]).then_inc(evA.h, 1)
                            if debug and blkx < 2:
                                e.tensor_copy(rawdbg_t[blkx][:, :],
                                              conv_in[:, 3:3 + HB])
                            wc = [convw_s[:, blkx * 4 + k:blkx * 4 + k + 1]
                                  for k in range(4)]
                            e.tensor_scalar(scr1[:, :], conv_in[:, 0:HB], wc[0],
                                            None, OP.mult)
                            if nxv >= 2:
                                e.wait_ge(siA.h, b_si + nxv - 1)
                            e.scalar_tensor_tensor(scr2[nxv % 2][:, :],
                                                   conv_in[:, 1:1 + HB],
                                                   wc[1], scr1[:, :], OP.mult,
                                                   OP.add)
                            e.scalar_tensor_tensor(scr1[:, :], conv_in[:, 2:2 + HB],
                                                   wc[2], scr2[nxv % 2][:, :],
                                                   OP.mult, OP.add)
                            if hf == 0:
                                e.tensor_copy(bsave[:, blkx * 3:(blkx + 1) * 3],
                                              conv_in[:, HB:HB + 3])
                            e.scalar_tensor_tensor(scr2[nxv % 2][:, :],
                                                   conv_in[:, 3:3 + HB],
                                                   wc[3], scr1[:, :], OP.mult,
                                                   OP.add).then_inc(cvA.h, 1)
                            nxv += 1
                        elif cb == 130:
                            e.wait_ge(tdA.h, b_td + i + 1)
                            e.tensor_copy(dt_fm[0:64, t0:t0 + HB],
                                          ps[0:64, 0:HB]).then_inc(evA.h, 1)
                cvA.n += NXB

                @blk.scalar
                def _(e):
                    for _w in range(6):
                        e.activation(warm_o[:, :], warm_i[:, :], AF.Silu)
                    nxs = 0
                    nzs = 0
                    for i, cb in enumerate(CB_ORDER):
                        ps = pa_ps[i % 2]
                        if 64 <= cb <= 129:
                            e.wait_ge(cvA.h, b_cv + nxs + 1)
                            if nxs >= 2:
                                e.wait_ge(spX.h, b_spx + 16 * (nxs - 1))
                            e.activation(xs_t[nxs % 2][:, :], scr2[nxs % 2][:, :],
                                         AF.Silu).then_inc(siA.h, 1)
                            e.dma_start(
                                xs_sp[(cb - 64) * 128:(cb - 63) * 128, t0:t0 + HB],
                                xs_t[nxs % 2][:, :]).then_inc(spX.h, 16)
                            nxs += 1
                        elif cb <= 63:
                            e.wait_ge(tdA.h, b_td + i + 1)
                            if nzs >= 2:
                                e.wait_ge(spZ.h, b_spz + 16 * (nzs - 1))
                            e.activation(zs_t[nzs % 2][:, :], ps[:, 0:HB],
                                         AF.Silu).then_inc(evA.h, 1)
                            e.dma_start(
                                zs_sp[cb * 128:(cb + 1) * 128, t0:t0 + HB],
                                zs_t[nzs % 2][:, :]).then_inc(spZ.h, 16)
                            nzs += 1
                    e.wait_ge(spX.h, b_spx + 16 * NXB)
                    e.wait_ge(spZ.h, b_spz + 16 * NZB)
                    if debug:
                        e.dma_start(rawb[0:128, t0:t0 + HB],
                                    rawdbg_t[0][:, :]).then_inc(spZ.h, 16)
                        e.dma_start(rawb[128:256, t0:t0 + HB],
                                    rawdbg_t[1][:, :]).then_inc(spZ.h, 16)
                        e.wait_ge(spZ.h, b_spz + 16 * NZB + 32)
                siA.n += NXB
                spX.n += 16 * NXB
                spZ.n += 16 * NZB + (32 if debug else 0)
                evA.n += NXB + 1 + NZB

            nc.all_engine_barrier()

    es34 = contextlib.ExitStack()

    def sb34(name, shape, dt):
        return es34.enter_context(nc.sbuf_tensor(name, shape, dt))
    lndt_b = sb34("lndt_b", [NH, S], BF16)
    nA = sb34("nA", [NH, S], BF16)
    egi = sb34("egi", [NH, S], BF16)
    wst = sb34("wst", [NH, S], BF16)
    strict_up = sb34("strict_up", [128, 128], BF16)
    negtri = sb34("negtri", [128, 128], BF16)
    tri01 = sb34("tri01", [128, 128], F32)
    sel64 = sb34("sel64", [64, DSSM], BF16)
    d_i = sb34("d_i", [128, DSSM], BF16)

    v3, s3, g3_, t3_ = sem("v3"), sem("s3"), sem("g3"), sem("t3")
    with contextlib.ExitStack() as ph:
        def psb(name, shape, dt):
            return ph.enter_context(nc.sbuf_tensor(name, shape, dt))
        tmp64 = psb("tmp64", [NH, S], F32)
        ta = psb("ta", [NH, S], F32)
        pa_f = psb("pa_f", [NH, S], F32)
        d_tile = psb("d_tile", [128, NH], F32)
        ps3 = ph.enter_context(nc.psum_tensor("ps3", [128, 512], F32))

        with nc.Block() as blk:
            @blk.gpsimd
            def _(e):
                e.memset(strict_up[:, :], 1.0)
                e.affine_select(strict_up[:, :], strict_up[:, :], [[-1, 128]],
                                OP.is_gt, 0.0, base=0, channel_multiplier=1)
                e.memset(negtri[:, :], 0.0)
                e.affine_select(negtri[:, :], negtri[:, :], [[-1, 128]],
                                OP.is_gt, -1.0, base=0, channel_multiplier=1)
                e.memset(tri01[:, :], 0.0)
                e.affine_select(tri01[:, :], tri01[:, :], [[-1, 128]],
                                OP.is_gt, 1.0, base=0, channel_multiplier=1)
                e.memset(sel64[:, :], 1.0)
                e.affine_select(sel64[:, :], sel64[:, :], [[-1, 64], [0, 128]],
                                OP.is_equal, 0.0, base=0,
                                channel_multiplier=1).then_inc(g3_.h, 1)
            g3_.n += 1

            @blk.vector
            def _(e):
                e.tensor_scalar(tmp64[:, :], dt_fm[:, :], dtb_c[0:64, 0:1], None,
                                OP.add).then_inc(v3.h, 1)
            v3.n += 1

            @blk.scalar
            def _(e):
                for _w in range(6):
                    e.activation(warm_o[:, :], warm_i[:, :], AF.Exp)
                e.wait_ge(v3.h, v3.n)
                e.activation(ta[:, :], tmp64[:, :], AF.Abs)
                e.activation(ta[:, :], ta[:, :], AF.Exp, scale=-1.0)
                e.activation(ta[:, :], ta[:, :], AF.Ln, bias=1.0)
                e.activation(pa_f[:, :], tmp64[:, :], AF.Relu).then_inc(s3.h, 1)
            s3.n += 1

            @blk.vector
            def _(e):
                e.wait_ge(s3.h, s3.n)
                e.tensor_tensor(tmp64[:, :], pa_f[:, :], ta[:, :],
                                OP.add).then_inc(v3.h, 1)
            v3.n += 1

            @blk.scalar
            def _(e):
                e.wait_ge(v3.h, v3.n)
                e.activation(lndt_b[:, :], tmp64[:, :], AF.Ln).then_inc(s3.h, 1)
            s3.n += 1

            @blk.vector
            def _(e):
                e.wait_ge(s3.h, s3.n)
                e.tensor_scalar(nA[:, :], tmp64[:, :], a_c[0:64, 0:1], None,
                                OP.mult)
                i = None
                for c in range(NCH):
                    sl = slice(c * Q, (c + 1) * Q)
                    i = e.tensor_tensor_scan(pa_f[:, sl], ones64[:, :], nA[:, sl],
                                             0.0, OP.mult, OP.add)
                i.then_inc(v3.h, 1)
            v3.n += 1

            @blk.scalar
            def _(e):
                e.wait_ge(v3.h, v3.n)
                e.activation(egi[:, :], pa_f[:, :], AF.Exp,
                             scale=-1.0).then_inc(s3.h, 1)
            s3.n += 1

            @blk.vector
            def _(e):
                e.wait_ge(s3.h, s3.n)
                i = None
                for c in range(NCH):
                    sl = slice(c * Q, (c + 1) * Q)
                    i = e.tensor_scalar(ta[:, sl], pa_f[:, sl],
                                        pa_f[:, c * Q + Q - 1:c * Q + Q], None,
                                        OP.subtract)
                i.then_inc(v3.h, 1)
            v3.n += 1

            @blk.scalar
            def _(e):
                e.wait_ge(v3.h, v3.n)
                e.activation(pa_f[:, :], ta[:, :], AF.Exp).then_inc(s3.h, 1)
            s3.n += 1

            @blk.vector
            def _(e):
                e.wait_ge(s3.h, s3.n)
                e.tensor_tensor(wst[:, :], pa_f[:, :], tmp64[:, :],
                                OP.mult).then_inc(v3.h, 1)
            v3.n += 1

            @blk.tensor
            def _(e):
                e.matmul(ps3[0:128, 0:64], ones_row[0:1, 0:128], dpar_r[0:1, 0:64],
                         start=True, stop=True).then_inc(t3_.h, 1)
            t3_.n += 1

            @blk.vector
            def _(e):
                e.wait_ge(t3_.h, t3_.n)
                e.tensor_copy(d_tile[:, 0:64], ps3[0:128, 0:64])
                e.tensor_tensor(
                    d_i[:, :].rearrange("p (h q) -> p h q", h=64),
                    ident_b[:, :].unsqueeze(1).broadcast_to([128, 64, 128]),
                    d_tile[:, 0:64].unsqueeze(2).broadcast_to([128, 64, 128]),
                    OP.mult).then_inc(v3.h, 1)
            v3.n += 1
        nc.all_engine_barrier()

    sT, sV, sA, sV1, sP = sem("sc_t"), sem("sc_v"), sem("sc_a"), sem("sc_v1"), sem("sc_p")
    ldX, ldZ, ldB, spY = sem("sc_ldx"), sem("sc_ldz"), sem("sc_ldb"), sem("sc_spy")
    VPC, TPC, APC, PPC = 57, 71, 50, 32
    with contextlib.ExitStack() as ph:
        def psb(name, shape, dt):
            return ph.enter_context(nc.sbuf_tensor(name, shape, dt))
        S_bf = psb("S_bf", [128, DSSM], BF16)
        X_tm = [psb(f"X_tm{i}", [128, DSSM], BF16) for i in range(3)]
        Zs_tm = [psb(f"Zs_tm{i}", [128, DSSM // 2], BF16) for i in range(2)]
        yg = psb("yg", [128, DSSM], BF16)
        yrms_t = psb("yrms_t", [128, DSSM], BF16)
        B_c = [psb(f"B_c{i}", [128, 128], BF16) for i in range(2)]
        C_c = [psb(f"C_c{i}", [128, 128], BF16) for i in range(2)]
        B_T = psb("B_T", [128, 128], BF16)
        S_cbT = psb("S_cbT", [128, 128], BF16)
        nA_T = psb("nA_T", [128, 64], F32)
        egi_T = psb("egi_T", [128, 64], F32)
        w_T = psb("w_T", [128, 64], F32)
        eG_row = psb("eG_row", [1, 64], F32)
        eG_tile = psb("eG_tile", [128, 64], F32)
        rhs_g = [psb(f"rhs_g{i}", [128, GW], BF16) for i in range(4)]
        E_t = [psb(f"E_t{i}", [128, GW], BF16) for i in range(2)]
        M2t = [psb(f"M2t{i}", [128, GW], BF16) for i in range(2)]
        M2 = [psb(f"M2_{i}", [128, GW], BF16) for i in range(2)]
        xw = [psb(f"xw{i}", [128, GW], BF16) for i in range(2)]
        tbuf = [psb(f"tb{i}", [128, GW], BF16) for i in range(2)]
        t2b = [psb(f"t2b{i}", [128, GW], BF16) for i in range(2)]
        sq_scr = psb("sq_scr", [128, GW], F32)
        ss16 = psb("ss16", [128, 16], F32)
        ss_c = psb("ss_c", [128, 2], F32)
        PS_D = [ph.enter_context(nc.psum_tensor(f"psD{i}", [128, GW], F32))
                for i in range(2)]
        PS_I = [ph.enter_context(nc.psum_tensor(f"psI{i}", [128, GW], F32))
                for i in range(2)]
        PS_Y = [ph.enter_context(nc.psum_tensor(f"psY{i}", [128, GW], F32))
                for i in range(2)]
        PS_S = [ph.enter_context(nc.psum_tensor(f"psS{i}", [128, GW], F32))
                for i in range(2)]

        with nc.Block() as blk:
            @blk.gpsimd
            def _(e):
                e.memset(S_bf[:, :], 0.0)
        nc.all_engine_barrier()

        VORD = [("pre", k) for k in range(1, 8)]
        for _i in range(NGRP + 2):
            if _i < NGRP:
                VORD.append(("V5", _i))
            if 1 <= _i <= NGRP:
                VORD.append(("V3", _i - 1))
                VORD.append(("V4", _i - 1))
            if 2 <= _i <= NGRP + 1:
                VORD.append(("V7", _i - 2))
        VORD += [("ss", 0), ("yr", 0)]
        VPOS = {k: i + 1 for i, k in enumerate(VORD)}
        VPC = len(VORD)
        AORD = []
        for _i in range(NGRP + 2):
            if _i < NGRP:
                AORD.append(("A1", _i))
            if 1 <= _i <= NGRP:
                AORD.append(("A4", _i - 1))
            if 2 <= _i <= NGRP + 1:
                AORD.append(("A2", _i - 2))
        AORD += [("ln", 0), ("ex", 0)]
        APOS = {k: i + 1 for i, k in enumerate(AORD)}
        APC = len(AORD)
        TPC = 71
        PPC = NGRP

        def _norm(c, g):
            while g < 0:
                c -= 1
                g += NGRP
            return (c, g) if c >= 0 else None

        def tT(c, g, j):
            return c * TPC + 7 + 4 * g + j

        def tPre(c, k):
            return c * TPC + k

        def vI(c, kind, g=0):
            return c * VPC + VPOS[(kind, g)]

        def aI(c, kind, g=0):
            return c * APC + APOS[(kind, g)]

        def pI(c, g):
            return c * PPC + g + 1

        def v1i(c, g):
            return c * NGRP + g + 1

        with nc.Block() as blk:
            @blk.sync
            def _(e):
                e.dma_start(X_tm[0][:, :], xs_sp[0:DSSM, 0:128],
                            transpose=True).then_inc(ldX.h, 16)
                e.dma_start(X_tm[1][:, :], xs_sp[0:DSSM, 128:256],
                            transpose=True).then_inc(ldX.h, 16)
                e.dma_start(B_c[0][:, :],
                            xs_sp[DSSM:DSSM + 128, 0:128]).then_inc(ldB.h, 16)
                e.dma_start(C_c[0][:, :],
                            xs_sp[DSSM + 128:DSSM + 256, 0:128]).then_inc(ldB.h, 16)
                for c in range(NCH):
                    sl = slice(c * Q, (c + 1) * Q)
                    if c >= 1:
                        e.wait_ge(sV.h, vI(c - 1, "yr"))
                        e.dma_start(yr_sp[(c - 1) * Q:c * Q, :],
                                    yrms_t[:, :]).then_inc(spY.h, 16)
                        e.wait_ge(sV.h, vI(c - 1, "V7", 7))
                    e.dma_start(Zs_tm[0][:, :], zs_sp[0:DSSM // 2, sl],
                                transpose=True).then_inc(ldZ.h, 16)
                    if c >= 1:
                        e.wait_ge(sV.h, vI(c - 1, "V7", 15))
                    e.dma_start(Zs_tm[1][:, :], zs_sp[DSSM // 2:DSSM, sl],
                                transpose=True).then_inc(ldZ.h, 16)
                    if c + 2 < NCH:
                        sl2 = slice((c + 2) * Q, (c + 3) * Q)
                        if c >= 1:
                            e.wait_ge(sT.h, tT(c - 1, 15, 4))
                            e.wait_ge(sV.h, vI(c - 1, "V5", 15))
                        e.dma_start(X_tm[(c + 2) % 3][:, :], xs_sp[0:DSSM, sl2],
                                    transpose=True).then_inc(ldX.h, 16)
                    if c + 1 < NCH:
                        sl1 = slice((c + 1) * Q, (c + 2) * Q)
                        if c >= 1:
                            e.wait_ge(sT.h, tT(c - 1, 15, 4))
                        e.dma_start(B_c[(c + 1) % 2][:, :],
                                    xs_sp[DSSM:DSSM + 128, sl1]).then_inc(ldB.h, 16)
                        e.dma_start(C_c[(c + 1) % 2][:, :],
                                    xs_sp[DSSM + 128:DSSM + 256, sl1]).then_inc(
                            ldB.h, 16)
                e.wait_ge(sV.h, vI(NCH - 1, "yr"))
                e.dma_start(yr_sp[(NCH - 1) * Q:NCH * Q, :],
                            yrms_t[:, :]).then_inc(spY.h, 16)
                e.wait_ge(spY.h, 16 * NCH)
            ldX.n += 16 * NCH
            ldZ.n += 32 * NCH
            ldB.n += 32 * NCH
            spY.n += 16 * NCH

            @blk.tensor
            def _(e):
                for c in range(NCH):
                    sl = slice(c * Q, (c + 1) * Q)
                    if c >= 1:
                        e.wait_ge(sA.h, aI(c - 1, "A1", 14))
                        e.wait_ge(sV.h, vI(c - 1, "V4", 14))
                    e.matmul(PS_D[0][0:128, 0:64], nA[:, sl], ident_b[0:64, 0:64],
                             start=True, stop=True).then_inc(sT.h, 1)
                    e.matmul(PS_D[0][0:128, 64:128], egi[:, sl],
                             ident_b[0:64, 0:64], start=True,
                             stop=True).then_inc(sT.h, 1)
                    e.matmul(PS_D[0][0:128, 128:192], wst[:, sl],
                             ident_b[0:64, 0:64], start=True,
                             stop=True).then_inc(sT.h, 1)
                    e.matmul(PS_D[0][0:1, 192:256],
                             egi[:, c * Q + Q - 1:c * Q + Q],
                             ident_b[0:64, 0:64], start=True,
                             stop=True).then_inc(sT.h, 1)
                    e.wait_ge(sV.h, c * VPC + 4)
                    e.matmul(PS_D[0][0:128, 256:320], ones_row[0:1, 0:128],
                             eG_row[0:1, 0:64], start=True,
                             stop=True).then_inc(sT.h, 1)
                    e.wait_ge(ldB.h, 32 * (c + 1))
                    e.matmul(PS_I[0][0:128, 0:128], B_c[c % 2][:, :],
                             C_c[c % 2][:, :], start=True,
                             stop=True).then_inc(sT.h, 1)
                    e.matmul(PS_I[0][0:128, 128:256], B_c[c % 2][:, :],
                             ident_b[:, :], start=True, stop=True).then_inc(sT.h, 1)
                    for g in range(NGRP):
                        p_ = g % 2
                        gsl = slice(g * GW, (g + 1) * GW)
                        pv = _norm(c, g - 2)
                        e.wait_ge(sV1.h, v1i(c, g))
                        if pv:
                            e.wait_ge(sA.h, aI(pv[0], "A1", pv[1]))
                        e.matmul(PS_D[p_][:, :], strict_up[:, :],
                                 rhs_g[g % 4][:, :], start=True, stop=False)
                        e.matmul(PS_D[p_][:, :], lndt_b[:, sl],
                                 sel64[:, g * GW:(g + 1) * GW],
                                 start=False, stop=True).then_inc(sT.h, 1)
                        if pv:
                            e.wait_ge(sV.h, vI(pv[0], "V4", pv[1]))
                        if c >= 1:
                            e.wait_ge(sA.h, aI(c - 1, "A4", g))
                        e.matmul(PS_I[p_][:, :], C_c[c % 2][:, :], S_bf[:, gsl],
                                 start=True, stop=True).then_inc(sT.h, 1)
                        e.wait_ge(sV.h, vI(c, "V4", g))
                        e.matmul(PS_Y[p_][:, :], ident_b[:, :], tbuf[p_][:, :],
                                 start=True, stop=False, skip_group_check=True)
                        mi = None
                        for hh in range(4):
                            mi = e.matmul(PS_Y[p_][:, hh * 128:(hh + 1) * 128],
                                          M2[p_][:, hh * 128:(hh + 1) * 128],
                                          X_tm[c % 3][:, (4 * g + hh) * 128:
                                                      (4 * g + hh + 1) * 128],
                                          start=False, stop=(hh == 3),
                                          skip_group_check=True)
                        mi.then_inc(sT.h, 1)
                        if pv:
                            e.wait_ge(sA.h, aI(pv[0], "A4", pv[1]))
                        e.wait_ge(sV.h, vI(c, "V5", g))
                        e.wait_ge(sP.h, pI(c, g))
                        e.matmul(PS_S[p_][:, :], B_T[:, :], xw[p_][:, :],
                                 start=True, stop=False)
                        e.matmul(PS_S[p_][:, :], ident_b[:, :], t2b[p_][:, :],
                                 start=False, stop=True,
                                 skip_group_check=True).then_inc(sT.h, 1)
            sT.n += TPC * NCH

            @blk.vector
            def _(e):
                def emit_v1(c, g):
                    pv4 = _norm(c, g - 4)
                    if pv4:
                        e.wait_ge(sT.h, tT(pv4[0], pv4[1], 1))
                    hsl = slice(4 * g, 4 * g + 4)
                    e.tensor_tensor(
                        rhs_g[g % 4][:, :].rearrange("p (h q) -> p h q", h=4),
                        negtri[:, :].unsqueeze(1).broadcast_to([128, 4, 128]),
                        nA_T[:, hsl].unsqueeze(2).broadcast_to([128, 4, 128]),
                        OP.mult).then_inc(sV1.h, 1)

                for c in range(NCH):
                    e.wait_ge(sT.h, tPre(c, 4))
                    e.tensor_copy(nA_T[:, :], PS_D[0][0:128, 0:64]).then_inc(sV.h, 1)
                    e.tensor_copy(egi_T[:, :],
                                  PS_D[0][0:128, 64:128]).then_inc(sV.h, 1)
                    e.tensor_copy(w_T[:, :],
                                  PS_D[0][0:128, 128:192]).then_inc(sV.h, 1)
                    e.tensor_copy(eG_row[0:1, :],
                                  PS_D[0][0:1, 192:256]).then_inc(sV.h, 1)
                    e.wait_ge(sT.h, tPre(c, 5))
                    e.tensor_copy(eG_tile[:, :],
                                  PS_D[0][0:128, 256:320]).then_inc(sV.h, 1)
                    e.wait_ge(sT.h, tPre(c, 6))
                    e.tensor_tensor(S_cbT[:, :], PS_I[0][0:128, 0:128],
                                    tri01[:, :], OP.mult).then_inc(sV.h, 1)
                    e.wait_ge(sT.h, tPre(c, 7))
                    e.tensor_copy(B_T[:, :],
                                  PS_I[0][0:128, 128:256]).then_inc(sV.h, 1)
                    emit_v1(c, 0)
                    emit_v1(c, 1)
                    for i in range(NGRP + 2):
                        if i < NGRP:
                            g = i
                            p_ = g % 2
                            gsl = slice(g * GW, (g + 1) * GW)
                            hsl = slice(4 * g, 4 * g + 4)
                            pv = _norm(c, g - 2)
                            e.wait_ge(ldX.h, 16 * (c + 1))
                            if pv:
                                e.wait_ge(sT.h, tT(pv[0], pv[1], 4))
                            e.tensor_tensor(
                                xw[p_][:, :].rearrange("p (h q) -> p h q", h=4),
                                X_tm[c % 3][:, gsl].rearrange(
                                    "p (h q) -> p h q", h=4),
                                w_T[:, hsl].unsqueeze(2).broadcast_to(
                                    [128, 4, 128]),
                                OP.mult).then_inc(sV.h, 1)
                        if 1 <= i <= NGRP:
                            g = i - 1
                            p_ = g % 2
                            gsl = slice(g * GW, (g + 1) * GW)
                            hsl = slice(4 * g, 4 * g + 4)
                            pv = _norm(c, g - 2)
                            e.wait_ge(sA.h, aI(c, "A1", g))
                            if pv:
                                e.wait_ge(sT.h, tT(pv[0], pv[1], 3))
                            e.tensor_tensor(
                                M2t[p_][:, :].rearrange("p (h q) -> p h q", h=4),
                                E_t[p_][:, :].rearrange("p (h q) -> p h q", h=4),
                                S_cbT[:, :].unsqueeze(1).broadcast_to(
                                    [128, 4, 128]),
                                OP.mult)
                            e.tensor_tensor(M2[p_][:, :], M2t[p_][:, :],
                                            d_i[:, gsl], OP.add).then_inc(sV.h, 1)
                            e.wait_ge(sT.h, tT(c, g, 2))
                            e.tensor_tensor(
                                tbuf[p_][:, :].rearrange("p (h q) -> p h q", h=4),
                                PS_I[p_][:, :].rearrange("p (h q) -> p h q", h=4),
                                egi_T[:, hsl].unsqueeze(2).broadcast_to(
                                    [128, 4, 128]),
                                OP.mult).then_inc(sV.h, 1)
                        if 2 <= i <= NGRP + 1:
                            g = i - 2
                            p_ = g % 2
                            gsl = slice(g * GW, (g + 1) * GW)
                            e.wait_ge(sT.h, tT(c, g, 3))
                            e.wait_ge(ldZ.h, 32 * c + (16 if g < 8 else 32))
                            e.tensor_tensor(yg[:, gsl], PS_Y[p_][:, :],
                                            Zs_tm[g // 8][:, (g % 8) * GW:
                                                          (g % 8 + 1) * GW],
                                            OP.mult).then_inc(sV.h, 1)
                        if i + 2 < NGRP:
                            emit_v1(c, i + 2)
                    e.wait_ge(sA.h, aI(c, "A2", 15))
                    e.reduce_sum(ss_c[:, 0:1], ss16[:, :],
                                 axis=AX.X).then_inc(sV.h, 1)
                    e.wait_ge(sA.h, aI(c, "ex"))
                    if c >= 1:
                        e.wait_ge(spY.h, 16 * c)
                    e.tensor_scalar(yrms_t[:, :], yg[:, :], ss_c[:, 1:2], None,
                                    OP.mult).then_inc(sV.h, 1)
            sV.n += VPC * NCH
            sV1.n += NGRP * NCH

            @blk.gpsimd
            def _(e):
                for c in range(NCH):
                    for g in range(NGRP):
                        p_ = g % 2
                        gsl = slice(g * GW, (g + 1) * GW)
                        hsl = slice(4 * g, 4 * g + 4)
                        pv = _norm(c, g - 2)
                        if c >= 1:
                            e.wait_ge(sA.h, aI(c - 1, "A4", g))
                        if pv:
                            e.wait_ge(sT.h, tT(pv[0], pv[1], 4))
                        e.wait_ge(sV.h, c * VPC + 5)
                        e.tensor_tensor(
                            t2b[p_][:, :].rearrange("p (h q) -> p h q", h=4),
                            S_bf[:, gsl].rearrange("p (h q) -> p h q", h=4),
                            eG_tile[:, hsl].unsqueeze(2).broadcast_to(
                                [128, 4, 128]),
                            OP.mult).then_inc(sP.h, 1)
            sP.n += PPC * NCH

            @blk.scalar
            def _(e):
                for c in range(NCH):
                    for i in range(NGRP + 2):
                        if i < NGRP:
                            g = i
                            p_ = g % 2
                            pv = _norm(c, g - 2)
                            e.wait_ge(sT.h, tT(c, g, 1))
                            if pv:
                                e.wait_ge(sV.h, vI(pv[0], "V3", pv[1]))
                            e.activation(E_t[p_][:, :], PS_D[p_][:, :],
                                         AF.Exp).then_inc(sA.h, 1)
                        if 1 <= i <= NGRP:
                            g = i - 1
                            p_ = g % 2
                            gsl = slice(g * GW, (g + 1) * GW)
                            e.wait_ge(sT.h, tT(c, g, 4))
                            e.activation(S_bf[:, gsl], PS_S[p_][:, :],
                                         AF.Copy).then_inc(sA.h, 1)
                        if 2 <= i <= NGRP + 1:
                            g = i - 2
                            gsl = slice(g * GW, (g + 1) * GW)
                            e.wait_ge(sV.h, vI(c, "V7", g))
                            if g == 0 and c >= 1:
                                e.wait_ge(sV.h, vI(c - 1, "ss"))
                            e.activation(sq_scr[:, :], yg[:, gsl], AF.Square,
                                         accum_out=ss16[:, g:g + 1]).then_inc(
                                sA.h, 1)
                    e.wait_ge(sV.h, vI(c, "ss"))
                    e.activation(ss_c[:, 1:2], ss_c[:, 0:1], AF.Ln,
                                 scale=1.0 / DSSM,
                                 bias=eps_c[:, 0:1]).then_inc(sA.h, 1)
                    e.activation(ss_c[:, 1:2], ss_c[:, 1:2], AF.Exp,
                                 scale=-0.5).then_inc(sA.h, 1)
            sA.n += APC * NCH
        nc.all_engine_barrier()
    es34.close()

    ldYT, ldWo = sem("c_ldyt"), sem("c_ldwo")
    tdC, evC, spC, vC = sem("c_td"), sem("c_ev"), sem("c_sp"), sem("c_v")
    with contextlib.ExitStack() as ph:
        def psb(name, shape, dt):
            return ph.enter_context(nc.sbuf_tensor(name, shape, dt))
        yk = psb("yk", [128, NZB * HB], BF16)
        ystg = [psb(f"ystg{i}", [128, HB], BF16) for i in range(2)]
        wo = [psb(f"wo{i}", [128, NZB * 128], BF16) for i in range(2)]
        at_t = [psb(f"at_t{i}", [128, HB], BF16) for i in range(2)]
        psC = [ph.enter_context(nc.psum_tensor(f"psC{i}", [128, HB], F32))
               for i in range(2)]

        for hf in range(2):
            t0 = hf * HB
            b_ldyt, b_ldwo, b_td, b_ev, b_sp, b_v = (ldYT.n, ldWo.n, tdC.n,
                                                     evC.n, spC.n, vC.n)
            with nc.Block() as blk:
                @blk.sync
                def _(e):
                    for k in range(NZB):
                        if k >= 2:
                            e.wait_ge(vC.h, b_v + k - 1)
                        e.dma_start(ystg[k % 2][:, :],
                                    yr_sp[t0:t0 + HB, k * 128:(k + 1) * 128],
                                    transpose=True).then_inc(ldYT.h, 16)
                ldYT.n += 16 * NZB

                @blk.vector
                def _(e):
                    for k in range(NZB):
                        e.wait_ge(ldYT.h, b_ldyt + 16 * (k + 1))
                        e.tensor_scalar(yk[:, k * HB:(k + 1) * HB],
                                        ystg[k % 2][:, :], nwt_s[:, k:k + 1],
                                        None, OP.mult).then_inc(vC.h, 1)
                vC.n += NZB

                @blk.gpsimd
                def _(e):
                    for ob in range(32):
                        if ob >= 2:
                            e.wait_ge(tdC.h, b_td + ob - 1)
                        e.dma_start(
                            wo[ob % 2][:, :].rearrange("p (k c) -> p k c", k=NZB),
                            wout_h[:, ob * 128:(ob + 1) * 128].rearrange(
                                "(k p) c -> p k c", p=128),
                        ).then_inc(ldWo.h, 16)
                ldWo.n += 16 * 32

                @blk.tensor
                def _(e):
                    for ob in range(32):
                        e.wait_ge(ldWo.h, b_ldwo + 16 * (ob + 1))
                        if ob >= 2:
                            e.wait_ge(evC.h, b_ev + ob - 1)
                        mi = None
                        for k in range(NZB):
                            if ob == 0:
                                e.wait_ge(vC.h, b_v + k + 1)
                            for n in range(2):
                                mi = e.matmul(
                                    psC[ob % 2][:, n * 512:(n + 1) * 512],
                                    wo[ob % 2][:, k * 128:(k + 1) * 128],
                                    yk[:, k * HB + n * 512:k * HB + (n + 1) * 512],
                                    start=(k == 0), stop=(k == NZB - 1))
                        mi.then_inc(tdC.h, 1)
                tdC.n += 32

                @blk.vector
                def _(e):
                    for ob in range(32):
                        e.wait_ge(tdC.h, b_td + ob + 1)
                        if ob >= 2:
                            e.wait_ge(spC.h, b_sp + 16 * (ob - 1))
                        e.tensor_copy(at_t[ob % 2][:, :],
                                      psC[ob % 2][:, :]).then_inc(evC.h, 1)
                vC.n += 0

                @blk.scalar
                def _(e):
                    for ob in range(32):
                        e.wait_ge(evC.h, b_ev + ob + 1)
                        e.dma_start(at_sp[ob * 128:(ob + 1) * 128, t0:t0 + HB],
                                    at_t[ob % 2][:, :]).then_inc(spC.h, 16)
                    e.wait_ge(spC.h, b_sp + 16 * 32)
                evC.n += 32
                spC.n += 16 * 32
            nc.all_engine_barrier()

    ldE, vE, sE, spE = sem("e_ld"), sem("e_v"), sem("e_s"), sem("e_sp")
    ldF, vF, sF, stO = sem("f_ld"), sem("f_v"), sem("f_s"), sem("f_out")
    ldXT, ldWf = sem("d_ldxt"), sem("d_ldwf")
    tdD, evD, spD, vD = sem("d_td"), sem("d_ev"), sem("d_sp"), sem("d_v")
    with contextlib.ExitStack() as ph:
        def psb(name, shape, dt):
            return ph.enter_context(nc.sbuf_tensor(name, shape, dt))
        tm_t = [psb(f"tm_t{i}", [128, H], BF16) for i in range(2)]
        xr_t = [psb(f"xr_t{i}", [128, H], F32) for i in range(2)]
        res_t = [psb(f"res_t{i}", [128, H], F32) for i in range(2)]
        nres_t = [psb(f"nres_t{i}", [128, H], BF16) for i in range(2)]
        st_e = psb("st_e", [128, 8], F32)
        xk2 = psb("xk2", [128, NKB * HB], BF16)
        wf = [psb(f"wf{i}", [128, NKB * 128], BF16) for i in range(2)]
        la_s = psb("la_s", [128, NKB * 48], BF16)
        lb_s = psb("lb_s", [48, H], BF16)
        bias_s = psb("bias_s", [128, 32], F32)
        u48 = psb("u48", [48, HB], BF16)
        ff_t = [psb(f"ff_t{i}", [128, HB], BF16) for i in range(2)]
        psU = ph.enter_context(nc.psum_tensor("psU", [48, HB], F32))
        psD = [ph.enter_context(nc.psum_tensor(f"psDD{i}", [128, HB], F32))
               for i in range(3)]

        def e1_sync(e, tc):
            if tc == 0:
                e.wait_ge(spC.h, 16 * 32)
            if tc == 8:
                e.wait_ge(spC.h, 16 * 64)
            if tc >= 2:
                e.wait_ge(vE.h, 4 * (tc - 2) + 4)
                e.wait_ge(sE.h, 4 * (tc - 2) + 3)
            e.dma_start(tm_t[tc % 2][:, :], at_sp[0:H, tc * 128:(tc + 1) * 128],
                        transpose=True).then_inc(ldE.h, 16)
            e.dma_start(xr_t[tc % 2][:, :],
                        x_h[tc * 128:(tc + 1) * 128, :]).then_inc(ldE.h, 16)

        def e1_scalar_p(e, tc, p):
            if p == 0:
                e.wait_ge(ldE.h, 32 * (tc + 1))
                if tc >= 2:
                    e.wait_ge(vE.h, 4 * (tc - 2) + 3)
                e.activation(res_t[tc % 2][:, :], tm_t[tc % 2][:, :],
                             AF.Square).then_inc(sE.h, 1)
            elif p == 1:
                e.wait_ge(vE.h, 4 * tc + 1)
                e.activation(st_e[:, 6:7], st_e[:, 4:5], AF.Ln, scale=-1.0,
                             bias=eps_c[:, 0:1])
                e.activation(st_e[:, 7:8], st_e[:, 6:7], AF.Exp,
                             scale=-0.5).then_inc(sE.h, 1)
            elif p == 2:
                e.wait_ge(vE.h, 4 * tc + 2)
                e.activation(xr_t[tc % 2][:, :], res_t[tc % 2][:, :],
                             AF.Square).then_inc(sE.h, 1)
            else:
                e.wait_ge(vE.h, 4 * tc + 3)
                e.activation(st_e[:, 6:7], st_e[:, 4:5], AF.Ln, scale=-1.0,
                             bias=eps_c[:, 0:1])
                e.activation(st_e[:, 7:8], st_e[:, 6:7], AF.Exp,
                             scale=-0.5).then_inc(sE.h, 1)

        def e1_scalar(e, tc):
            for p in range(4):
                e1_scalar_p(e, tc, p)

        def e1_vector_p(e, tc, p):
            if p == 0:
                e.wait_ge(ldE.h, 32 * (tc + 1))
                e.reduce_sum(st_e[:, 0:1], tm_t[tc % 2][:, :], axis=AX.X)
                e.wait_ge(sE.h, 4 * tc + 1)
                e.reduce_sum(st_e[:, 1:2], res_t[tc % 2][:, :], axis=AX.X)
                e.tensor_scalar(st_e[:, 2:3], st_e[:, 0:1], 1.0 / H, None, OP.mult)
                e.tensor_scalar(st_e[:, 3:4], st_e[:, 1:2], 1.0 / H, None, OP.mult)
                e.tensor_scalar(st_e[:, 0:1], st_e[:, 0:1], 0.0, None, OP.mult)
                e.scalar_tensor_tensor(st_e[:, 4:5], st_e[:, 2:3], st_e[:, 2:3],
                                       st_e[:, 3:4], OP.mult,
                                       OP.subtract).then_inc(vE.h, 1)
            elif p == 1:
                e.wait_ge(sE.h, 4 * tc + 2)
                if tc >= 2:
                    e.wait_ge(spE.h, 32 * (tc - 2) + 16)
                e.tensor_scalar(res_t[tc % 2][:, :], tm_t[tc % 2][:, :],
                                st_e[:, 2:3], st_e[:, 7:8], OP.subtract, OP.mult)
                e.tensor_tensor(res_t[tc % 2][:, :], res_t[tc % 2][:, :],
                                xr_t[tc % 2][:, :], OP.add).then_inc(vE.h, 1)
            elif p == 2:
                e.wait_ge(sE.h, 4 * tc + 3)
                e.reduce_sum(st_e[:, 0:1], res_t[tc % 2][:, :], axis=AX.X)
                e.reduce_sum(st_e[:, 1:2], xr_t[tc % 2][:, :], axis=AX.X)
                e.tensor_scalar(st_e[:, 2:3], st_e[:, 0:1], 1.0 / H, None, OP.mult)
                e.tensor_scalar(st_e[:, 3:4], st_e[:, 1:2], 1.0 / H, None, OP.mult)
                e.tensor_scalar(st_e[:, 0:1], st_e[:, 0:1], 0.0, None, OP.mult)
                e.scalar_tensor_tensor(st_e[:, 4:5], st_e[:, 2:3], st_e[:, 2:3],
                                       st_e[:, 3:4], OP.mult,
                                       OP.subtract).then_inc(vE.h, 1)
            else:
                e.wait_ge(sE.h, 4 * tc + 4)
                if tc >= 2:
                    e.wait_ge(spE.h, 32 * (tc - 2) + 32)
                e.tensor_scalar(nres_t[tc % 2][:, :], res_t[tc % 2][:, :],
                                st_e[:, 2:3], st_e[:, 7:8], OP.subtract,
                                OP.mult).then_inc(vE.h, 1)

        def e1_vector(e, tc):
            for p in range(4):
                e1_vector_p(e, tc, p)

        def e1_gpsimd(e, tc):
            e.wait_ge(vE.h, 4 * tc + 2)
            e.dma_start(x2_sp[tc * 128:(tc + 1) * 128, :],
                        res_t[tc % 2][:, :]).then_inc(spE.h, 16)
            e.wait_ge(vE.h, 4 * tc + 4)
            e.dma_start(xn2_sp[tc * 128:(tc + 1) * 128, :],
                        nres_t[tc % 2][:, :]).then_inc(spE.h, 16)

        def e2_sync(e, tc):
            if tc == 0:
                e.wait_ge(spD.h, 16 * 32)
            if tc == 8:
                e.wait_ge(spD.h, 16 * 64)
            if tc >= 2:
                e.wait_ge(vF.h, 3 * (tc - 2) + 3)
                e.wait_ge(sF.h, 2 * (tc - 2) + 1)
            e.dma_start(tm_t[tc % 2][:, :], ff_sp[0:H, tc * 128:(tc + 1) * 128],
                        transpose=True).then_inc(ldF.h, 16)
            e.dma_start(xr_t[tc % 2][:, :],
                        x2_sp[tc * 128:(tc + 1) * 128, :]).then_inc(ldF.h, 16)

        def e2_scalar_p(e, tc, p):
            if p == 0:
                e.wait_ge(ldF.h, 32 * (tc + 1))
                if tc >= 2:
                    e.wait_ge(vF.h, 3 * (tc - 2) + 1)
                e.activation(res_t[tc % 2][:, :], tm_t[tc % 2][:, :],
                             AF.Square).then_inc(sF.h, 1)
            elif p == 1:
                e.wait_ge(vF.h, 3 * tc + 1)
                e.activation(st_e[:, 6:7], st_e[:, 4:5], AF.Ln, scale=-1.0,
                             bias=eps_c[:, 0:1])
                e.activation(st_e[:, 7:8], st_e[:, 6:7], AF.Exp,
                             scale=-0.5).then_inc(sF.h, 1)

        def e2_scalar(e, tc):
            for p in range(2):
                e2_scalar_p(e, tc, p)

        def e2_vector_p(e, tc, p):
            if p == 0:
                e.wait_ge(ldF.h, 32 * (tc + 1))
                e.reduce_sum(st_e[:, 0:1], tm_t[tc % 2][:, :], axis=AX.X)
                e.wait_ge(sF.h, 2 * tc + 1)
                e.reduce_sum(st_e[:, 1:2], res_t[tc % 2][:, :], axis=AX.X)
                e.tensor_scalar(st_e[:, 2:3], st_e[:, 0:1], 1.0 / H, None, OP.mult)
                e.tensor_scalar(st_e[:, 3:4], st_e[:, 1:2], 1.0 / H, None, OP.mult)
                e.tensor_scalar(st_e[:, 0:1], st_e[:, 0:1], 0.0, None, OP.mult)
                e.scalar_tensor_tensor(st_e[:, 4:5], st_e[:, 2:3], st_e[:, 2:3],
                                       st_e[:, 3:4], OP.mult,
                                       OP.subtract).then_inc(vF.h, 1)
            elif p == 1:
                e.wait_ge(sF.h, 2 * tc + 2)
                if tc >= 2:
                    e.wait_ge(stO.h, 16 * (tc - 1))
                e.tensor_scalar(res_t[tc % 2][:, :], tm_t[tc % 2][:, :],
                                st_e[:, 2:3], st_e[:, 7:8], OP.subtract,
                                OP.mult).then_inc(vF.h, 1)
            else:
                e.tensor_tensor(res_t[tc % 2][:, :], res_t[tc % 2][:, :],
                                xr_t[tc % 2][:, :], OP.add).then_inc(vF.h, 1)

        def e2_vector(e, tc):
            for p in range(3):
                e2_vector_p(e, tc, p)

        def e2_gpsimd(e, tc):
            e.wait_ge(vF.h, 3 * tc + 3)
            e.dma_start(out_h[tc * 128:(tc + 1) * 128, :],
                        res_t[tc % 2][:, :]).then_inc(stO.h, 16)

        with nc.Block() as blk:
            @blk.sync
            def _(e):
                for tc in range(8):
                    e1_sync(e, tc)

            @blk.scalar
            def _(e):
                for tc in range(8):
                    e1_scalar(e, tc)

            @blk.vector
            def _(e):
                for tc in range(8):
                    e1_vector(e, tc)

            @blk.gpsimd
            def _(e):
                for tc in range(8):
                    e1_gpsimd(e, tc)
        ldE.n += 32 * 8
        sE.n += 4 * 8
        vE.n += 4 * 8
        spE.n += 32 * 8
        nc.all_engine_barrier()

        for hf in range(2):
            t0 = hf * HB
            b_xt, b_wf, b_td, b_ev, b_sp = ldXT.n, ldWf.n, tdD.n, evD.n, spD.n
            epi_emit = (lambda fns, e, j: fns(e, 8 + j)) if hf == 0 else                        (lambda fns, e, j: fns(e, j))
            e_s, e_sc, e_v, e_g = (e1_sync, e1_scalar, e1_vector, e1_gpsimd)                 if hf == 0 else (e2_sync, e2_scalar, e2_vector, e2_gpsimd)
            with nc.Block() as blk:
                @blk.sync
                def _(e):
                    if hf == 0:
                        e.wait_ge(spE.h, 32 * 8)
                    if hf == 1:
                        e.wait_ge(tdD.h, b_td)
                        e.wait_ge(spE.h, 32 * 16)
                    for k in range(NKB):
                        e.dma_start(xk2[:, k * HB:(k + 1) * HB],
                                    xn2_sp[t0:t0 + HB, k * 128:(k + 1) * 128],
                                    transpose=True).then_inc(ldXT.h, 16)
                    for j in range(8):
                        epi_emit(e_s, e, j)
                ldXT.n += 16 * NKB
                ldE.n += (32 * 8 if hf == 0 else 0)
                ldF.n += (32 * 8 if hf == 1 else 0)

                @blk.gpsimd
                def _(e):
                    if hf == 0:
                        e.dma_start(bias_s[:, :], biast_h[:, :]).then_inc(
                            ldWf.h, 16)
                        e.dma_start(la_s[:, :].rearrange("p (k r) -> p k r",
                                                         k=NKB),
                                    lat_h[:, :].rearrange("(k p) r -> p k r",
                                                          p=128)).then_inc(
                            ldWf.h, 16)
                        e.dma_start(lb_s[:, :], lbt_h[:, :]).then_inc(ldWf.h, 16)
                    for ob in range(32):
                        if ob >= 2:
                            e.wait_ge(tdD.h, b_td + 1 + ob - 1)
                        e.dma_start(
                            wf[ob % 2][:, :].rearrange("p (k c) -> p k c", k=NKB),
                            wft_h[:, ob * 128:(ob + 1) * 128].rearrange(
                                "(k p) c -> p k c", p=128),
                        ).then_inc(ldWf.h, 16)
                        if ob % 4 == 3:
                            epi_emit(e_g, e, ob // 4)
                    if hf == 1:
                        e.wait_ge(spE.h, 32 * 16)
                ldWf.n += 16 * 32 + (48 if hf == 0 else 0)
                spE.n += (32 * 8 if hf == 0 else 0)
                stO.n += (16 * 8 if hf == 1 else 0)

                @blk.tensor
                def _(e):
                    if hf == 0:
                        e.wait_ge(ldWf.h, 16 * 3)
                    mi = None
                    for k in range(NKB):
                        e.wait_ge(ldXT.h, b_xt + 16 * (k + 1))
                        for n in range(2):
                            mi = e.matmul(
                                psU[0:48, n * 512:(n + 1) * 512],
                                la_s[:, k * 48:(k + 1) * 48],
                                xk2[:, k * HB + n * 512:k * HB + (n + 1) * 512],
                                start=(k == 0), stop=(k == NKB - 1))
                    mi.then_inc(tdD.h, 1)
                    for ob in range(32):
                        e.wait_ge(ldWf.h, b_wf + (48 if hf == 0 else 0)
                                  + 16 * (ob + 1))
                        if ob == 0:
                            e.wait_ge(vD.h, hf + 1)
                        if ob >= 3:
                            e.wait_ge(evD.h, b_ev + ob - 2)
                        for k in range(NKB):
                            for n in range(2):
                                e.matmul(psD[ob % 3][:, n * 512:(n + 1) * 512],
                                         wf[ob % 2][:, k * 128:(k + 1) * 128],
                                         xk2[:, k * HB + n * 512:
                                             k * HB + (n + 1) * 512],
                                         start=(k == 0), stop=False)
                        mi = None
                        for n in range(2):
                            mi = e.matmul(psD[ob % 3][:, n * 512:(n + 1) * 512],
                                          lb_s[0:48, ob * 128:(ob + 1) * 128],
                                          u48[0:48, n * 512:(n + 1) * 512],
                                          start=False, stop=True)
                        mi.then_inc(tdD.h, 1)
                tdD.n += 33

                @blk.vector
                def _(e):
                    if hf == 0:
                        e.wait_ge(ldWf.h, 16 * 3)
                    e.wait_ge(tdD.h, b_td + 1)
                    e.tensor_scalar(u48[0:48, :], psU[0:48, :], m48[0:48, 0:1],
                                    None, OP.mult).then_inc(vD.h, 1)
                    for ob in range(32):
                        e.wait_ge(tdD.h, b_td + 1 + ob + 1)
                        if ob >= 2:
                            e.wait_ge(spD.h, b_sp + 16 * (ob - 1))
                        e.tensor_scalar(ff_t[ob % 2][:, :], psD[ob % 3][:, :],
                                        bias_s[:, ob:ob + 1], None,
                                        OP.add).then_inc(evD.h, 1)
                        if hf == 0:
                            e1_vector_p(e, 8 + ob // 4, ob % 4)
                        elif ob % 4 < 3:
                            e2_vector_p(e, ob // 4, ob % 4)
                vD.n += 1
                evD.n += 32
                vE.n += (4 * 8 if hf == 0 else 0)
                vF.n += (3 * 8 if hf == 1 else 0)

                @blk.scalar
                def _(e):
                    for ob in range(32):
                        e.wait_ge(evD.h, b_ev + ob + 1)
                        e.dma_start(ff_sp[ob * 128:(ob + 1) * 128, t0:t0 + HB],
                                    ff_t[ob % 2][:, :]).then_inc(spD.h, 16)
                        if hf == 0:
                            e1_scalar_p(e, 8 + ob // 4, ob % 4)
                        elif ob % 4 < 2:
                            e2_scalar_p(e, ob // 4, ob % 4)
                    e.wait_ge(spD.h, b_sp + 16 * 32)
                spD.n += 16 * 32
                sE.n += (4 * 8 if hf == 0 else 0)
                sF.n += (2 * 8 if hf == 1 else 0)
            nc.all_engine_barrier()

        with nc.Block() as blk:
            @blk.sync
            def _(e):
                for tc in range(8, 16):
                    e2_sync(e, tc)

            @blk.scalar
            def _(e):
                for tc in range(8, 16):
                    e2_scalar(e, tc)
                e.wait_ge(stO.h, 16 * NCH)

            @blk.vector
            def _(e):
                for tc in range(8, 16):
                    e2_vector(e, tc)
                e.wait_ge(stO.h, 16 * NCH)

            @blk.gpsimd
            def _(e):
                for tc in range(8, 16):
                    e2_gpsimd(e, tc)
                e.wait_ge(stO.h, 16 * NCH)
        ldF.n += 32 * 8
        sF.n += 2 * 8
        vF.n += 3 * 8
        stO.n += 16 * 8

    es.close()
    return nc


def make_in_maps(ins, cores=range(8)):
    x = np.asarray(ins["L0_attn_hidden_states"])
    W_in = np.ascontiguousarray(np.asarray(ins["L0_attn_W_in"], dtype=np.float32))
    conv_w = np.asarray(ins["L0_attn_conv_w"], dtype=np.float32)
    convw_t = np.ascontiguousarray(conv_w.reshape(NXB, 128, KCONV)
                                   .transpose(1, 0, 2).reshape(128, NXB * KCONV))
    dtb = np.ascontiguousarray(np.asarray(ins["L0_attn_dt_bias"],
                                          dtype=np.float32).reshape(1, NH))
    alog = np.ascontiguousarray(np.asarray(ins["L0_attn_A_log"],
                                           dtype=np.float32).reshape(1, NH))
    dpar = np.ascontiguousarray(np.asarray(ins["L0_attn_D_param"],
                                           dtype=np.float32).reshape(1, NH))
    nwt = np.ascontiguousarray(np.asarray(ins["L0_attn_norm_weight"],
                                          dtype=np.float32).reshape(NZB, 128).T)
    w_out = np.ascontiguousarray(np.asarray(ins["L0_attn_W_out"],
                                            dtype=np.float32))
    wf_t = np.ascontiguousarray(np.asarray(ins["L0_ffn_weight"],
                                           dtype=np.float32).T)
    bias_t = np.ascontiguousarray(np.asarray(ins["L0_ffn_bias"],
                                             dtype=np.float32).reshape(32, 128).T)
    la_t = np.ascontiguousarray(np.asarray(ins["L0_ffn_lora_A"], dtype=np.float32)
                                .transpose(2, 0, 1).reshape(H, 48))
    lb_t = np.ascontiguousarray(np.asarray(ins["L0_ffn_lora_B"], dtype=np.float32)
                                .transpose(0, 2, 1).reshape(48, H))
    lsi = np.ascontiguousarray(
        np.asarray(ins["L0_ffn_lora_start_indices"]).astype(np.int32).reshape(1, 4))
    maps = []
    for b in cores:
        maps.append({
            "x": np.ascontiguousarray(x[b]).astype(np.float32),
            "w_in": W_in, "conv_w_t": convw_t, "dtb": dtb, "alog": alog,
            "dpar": dpar, "nwt": nwt, "w_out": w_out, "wf_t": wf_t,
            "bias_t": bias_t, "la_t": la_t, "lb_t": lb_t, "lsi": lsi,
            "bval": np.array([[float(b)]], dtype=np.float32),
        })
    return maps


_NC_CACHE = {}


def kernel(**inputs):
    ins = inputs
    if "prog" not in _NC_CACHE:
        _NC_CACHE["prog"] = build(debug=False)
    nc = _NC_CACHE["prog"]
    maps = make_in_maps(ins, cores=range(8))
    from concourse.bass_utils import run_bass_kernel_spmd
    res = run_bass_kernel_spmd(nc, maps, core_ids=list(range(8)))
    out = np.stack([np.asarray(res.results[b]["out"]) for b in range(8)], 0)
    return out.astype(np.float32)


# version 10
# speedup vs baseline: 1.1536x; 1.0153x over previous
import numpy as np
import ml_dtypes
import contextlib
import concourse.bass as bass
import concourse.mybir as mybir

F32 = mybir.dt.float32
BF16 = mybir.dt.bfloat16
I32 = mybir.dt.int32
AF = mybir.ActivationFunctionType
OP = mybir.AluOpType
AX = mybir.AxisListType
bf16 = ml_dtypes.bfloat16

H, DSSM, CONV, PROJ = 4096, 8192, 8448, 16704
NH, Q, S = 64, 128, 2048
NCH = S // Q
KCONV = 4
EPS = 1e-5
HB = S // 2
NZB = DSSM // 128
NXB = CONV // 128
NKB = H // 128
NGRP = 16
GW = 512


def build(debug=False):
    nc = bass.Bass("TRN2", target_bir_lowering=False)
    es = contextlib.ExitStack()

    def dram(name, shape, dt, kind):
        return nc.dram_tensor(name, shape, dt, kind=kind).ap()

    def sbuf(name, shape, dt):
        return es.enter_context(nc.sbuf_tensor(name, shape, dt))

    EI, EO = "ExternalInput", "ExternalOutput"
    SCR = EO if debug else "Internal"

    x_h = dram("x", [S, H], F32, EI)
    win_h = dram("w_in", [H, PROJ], F32, EI)
    convw_h = dram("conv_w_t", [128, NXB * KCONV], F32, EI)
    dtb_h = dram("dtb", [1, NH], F32, EI)
    alog_h = dram("alog", [1, NH], F32, EI)
    dpar_h = dram("dpar", [1, NH], F32, EI)
    nwt_h = dram("nwt", [128, NZB], F32, EI)
    wout_h = dram("w_out", [DSSM, H], F32, EI)
    wft_h = dram("wf_t", [H, H], F32, EI)
    biast_h = dram("bias_t", [128, 32], F32, EI)
    lat_h = dram("la_t", [H, 48], F32, EI)
    lbt_h = dram("lb_t", [48, H], F32, EI)
    lsi_h = dram("lsi", [1, 4], I32, EI)
    bval_h = dram("bval", [1, 1], F32, EI)

    xn_sp = dram("xn_sp", [S, H], BF16, SCR)
    zs_sp = dram("zs_sp", [DSSM, S], BF16, SCR)
    xs_sp = dram("xs_sp", [CONV, S], BF16, SCR)
    yr_sp = dram("yr_sp", [S, DSSM], BF16, SCR)
    at_sp = dram("at_sp", [H, S], BF16, SCR)
    x2_sp = dram("x2_sp", [S, H], F32, SCR)
    xn2_sp = dram("xn2_sp", [S, H], BF16, SCR)
    ff_sp = dram("ff_sp", [H, S], BF16, SCR)
    rawb = dram("rawb", [256, S], F32, SCR)
    out_h = dram("out", [S, H], F32, EO)

    ident_f = sbuf("ident_f", [128, 128], F32)
    ident_b = sbuf("ident_b", [128, 128], BF16)
    ones_row = sbuf("ones_row", [1, 128], F32)
    ones64 = sbuf("ones64", [64, 128], F32)
    convw_s = sbuf("convw_s", [128, NXB * KCONV], F32)
    dtb_r = sbuf("dtb_r", [1, NH], F32)
    alog_r = sbuf("alog_r", [1, NH], F32)
    dpar_r = sbuf("dpar_r", [1, NH], F32)
    nwt_s = sbuf("nwt_s", [128, NZB], F32)
    lsi_s = sbuf("lsi_s", [1, 4], I32)
    lsi_f = sbuf("lsi_f", [1, 4], F32)
    bval_s = sbuf("bval_s", [1, 1], F32)
    dtb_c = sbuf("dtb_c", [NH, 1], F32)
    a_c = sbuf("a_c", [NH, 1], F32)
    m48 = sbuf("m48", [48, 1], F32)
    mrow = sbuf("mrow", [1, 48], F32)
    g4 = sbuf("g4", [1, 4], F32)
    g4b = sbuf("g4b", [1, 4], F32)
    sc1 = sbuf("sc1", [1, 8], F32)
    dt_fm = sbuf("dt_fm", [NH, S], F32)
    eps_c = sbuf("eps_c", [128, 1], F32)
    warm_i = sbuf("warm_i", [128, 512], F32)
    warm_o = sbuf("warm_o", [128, 512], F32)

    sems = {}

    def sem(name):
        if name not in sems:
            class _S:
                pass
            s = _S()
            s.h = es.enter_context(nc.semaphore(name))
            s.n = 0
            sems[name] = s
        return sems[name]

    with nc.Block() as blk:
        @blk.gpsimd
        def _(e):
            e.dma_reset(range(152, 256))
            e.sem_clear(range(152, 256))
    nc.all_engine_barrier()

    ld0, v0, s0, t0_ = sem("ld0"), sem("v0"), sem("s0"), sem("t0")
    with nc.Block() as blk:
        @blk.sync
        def _(e):
            for dst, src in [(convw_s, convw_h), (dtb_r, dtb_h), (alog_r, alog_h),
                             (dpar_r, dpar_h), (nwt_s, nwt_h), (lsi_s, lsi_h),
                             (bval_s, bval_h)]:
                e.dma_start(dst[:, :], src[:, :]).then_inc(ld0.h, 16)
        ld0.n += 16 * 7

        @blk.gpsimd
        def _(e):
            e.memset(ident_f[:, :], 1.0)
            e.affine_select(ident_f[:, :], ident_f[:, :], [[-1, 128]],
                            OP.is_equal, 0.0, base=0, channel_multiplier=1)
            e.memset(ident_b[:, :], 1.0)
            e.affine_select(ident_b[:, :], ident_b[:, :], [[-1, 128]],
                            OP.is_equal, 0.0, base=0, channel_multiplier=1)
            e.memset(ones_row[:, :], 1.0)
            e.memset(eps_c[:, :], EPS)
            e.memset(warm_i[:, :], 0.5)
            e.memset(ones64[:, :], 1.0).then_inc(v0.h, 1)
        v0.n += 1

        @blk.vector
        def _(e):
            e.wait_ge(ld0.h, 16 * 7)
            e.wait_ge(v0.h, 1)

            def spacer():
                e.tensor_scalar(g4[0:1, 0:4], ones64[0:1, 0:4], 0.0, None, OP.mult)
            e.tensor_copy(lsi_f[:, :], lsi_s[:, :])
            spacer()
            e.tensor_scalar(sc1[0:1, 0:4], lsi_f[:, :], -1.0, 0.5, OP.mult, OP.add)
            spacer()
            e.tensor_scalar(sc1[0:1, 4:8], sc1[0:1, 0:4], bval_s[0:1, 0:1], None,
                            OP.add).then_inc(v0.h, 1)
        v0.n += 1

        @blk.scalar
        def _(e):
            e.wait_ge(v0.h, v0.n)
            for fn in (AF.Silu, AF.Square, AF.Abs, AF.Relu, AF.Sign, AF.Exp,
                       AF.Ln, AF.Copy):
                e.activation(mrow[0:1, 0:1], eps_c[0:1, 0:1], fn)
            e.activation(g4[:, :], sc1[0:1, 4:8], AF.Sign)
            e.activation(g4b[:, :], g4[:, :], AF.Copy, scale=0.5,
                         bias=0.5).then_inc(s0.h, 1)
        s0.n += 1

        @blk.vector
        def _(e):
            e.wait_ge(s0.h, s0.n)

            def spacer():
                e.tensor_scalar(g4[0:1, 0:4], ones64[0:1, 0:4], 0.0, None, OP.mult)
            e.tensor_scalar(sc1[0:1, 0:1], g4b[0:1, 3:4], 1.0, 3.0, OP.mult, OP.add)
            e.tensor_scalar(sc1[0:1, 2:3], g4b[0:1, 1:2], -1.0, 1.0, OP.mult, OP.add)
            e.tensor_scalar(sc1[0:1, 4:5], g4b[0:1, 1:2], 2.0, None, OP.mult)
            e.tensor_scalar(sc1[0:1, 5:6], g4b[0:1, 2:3], -1.0, 1.0, OP.mult, OP.add)
            e.tensor_tensor(sc1[0:1, 1:2], sc1[0:1, 0:1], g4b[0:1, 2:3], OP.mult)
            e.tensor_tensor(sc1[0:1, 3:4], sc1[0:1, 2:3], g4b[0:1, 0:1], OP.mult)
            spacer()
            e.tensor_tensor(sc1[0:1, 4:5], sc1[0:1, 4:5], sc1[0:1, 3:4], OP.add)
            spacer()
            e.tensor_tensor(sc1[0:1, 5:6], sc1[0:1, 5:6], sc1[0:1, 4:5], OP.mult)
            spacer()
            e.tensor_tensor(sc1[0:1, 6:7], sc1[0:1, 1:2], sc1[0:1, 5:6], OP.add)
            spacer()
            i = None
            for k in range(3):
                e.tensor_scalar(sc1[0:1, 7:8], sc1[0:1, 6:7], -(k + 2.0), None,
                                OP.add)
                spacer()
                e.tensor_tensor(sc1[0:1, 7:8], sc1[0:1, 7:8], sc1[0:1, 7:8], OP.mult)
                spacer()
                e.tensor_scalar(sc1[0:1, 7:8], sc1[0:1, 7:8], -1.0, 1.0, OP.mult,
                                OP.add)
                spacer()
                e.tensor_relu(sc1[0:1, 7:8], sc1[0:1, 7:8])
                spacer()
                i = e.tensor_scalar(mrow[0:1, k * 16:(k + 1) * 16],
                                    ones_row[0:1, 0:16], sc1[0:1, 7:8], None,
                                    OP.mult)
            i.then_inc(v0.h, 1)
        v0.n += 1

    nc.all_engine_barrier()

    with contextlib.ExitStack() as ph:
        ps0 = ph.enter_context(nc.psum_tensor("ps0", [128, 512], F32))
        with nc.Block() as blk:
            @blk.tensor
            def _(e):
                e.matmul(ps0[0:64, 0:1], dtb_r[0:1, 0:64], ones_row[0:1, 0:1],
                         start=True, stop=True)
                e.matmul(ps0[0:64, 1:2], alog_r[0:1, 0:64], ones_row[0:1, 0:1],
                         start=True, stop=True)
                e.matmul(ps0[0:48, 2:3], mrow[0:1, 0:48], ones_row[0:1, 0:1],
                         start=True, stop=True).then_inc(t0_.h, 1)
            t0_.n += 1

            @blk.vector
            def _(e):
                e.wait_ge(t0_.h, t0_.n)
                e.tensor_copy(dtb_c[0:64, 0:1], ps0[0:64, 0:1])
                e.tensor_copy(m48[0:48, 0:1], ps0[0:48, 2:3]).then_inc(v0.h, 1)
            v0.n += 1

            @blk.scalar
            def _(e):
                e.wait_ge(v0.h, v0.n)
                e.activation(a_c[0:64, 0:1], ps0[0:64, 1:2],
                             AF.Exp).then_inc(s0.h, 1)
            s0.n += 1
        nc.all_engine_barrier()

    ld1, v1, s1, sp1 = sem("ld1"), sem("v1"), sem("s1"), sem("sp1")

    CB_ORDER = list(range(64, 131)) + list(range(64))
    ldT, ldW = sem("a_ldT"), sem("a_ldW")
    tdA, evA, cvA, siA = sem("a_td"), sem("a_ev"), sem("a_cv"), sem("a_si")
    spX, spZ = sem("a_spx"), sem("a_spz")
    with contextlib.ExitStack() as ph:
        def psb(name, shape, dt):
            return ph.enter_context(nc.sbuf_tensor(name, shape, dt))
        x_t = [psb(f"x_t{i}", [128, H], F32) for i in range(2)]
        sq_t = psb("sq_t", [128, H], F32)
        xn_t = [psb(f"xn_t{i}", [128, H], BF16) for i in range(2)]
        st_t = psb("st_t", [128, 8], F32)
        xn_fm = psb("xn_fm", [128, NKB * HB], BF16)
        wb = [psb(f"wb{i}", [128, NKB * 128], BF16) for i in range(2)]
        conv_in = psb("conv_in", [128, HB + 3], F32)
        scr1 = psb("scr1", [128, HB], F32)
        scr2 = [psb(f"scr2_{i}", [128, HB], F32) for i in range(2)]
        xs_t = [psb(f"xs_t{i}", [128, HB], BF16) for i in range(2)]
        zs_t = [psb(f"zs_t{i}", [128, HB], BF16) for i in range(2)]
        bsave = psb("bsave", [128, NXB * 3], F32)
        rawdbg_t = [psb(f"rawdbg{i}", [128, HB], F32) for i in range(2)] if debug else None
        pa_ps = [ph.enter_context(nc.psum_tensor(f"pa_ps{i}", [128, 1024], F32))
                 for i in range(2)]

        with nc.Block() as blk:
            @blk.vector
            def _(e):
                e.memset(conv_in[:, 0:3], 0.0)
        nc.all_engine_barrier()

        for hf in range(2):
            t0 = hf * HB
            b_ldT, b_ldW, b_td, b_ev = ldT.n, ldW.n, tdA.n, evA.n
            b_cv, b_si, b_spx, b_spz = cvA.n, siA.n, spX.n, spZ.n

            with nc.Block() as blk:
                @blk.sync
                def _(e):
                    if hf == 0:
                        for tc in range(16):
                            if tc >= 2:
                                e.wait_ge(v1.h, 2 * (tc - 2) + 2)
                                e.wait_ge(s1.h, 2 * (tc - 2) + 1)
                            e.dma_start(x_t[tc % 2][:, :],
                                        x_h[tc * 128:(tc + 1) * 128, :]
                                        ).then_inc(ld1.h, 16)
                            if tc == 9:
                                e.wait_ge(sp1.h, 16 * 8)
                                for k in range(NKB):
                                    e.dma_start(
                                        xn_fm[:, k * HB:(k + 1) * HB],
                                        xn_sp[t0:t0 + HB,
                                              k * 128:(k + 1) * 128],
                                        transpose=True).then_inc(ldT.h, 16)
                    else:
                        e.wait_ge(sp1.h, 16 * 16)
                        for k in range(NKB):
                            e.dma_start(xn_fm[:, k * HB:(k + 1) * HB],
                                        xn_sp[t0:t0 + HB, k * 128:(k + 1) * 128],
                                        transpose=True).then_inc(ldT.h, 16)

                @blk.scalar
                def _(e):
                    if hf == 0:
                        for tc in range(16):
                            e.wait_ge(ld1.h, 16 * (tc + 1))
                            if tc >= 1:
                                e.wait_ge(v1.h, 2 * (tc - 1) + 1)
                            e.activation(sq_t[:, :], x_t[tc % 2][:, :],
                                         AF.Square).then_inc(s1.h, 1)
                            e.wait_ge(v1.h, 2 * tc + 1)
                            e.activation(st_t[:, 6:7], st_t[:, 4:5], AF.Ln,
                                         scale=-1.0, bias=eps_c[:, 0:1])
                            e.activation(st_t[:, 7:8], st_t[:, 6:7], AF.Exp,
                                         scale=-0.5).then_inc(s1.h, 1)

                @blk.vector
                def _(e):
                    if hf == 0:
                        for tc in range(16):
                            e.wait_ge(ld1.h, 16 * (tc + 1))
                            e.reduce_sum(st_t[:, 0:1], x_t[tc % 2][:, :],
                                         axis=AX.X)
                            e.wait_ge(s1.h, 2 * tc + 1)
                            e.reduce_sum(st_t[:, 1:2], sq_t[:, :], axis=AX.X)
                            e.tensor_scalar(st_t[:, 2:3], st_t[:, 0:1], 1.0 / H,
                                            None, OP.mult)
                            e.tensor_scalar(st_t[:, 3:4], st_t[:, 1:2], 1.0 / H,
                                            None, OP.mult)
                            e.tensor_scalar(st_t[:, 0:1], st_t[:, 0:1], 0.0,
                                            None, OP.mult)
                            e.scalar_tensor_tensor(st_t[:, 4:5], st_t[:, 2:3],
                                                   st_t[:, 2:3], st_t[:, 3:4],
                                                   OP.mult,
                                                   OP.subtract).then_inc(v1.h, 1)
                            e.wait_ge(s1.h, 2 * tc + 2)
                            if tc >= 2:
                                e.wait_ge(sp1.h, 16 * (tc - 1))
                            e.tensor_scalar(xn_t[tc % 2][:, :], x_t[tc % 2][:, :],
                                            st_t[:, 2:3], st_t[:, 7:8],
                                            OP.subtract,
                                            OP.mult).then_inc(v1.h, 1)

                @blk.gpsimd
                def _(e):
                    if hf == 0:
                        for i, cb in enumerate(CB_ORDER[:2]):
                            e.dma_start(
                                wb[i % 2][:, 0:NKB * 128].rearrange(
                                    "p (k c) -> p k c", k=NKB),
                                win_h[:, cb * 128:(cb + 1) * 128].rearrange(
                                    "(k p) c -> p k c", p=128),
                            ).then_inc(ldW.h, 16)
                        for tc in range(16):
                            e.wait_ge(v1.h, 2 * tc + 2)
                            e.dma_start(xn_sp[tc * 128:(tc + 1) * 128, :],
                                        xn_t[tc % 2][:, :]).then_inc(sp1.h, 16)

                @blk.gpsimd
                def _(e):
                    start_i = 2 if hf == 0 else 0
                    for i, cb in enumerate(CB_ORDER):
                        if i < start_i:
                            continue
                        cw = 64 if cb == 130 else 128
                        if i >= 2:
                            e.wait_ge(tdA.h, b_td + i - 1)
                        e.dma_start(
                            wb[i % 2][:, 0:NKB * cw].rearrange(
                                "p (k c) -> p k c", k=NKB),
                            win_h[:, cb * 128:cb * 128 + cw].rearrange(
                                "(k p) c -> p k c", p=128),
                        ).then_inc(ldW.h, 16)
                ldW.n += 16 * len(CB_ORDER)
                ldT.n += 16 * NKB
                if hf == 0:
                    ld1.n += 16 * 16
                    s1.n += 32
                    v1.n += 32
                    sp1.n += 16 * 16

                @blk.tensor
                def _(e):
                    e.wait_ge(ldT.h, b_ldT + 16 * NKB)
                    for i, cb in enumerate(CB_ORDER):
                        cw = 64 if cb == 130 else 128
                        e.wait_ge(ldW.h, b_ldW + 16 * (i + 1))
                        if i >= 2:
                            e.wait_ge(evA.h, b_ev + i - 1)
                        ps = pa_ps[i % 2]
                        mi = None
                        for k in range(NKB):
                            for n in range(2):
                                mi = e.matmul(
                                    ps[0:cw, n * 512:(n + 1) * 512],
                                    wb[i % 2][:, k * cw:(k + 1) * cw],
                                    xn_fm[:, k * HB + n * 512:
                                          k * HB + (n + 1) * 512],
                                    start=(k == 0), stop=(k == NKB - 1),
                                )
                        mi.then_inc(tdA.h, 1)
                tdA.n += len(CB_ORDER)

                @blk.vector
                def _(e):
                    nxv = 0
                    for i, cb in enumerate(CB_ORDER):
                        ps = pa_ps[i % 2]
                        if 64 <= cb <= 129:
                            blkx = cb - 64
                            e.wait_ge(tdA.h, b_td + i + 1)
                            if hf == 1:
                                e.tensor_copy(conv_in[:, 0:3],
                                              bsave[:, blkx * 3:(blkx + 1) * 3])
                            e.tensor_copy(conv_in[:, 3:3 + HB],
                                          ps[:, 0:HB]).then_inc(evA.h, 1)
                            if debug and blkx < 2:
                                e.tensor_copy(rawdbg_t[blkx][:, :],
                                              conv_in[:, 3:3 + HB])
                            wc = [convw_s[:, blkx * 4 + k:blkx * 4 + k + 1]
                                  for k in range(4)]
                            e.tensor_scalar(scr1[:, :], conv_in[:, 0:HB], wc[0],
                                            None, OP.mult)
                            if nxv >= 2:
                                e.wait_ge(siA.h, b_si + nxv - 1)
                            e.scalar_tensor_tensor(scr2[nxv % 2][:, :],
                                                   conv_in[:, 1:1 + HB],
                                                   wc[1], scr1[:, :], OP.mult,
                                                   OP.add)
                            e.scalar_tensor_tensor(scr1[:, :], conv_in[:, 2:2 + HB],
                                                   wc[2], scr2[nxv % 2][:, :],
                                                   OP.mult, OP.add)
                            if hf == 0:
                                e.tensor_copy(bsave[:, blkx * 3:(blkx + 1) * 3],
                                              conv_in[:, HB:HB + 3])
                            e.scalar_tensor_tensor(scr2[nxv % 2][:, :],
                                                   conv_in[:, 3:3 + HB],
                                                   wc[3], scr1[:, :], OP.mult,
                                                   OP.add).then_inc(cvA.h, 1)
                            nxv += 1
                        elif cb == 130:
                            e.wait_ge(tdA.h, b_td + i + 1)
                            e.tensor_copy(dt_fm[0:64, t0:t0 + HB],
                                          ps[0:64, 0:HB]).then_inc(evA.h, 1)
                cvA.n += NXB

                @blk.scalar
                def _(e):
                    for _w in range(3):
                        e.activation(warm_o[:, :], warm_i[:, :], AF.Silu)
                    nxs = 0
                    nzs = 0
                    for i, cb in enumerate(CB_ORDER):
                        ps = pa_ps[i % 2]
                        if 64 <= cb <= 129:
                            e.wait_ge(cvA.h, b_cv + nxs + 1)
                            if nxs >= 2:
                                e.wait_ge(spX.h, b_spx + 16 * (nxs - 1))
                            e.activation(xs_t[nxs % 2][:, :], scr2[nxs % 2][:, :],
                                         AF.Silu).then_inc(siA.h, 1)
                            e.dma_start(
                                xs_sp[(cb - 64) * 128:(cb - 63) * 128, t0:t0 + HB],
                                xs_t[nxs % 2][:, :]).then_inc(spX.h, 16)
                            nxs += 1
                        elif cb <= 63:
                            e.wait_ge(tdA.h, b_td + i + 1)
                            if nzs >= 2:
                                e.wait_ge(spZ.h, b_spz + 16 * (nzs - 1))
                            e.activation(zs_t[nzs % 2][:, :], ps[:, 0:HB],
                                         AF.Silu).then_inc(evA.h, 1)
                            e.dma_start(
                                zs_sp[cb * 128:(cb + 1) * 128, t0:t0 + HB],
                                zs_t[nzs % 2][:, :]).then_inc(spZ.h, 16)
                            nzs += 1
                    e.wait_ge(spX.h, b_spx + 16 * NXB)
                    e.wait_ge(spZ.h, b_spz + 16 * NZB)
                    if debug:
                        e.dma_start(rawb[0:128, t0:t0 + HB],
                                    rawdbg_t[0][:, :]).then_inc(spZ.h, 16)
                        e.dma_start(rawb[128:256, t0:t0 + HB],
                                    rawdbg_t[1][:, :]).then_inc(spZ.h, 16)
                        e.wait_ge(spZ.h, b_spz + 16 * NZB + 32)
                siA.n += NXB
                spX.n += 16 * NXB
                spZ.n += 16 * NZB + (32 if debug else 0)
                evA.n += NXB + 1 + NZB

            nc.all_engine_barrier()

    es34 = contextlib.ExitStack()

    def sb34(name, shape, dt):
        return es34.enter_context(nc.sbuf_tensor(name, shape, dt))
    lndt_b = sb34("lndt_b", [NH, S], BF16)
    nA = sb34("nA", [NH, S], BF16)
    egi = sb34("egi", [NH, S], BF16)
    wst = sb34("wst", [NH, S], BF16)
    strict_up = sb34("strict_up", [128, 128], BF16)
    negtri = sb34("negtri", [128, 128], BF16)
    tri01 = sb34("tri01", [128, 128], F32)
    sel64 = sb34("sel64", [64, DSSM], BF16)
    d_i = sb34("d_i", [128, DSSM], BF16)

    v3, s3, g3_, t3_ = sem("v3"), sem("s3"), sem("g3"), sem("t3")
    with contextlib.ExitStack() as ph:
        def psb(name, shape, dt):
            return ph.enter_context(nc.sbuf_tensor(name, shape, dt))
        tmp64 = psb("tmp64", [NH, S], F32)
        ta = psb("ta", [NH, S], F32)
        pa_f = psb("pa_f", [NH, S], F32)
        d_tile = psb("d_tile", [128, NH], F32)
        ps3 = ph.enter_context(nc.psum_tensor("ps3", [128, 512], F32))

        with nc.Block() as blk:
            @blk.gpsimd
            def _(e):
                e.memset(strict_up[:, :], 1.0)
                e.affine_select(strict_up[:, :], strict_up[:, :], [[-1, 128]],
                                OP.is_gt, 0.0, base=0, channel_multiplier=1)
                e.memset(negtri[:, :], 0.0)
                e.affine_select(negtri[:, :], negtri[:, :], [[-1, 128]],
                                OP.is_gt, -1.0, base=0, channel_multiplier=1)
                e.memset(tri01[:, :], 0.0)
                e.affine_select(tri01[:, :], tri01[:, :], [[-1, 128]],
                                OP.is_gt, 1.0, base=0, channel_multiplier=1)
                e.memset(sel64[:, :], 1.0)
                e.affine_select(sel64[:, :], sel64[:, :], [[-1, 64], [0, 128]],
                                OP.is_equal, 0.0, base=0,
                                channel_multiplier=1).then_inc(g3_.h, 1)
            g3_.n += 1

            @blk.vector
            def _(e):
                e.tensor_scalar(tmp64[:, :], dt_fm[:, :], dtb_c[0:64, 0:1], None,
                                OP.add).then_inc(v3.h, 1)
            v3.n += 1

            @blk.scalar
            def _(e):
                for _w in range(3):
                    e.activation(warm_o[:, :], warm_i[:, :], AF.Exp)
                e.wait_ge(v3.h, v3.n)
                e.activation(ta[:, :], tmp64[:, :], AF.Abs)
                e.activation(ta[:, :], ta[:, :], AF.Exp, scale=-1.0)
                e.activation(ta[:, :], ta[:, :], AF.Ln, bias=1.0)
                e.activation(pa_f[:, :], tmp64[:, :], AF.Relu).then_inc(s3.h, 1)
            s3.n += 1

            @blk.vector
            def _(e):
                e.wait_ge(s3.h, s3.n)
                e.tensor_tensor(tmp64[:, :], pa_f[:, :], ta[:, :],
                                OP.add).then_inc(v3.h, 1)
            v3.n += 1

            @blk.scalar
            def _(e):
                e.wait_ge(v3.h, v3.n)
                e.activation(lndt_b[:, :], tmp64[:, :], AF.Ln).then_inc(s3.h, 1)
            s3.n += 1

            @blk.vector
            def _(e):
                e.wait_ge(s3.h, s3.n)
                e.tensor_scalar(nA[:, :], tmp64[:, :], a_c[0:64, 0:1], None,
                                OP.mult)
                i = None
                for c in range(NCH):
                    sl = slice(c * Q, (c + 1) * Q)
                    i = e.tensor_tensor_scan(pa_f[:, sl], ones64[:, :], nA[:, sl],
                                             0.0, OP.mult, OP.add)
                i.then_inc(v3.h, 1)
            v3.n += 1

            @blk.scalar
            def _(e):
                e.wait_ge(v3.h, v3.n)
                e.activation(egi[:, :], pa_f[:, :], AF.Exp,
                             scale=-1.0).then_inc(s3.h, 1)
            s3.n += 1

            @blk.vector
            def _(e):
                e.wait_ge(s3.h, s3.n)
                i = None
                for c in range(NCH):
                    sl = slice(c * Q, (c + 1) * Q)
                    i = e.tensor_scalar(ta[:, sl], pa_f[:, sl],
                                        pa_f[:, c * Q + Q - 1:c * Q + Q], None,
                                        OP.subtract)
                i.then_inc(v3.h, 1)
            v3.n += 1

            @blk.scalar
            def _(e):
                e.wait_ge(v3.h, v3.n)
                e.activation(pa_f[:, :], ta[:, :], AF.Exp).then_inc(s3.h, 1)
            s3.n += 1

            @blk.vector
            def _(e):
                e.wait_ge(s3.h, s3.n)
                e.tensor_tensor(wst[:, :], pa_f[:, :], tmp64[:, :],
                                OP.mult).then_inc(v3.h, 1)
            v3.n += 1

            @blk.tensor
            def _(e):
                e.matmul(ps3[0:128, 0:64], ones_row[0:1, 0:128], dpar_r[0:1, 0:64],
                         start=True, stop=True).then_inc(t3_.h, 1)
            t3_.n += 1

            @blk.vector
            def _(e):
                e.wait_ge(t3_.h, t3_.n)
                e.tensor_copy(d_tile[:, 0:64], ps3[0:128, 0:64])
                e.tensor_tensor(
                    d_i[:, :].rearrange("p (h q) -> p h q", h=64),
                    ident_b[:, :].unsqueeze(1).broadcast_to([128, 64, 128]),
                    d_tile[:, 0:64].unsqueeze(2).broadcast_to([128, 64, 128]),
                    OP.mult).then_inc(v3.h, 1)
            v3.n += 1
        nc.all_engine_barrier()

    sT, sV, sA, sV1, sP = sem("sc_t"), sem("sc_v"), sem("sc_a"), sem("sc_v1"), sem("sc_p")
    ldX, ldZ, ldB, spY = sem("sc_ldx"), sem("sc_ldz"), sem("sc_ldb"), sem("sc_spy")
    VPC, TPC, APC, PPC = 57, 71, 50, 32
    with contextlib.ExitStack() as ph:
        def psb(name, shape, dt):
            return ph.enter_context(nc.sbuf_tensor(name, shape, dt))
        S_bf = psb("S_bf", [128, DSSM], BF16)
        X_tm = [psb(f"X_tm{i}", [128, DSSM], BF16) for i in range(2)]
        Zs_tm = [psb(f"Zs_tm{i}", [128, DSSM // 2], BF16) for i in range(2)]
        yg = psb("yg", [128, DSSM], BF16)
        yrms_t = psb("yrms_t", [128, DSSM], BF16)
        B_c = [psb(f"B_c{i}", [128, 128], BF16) for i in range(2)]
        C_c = [psb(f"C_c{i}", [128, 128], BF16) for i in range(2)]
        B_T = psb("B_T", [128, 128], BF16)
        S_cbT = psb("S_cbT", [128, 128], BF16)
        nA_T = psb("nA_T", [128, 64], F32)
        egi_T = psb("egi_T", [128, 64], F32)
        w_T = psb("w_T", [128, 64], F32)
        eG_row = psb("eG_row", [1, 64], F32)
        eG_tile = psb("eG_tile", [128, 64], F32)
        rhs_g = [psb(f"rhs_g{i}", [128, GW], BF16) for i in range(4)]
        E_t = [psb(f"E_t{i}", [128, GW], BF16) for i in range(2)]
        M2t = [psb(f"M2t{i}", [128, GW], BF16) for i in range(2)]
        M2 = [psb(f"M2_{i}", [128, GW], BF16) for i in range(2)]
        xw = [psb(f"xw{i}", [128, GW], BF16) for i in range(2)]
        tbuf = [psb(f"tb{i}", [128, GW], BF16) for i in range(2)]
        t2b = [psb(f"t2b{i}", [128, GW], BF16) for i in range(2)]
        sq_scr = psb("sq_scr", [128, GW], F32)
        ss16 = psb("ss16", [128, 16], F32)
        ss_c = psb("ss_c", [128, 2], F32)
        PS_D = [ph.enter_context(nc.psum_tensor(f"psD{i}", [128, GW], F32))
                for i in range(2)]
        PS_I = [ph.enter_context(nc.psum_tensor(f"psI{i}", [128, GW], F32))
                for i in range(2)]
        PS_Y = [ph.enter_context(nc.psum_tensor(f"psY{i}", [128, GW], F32))
                for i in range(2)]
        PS_S = [ph.enter_context(nc.psum_tensor(f"psS{i}", [128, GW], F32))
                for i in range(2)]

        with nc.Block() as blk:
            @blk.gpsimd
            def _(e):
                e.memset(S_bf[:, :], 0.0)
        nc.all_engine_barrier()

        VORD = [("pre", k) for k in range(1, 8)]
        for _i in range(NGRP + 2):
            if _i < NGRP:
                VORD.append(("V5", _i))
            if 1 <= _i <= NGRP:
                VORD.append(("V3", _i - 1))
                VORD.append(("V4", _i - 1))
            if 2 <= _i <= NGRP + 1:
                VORD.append(("V7", _i - 2))
        VORD += [("ss", 0), ("yr", 0)]
        VPOS = {k: i + 1 for i, k in enumerate(VORD)}
        VPC = len(VORD)
        AORD = []
        for _i in range(NGRP + 2):
            if _i < NGRP:
                AORD.append(("A1", _i))
            if 1 <= _i <= NGRP:
                AORD.append(("A4", _i - 1))
            if 2 <= _i <= NGRP + 1:
                AORD.append(("A2", _i - 2))
        AORD += [("ln", 0), ("ex", 0)]
        APOS = {k: i + 1 for i, k in enumerate(AORD)}
        APC = len(AORD)
        TPC = 71
        PPC = NGRP

        def _norm(c, g):
            while g < 0:
                c -= 1
                g += NGRP
            return (c, g) if c >= 0 else None

        def tT(c, g, j):
            return c * TPC + 7 + 4 * g + j

        def tPre(c, k):
            return c * TPC + k

        def vI(c, kind, g=0):
            return c * VPC + VPOS[(kind, g)]

        def aI(c, kind, g=0):
            return c * APC + APOS[(kind, g)]

        def pI(c, g):
            return c * PPC + g + 1

        def v1i(c, g):
            return c * NGRP + g + 1

        with nc.Block() as blk:
            @blk.sync
            def _(e):
                e.dma_start(X_tm[0][:, :], xs_sp[0:DSSM, 0:128],
                            transpose=True).then_inc(ldX.h, 16)
                e.dma_start(B_c[0][:, :],
                            xs_sp[DSSM:DSSM + 128, 0:128]).then_inc(ldB.h, 16)
                e.dma_start(C_c[0][:, :],
                            xs_sp[DSSM + 128:DSSM + 256, 0:128]).then_inc(ldB.h, 16)
                for c in range(NCH):
                    sl = slice(c * Q, (c + 1) * Q)
                    if c >= 1:
                        e.wait_ge(sV.h, vI(c - 1, "yr"))
                        e.dma_start(yr_sp[(c - 1) * Q:c * Q, :],
                                    yrms_t[:, :]).then_inc(spY.h, 16)
                        e.wait_ge(sV.h, vI(c - 1, "V7", 7))
                    e.dma_start(Zs_tm[0][:, :], zs_sp[0:DSSM // 2, sl],
                                transpose=True).then_inc(ldZ.h, 16)
                    if c >= 1:
                        e.wait_ge(sV.h, vI(c - 1, "V7", 15))
                    e.dma_start(Zs_tm[1][:, :], zs_sp[DSSM // 2:DSSM, sl],
                                transpose=True).then_inc(ldZ.h, 16)
                    if c + 1 < NCH:
                        sl1 = slice((c + 1) * Q, (c + 2) * Q)
                        if c >= 1:
                            e.wait_ge(sT.h, tT(c - 1, 15, 4))
                            e.wait_ge(sV.h, vI(c - 1, "V5", 15))
                        e.dma_start(X_tm[(c + 1) % 2][:, :], xs_sp[0:DSSM, sl1],
                                    transpose=True).then_inc(ldX.h, 16)
                        e.dma_start(B_c[(c + 1) % 2][:, :],
                                    xs_sp[DSSM:DSSM + 128, sl1]).then_inc(ldB.h, 16)
                        e.dma_start(C_c[(c + 1) % 2][:, :],
                                    xs_sp[DSSM + 128:DSSM + 256, sl1]).then_inc(
                            ldB.h, 16)
                e.wait_ge(sV.h, vI(NCH - 1, "yr"))
                e.dma_start(yr_sp[(NCH - 1) * Q:NCH * Q, :],
                            yrms_t[:, :]).then_inc(spY.h, 16)
                e.wait_ge(spY.h, 16 * NCH)
            ldX.n += 16 * NCH
            ldZ.n += 32 * NCH
            ldB.n += 32 * NCH
            spY.n += 16 * NCH

            @blk.tensor
            def _(e):
                for c in range(NCH):
                    sl = slice(c * Q, (c + 1) * Q)
                    if c >= 1:
                        e.wait_ge(sA.h, aI(c - 1, "A1", 14))
                        e.wait_ge(sV.h, vI(c - 1, "V4", 14))
                    e.matmul(PS_D[0][0:128, 0:64], nA[:, sl], ident_b[0:64, 0:64],
                             start=True, stop=True).then_inc(sT.h, 1)
                    e.matmul(PS_D[0][0:128, 64:128], egi[:, sl],
                             ident_b[0:64, 0:64], start=True,
                             stop=True).then_inc(sT.h, 1)
                    e.matmul(PS_D[0][0:128, 128:192], wst[:, sl],
                             ident_b[0:64, 0:64], start=True,
                             stop=True).then_inc(sT.h, 1)
                    e.matmul(PS_D[0][0:1, 192:256],
                             egi[:, c * Q + Q - 1:c * Q + Q],
                             ident_b[0:64, 0:64], start=True,
                             stop=True).then_inc(sT.h, 1)
                    e.wait_ge(sV.h, c * VPC + 4)
                    e.matmul(PS_D[0][0:128, 256:320], ones_row[0:1, 0:128],
                             eG_row[0:1, 0:64], start=True,
                             stop=True).then_inc(sT.h, 1)
                    e.wait_ge(ldB.h, 32 * (c + 1))
                    e.matmul(PS_I[0][0:128, 0:128], B_c[c % 2][:, :],
                             C_c[c % 2][:, :], start=True,
                             stop=True).then_inc(sT.h, 1)
                    e.matmul(PS_I[0][0:128, 128:256], B_c[c % 2][:, :],
                             ident_b[:, :], start=True, stop=True).then_inc(sT.h, 1)
                    for g in range(NGRP):
                        p_ = g % 2
                        gsl = slice(g * GW, (g + 1) * GW)
                        pv = _norm(c, g - 2)
                        e.wait_ge(sV1.h, v1i(c, g))
                        if pv:
                            e.wait_ge(sA.h, aI(pv[0], "A1", pv[1]))
                        e.matmul(PS_D[p_][:, :], strict_up[:, :],
                                 rhs_g[g % 4][:, :], start=True, stop=False)
                        e.matmul(PS_D[p_][:, :], lndt_b[:, sl],
                                 sel64[:, g * GW:(g + 1) * GW],
                                 start=False, stop=True).then_inc(sT.h, 1)
                        if pv:
                            e.wait_ge(sV.h, vI(pv[0], "V4", pv[1]))
                        if c >= 1:
                            e.wait_ge(sA.h, aI(c - 1, "A4", g))
                        e.matmul(PS_I[p_][:, :], C_c[c % 2][:, :], S_bf[:, gsl],
                                 start=True, stop=True).then_inc(sT.h, 1)
                        e.wait_ge(sV.h, vI(c, "V4", g))
                        e.matmul(PS_Y[p_][:, :], ident_b[:, :], tbuf[p_][:, :],
                                 start=True, stop=False, skip_group_check=True)
                        mi = None
                        for hh in range(4):
                            mi = e.matmul(PS_Y[p_][:, hh * 128:(hh + 1) * 128],
                                          M2[p_][:, hh * 128:(hh + 1) * 128],
                                          X_tm[c % 2][:, (4 * g + hh) * 128:
                                                      (4 * g + hh + 1) * 128],
                                          start=False, stop=(hh == 3),
                                          skip_group_check=True)
                        mi.then_inc(sT.h, 1)
                        if pv:
                            e.wait_ge(sA.h, aI(pv[0], "A4", pv[1]))
                        e.wait_ge(sV.h, vI(c, "V5", g))
                        e.wait_ge(sP.h, pI(c, g))
                        e.matmul(PS_S[p_][:, :], B_T[:, :], xw[p_][:, :],
                                 start=True, stop=False)
                        e.matmul(PS_S[p_][:, :], ident_b[:, :], t2b[p_][:, :],
                                 start=False, stop=True,
                                 skip_group_check=True).then_inc(sT.h, 1)
            sT.n += TPC * NCH

            @blk.vector
            def _(e):
                def emit_v1(c, g):
                    pv4 = _norm(c, g - 4)
                    if pv4:
                        e.wait_ge(sT.h, tT(pv4[0], pv4[1], 1))
                    hsl = slice(4 * g, 4 * g + 4)
                    e.tensor_tensor(
                        rhs_g[g % 4][:, :].rearrange("p (h q) -> p h q", h=4),
                        negtri[:, :].unsqueeze(1).broadcast_to([128, 4, 128]),
                        nA_T[:, hsl].unsqueeze(2).broadcast_to([128, 4, 128]),
                        OP.mult).then_inc(sV1.h, 1)

                for c in range(NCH):
                    e.wait_ge(sT.h, tPre(c, 4))
                    e.tensor_copy(nA_T[:, :], PS_D[0][0:128, 0:64]).then_inc(sV.h, 1)
                    e.tensor_copy(egi_T[:, :],
                                  PS_D[0][0:128, 64:128]).then_inc(sV.h, 1)
                    e.tensor_copy(w_T[:, :],
                                  PS_D[0][0:128, 128:192]).then_inc(sV.h, 1)
                    e.tensor_copy(eG_row[0:1, :],
                                  PS_D[0][0:1, 192:256]).then_inc(sV.h, 1)
                    e.wait_ge(sT.h, tPre(c, 5))
                    e.tensor_copy(eG_tile[:, :],
                                  PS_D[0][0:128, 256:320]).then_inc(sV.h, 1)
                    e.wait_ge(sT.h, tPre(c, 6))
                    e.tensor_tensor(S_cbT[:, :], PS_I[0][0:128, 0:128],
                                    tri01[:, :], OP.mult).then_inc(sV.h, 1)
                    e.wait_ge(sT.h, tPre(c, 7))
                    e.tensor_copy(B_T[:, :],
                                  PS_I[0][0:128, 128:256]).then_inc(sV.h, 1)
                    emit_v1(c, 0)
                    emit_v1(c, 1)
                    for i in range(NGRP + 2):
                        if i < NGRP:
                            g = i
                            p_ = g % 2
                            gsl = slice(g * GW, (g + 1) * GW)
                            hsl = slice(4 * g, 4 * g + 4)
                            pv = _norm(c, g - 2)
                            e.wait_ge(ldX.h, 16 * (c + 1))
                            if pv:
                                e.wait_ge(sT.h, tT(pv[0], pv[1], 4))
                            e.tensor_tensor(
                                xw[p_][:, :].rearrange("p (h q) -> p h q", h=4),
                                X_tm[c % 2][:, gsl].rearrange(
                                    "p (h q) -> p h q", h=4),
                                w_T[:, hsl].unsqueeze(2).broadcast_to(
                                    [128, 4, 128]),
                                OP.mult).then_inc(sV.h, 1)
                        if 1 <= i <= NGRP:
                            g = i - 1
                            p_ = g % 2
                            gsl = slice(g * GW, (g + 1) * GW)
                            hsl = slice(4 * g, 4 * g + 4)
                            pv = _norm(c, g - 2)
                            e.wait_ge(sA.h, aI(c, "A1", g))
                            if pv:
                                e.wait_ge(sT.h, tT(pv[0], pv[1], 3))
                            e.tensor_tensor(
                                M2t[p_][:, :].rearrange("p (h q) -> p h q", h=4),
                                E_t[p_][:, :].rearrange("p (h q) -> p h q", h=4),
                                S_cbT[:, :].unsqueeze(1).broadcast_to(
                                    [128, 4, 128]),
                                OP.mult)
                            e.tensor_tensor(M2[p_][:, :], M2t[p_][:, :],
                                            d_i[:, gsl], OP.add).then_inc(sV.h, 1)
                            e.wait_ge(sT.h, tT(c, g, 2))
                            e.tensor_tensor(
                                tbuf[p_][:, :].rearrange("p (h q) -> p h q", h=4),
                                PS_I[p_][:, :].rearrange("p (h q) -> p h q", h=4),
                                egi_T[:, hsl].unsqueeze(2).broadcast_to(
                                    [128, 4, 128]),
                                OP.mult).then_inc(sV.h, 1)
                        if 2 <= i <= NGRP + 1:
                            g = i - 2
                            p_ = g % 2
                            gsl = slice(g * GW, (g + 1) * GW)
                            e.wait_ge(sT.h, tT(c, g, 3))
                            e.wait_ge(ldZ.h, 32 * c + (16 if g < 8 else 32))
                            e.tensor_tensor(yg[:, gsl], PS_Y[p_][:, :],
                                            Zs_tm[g // 8][:, (g % 8) * GW:
                                                          (g % 8 + 1) * GW],
                                            OP.mult).then_inc(sV.h, 1)
                        if i + 2 < NGRP:
                            emit_v1(c, i + 2)
                    e.wait_ge(sA.h, aI(c, "A2", 15))
                    e.reduce_sum(ss_c[:, 0:1], ss16[:, :],
                                 axis=AX.X).then_inc(sV.h, 1)
                    e.wait_ge(sA.h, aI(c, "ex"))
                    if c >= 1:
                        e.wait_ge(spY.h, 16 * c)
                    e.tensor_scalar(yrms_t[:, :], yg[:, :], ss_c[:, 1:2], None,
                                    OP.mult).then_inc(sV.h, 1)
            sV.n += VPC * NCH
            sV1.n += NGRP * NCH

            @blk.gpsimd
            def _(e):
                for c in range(NCH):
                    for g in range(NGRP):
                        p_ = g % 2
                        gsl = slice(g * GW, (g + 1) * GW)
                        hsl = slice(4 * g, 4 * g + 4)
                        pv = _norm(c, g - 2)
                        if c >= 1:
                            e.wait_ge(sA.h, aI(c - 1, "A4", g))
                        if pv:
                            e.wait_ge(sT.h, tT(pv[0], pv[1], 4))
                        e.wait_ge(sV.h, c * VPC + 5)
                        e.tensor_tensor(
                            t2b[p_][:, :].rearrange("p (h q) -> p h q", h=4),
                            S_bf[:, gsl].rearrange("p (h q) -> p h q", h=4),
                            eG_tile[:, hsl].unsqueeze(2).broadcast_to(
                                [128, 4, 128]),
                            OP.mult).then_inc(sP.h, 1)
            sP.n += PPC * NCH

            @blk.scalar
            def _(e):
                for c in range(NCH):
                    for i in range(NGRP + 2):
                        if i < NGRP:
                            g = i
                            p_ = g % 2
                            pv = _norm(c, g - 2)
                            e.wait_ge(sT.h, tT(c, g, 1))
                            if pv:
                                e.wait_ge(sV.h, vI(pv[0], "V3", pv[1]))
                            e.activation(E_t[p_][:, :], PS_D[p_][:, :],
                                         AF.Exp).then_inc(sA.h, 1)
                        if 1 <= i <= NGRP:
                            g = i - 1
                            p_ = g % 2
                            gsl = slice(g * GW, (g + 1) * GW)
                            e.wait_ge(sT.h, tT(c, g, 4))
                            e.activation(S_bf[:, gsl], PS_S[p_][:, :],
                                         AF.Copy).then_inc(sA.h, 1)
                        if 2 <= i <= NGRP + 1:
                            g = i - 2
                            gsl = slice(g * GW, (g + 1) * GW)
                            e.wait_ge(sV.h, vI(c, "V7", g))
                            if g == 0 and c >= 1:
                                e.wait_ge(sV.h, vI(c - 1, "ss"))
                            e.activation(sq_scr[:, :], yg[:, gsl], AF.Square,
                                         accum_out=ss16[:, g:g + 1]).then_inc(
                                sA.h, 1)
                    e.wait_ge(sV.h, vI(c, "ss"))
                    e.activation(ss_c[:, 1:2], ss_c[:, 0:1], AF.Ln,
                                 scale=1.0 / DSSM,
                                 bias=eps_c[:, 0:1]).then_inc(sA.h, 1)
                    e.activation(ss_c[:, 1:2], ss_c[:, 1:2], AF.Exp,
                                 scale=-0.5).then_inc(sA.h, 1)
            sA.n += APC * NCH
        nc.all_engine_barrier()
    es34.close()

    ldYT, ldWo = sem("c_ldyt"), sem("c_ldwo")
    tdC, evC, spC, vC = sem("c_td"), sem("c_ev"), sem("c_sp"), sem("c_v")
    with contextlib.ExitStack() as ph:
        def psb(name, shape, dt):
            return ph.enter_context(nc.sbuf_tensor(name, shape, dt))
        yk = psb("yk", [128, NZB * HB], BF16)
        ystg = [psb(f"ystg{i}", [128, HB], BF16) for i in range(2)]
        wo = [psb(f"wo{i}", [128, NZB * 128], BF16) for i in range(2)]
        at_t = [psb(f"at_t{i}", [128, HB], BF16) for i in range(2)]
        psC = [ph.enter_context(nc.psum_tensor(f"psC{i}", [128, HB], F32))
               for i in range(2)]

        for hf in range(2):
            t0 = hf * HB
            b_ldyt, b_ldwo, b_td, b_ev, b_sp, b_v = (ldYT.n, ldWo.n, tdC.n,
                                                     evC.n, spC.n, vC.n)
            with nc.Block() as blk:
                @blk.sync
                def _(e):
                    for k in range(NZB):
                        if k >= 2:
                            e.wait_ge(vC.h, b_v + k - 1)
                        e.dma_start(ystg[k % 2][:, :],
                                    yr_sp[t0:t0 + HB, k * 128:(k + 1) * 128],
                                    transpose=True).then_inc(ldYT.h, 16)
                ldYT.n += 16 * NZB

                @blk.vector
                def _(e):
                    for k in range(NZB):
                        e.wait_ge(ldYT.h, b_ldyt + 16 * (k + 1))
                        e.tensor_scalar(yk[:, k * HB:(k + 1) * HB],
                                        ystg[k % 2][:, :], nwt_s[:, k:k + 1],
                                        None, OP.mult).then_inc(vC.h, 1)
                vC.n += NZB

                @blk.gpsimd
                def _(e):
                    for ob in range(32):
                        if ob >= 2:
                            e.wait_ge(tdC.h, b_td + ob - 1)
                        e.dma_start(
                            wo[ob % 2][:, :].rearrange("p (k c) -> p k c", k=NZB),
                            wout_h[:, ob * 128:(ob + 1) * 128].rearrange(
                                "(k p) c -> p k c", p=128),
                        ).then_inc(ldWo.h, 16)
                ldWo.n += 16 * 32

                @blk.tensor
                def _(e):
                    for ob in range(32):
                        e.wait_ge(ldWo.h, b_ldwo + 16 * (ob + 1))
                        if ob >= 2:
                            e.wait_ge(evC.h, b_ev + ob - 1)
                        mi = None
                        for k in range(NZB):
                            if ob == 0:
                                e.wait_ge(vC.h, b_v + k + 1)
                            for n in range(2):
                                mi = e.matmul(
                                    psC[ob % 2][:, n * 512:(n + 1) * 512],
                                    wo[ob % 2][:, k * 128:(k + 1) * 128],
                                    yk[:, k * HB + n * 512:k * HB + (n + 1) * 512],
                                    start=(k == 0), stop=(k == NZB - 1))
                        mi.then_inc(tdC.h, 1)
                tdC.n += 32

                @blk.vector
                def _(e):
                    for ob in range(32):
                        e.wait_ge(tdC.h, b_td + ob + 1)
                        if ob >= 2:
                            e.wait_ge(spC.h, b_sp + 16 * (ob - 1))
                        e.tensor_copy(at_t[ob % 2][:, :],
                                      psC[ob % 2][:, :]).then_inc(evC.h, 1)
                vC.n += 0

                @blk.scalar
                def _(e):
                    for ob in range(32):
                        e.wait_ge(evC.h, b_ev + ob + 1)
                        e.dma_start(at_sp[ob * 128:(ob + 1) * 128, t0:t0 + HB],
                                    at_t[ob % 2][:, :]).then_inc(spC.h, 16)
                    e.wait_ge(spC.h, b_sp + 16 * 32)
                evC.n += 32
                spC.n += 16 * 32
            nc.all_engine_barrier()

    ldE, vE, sE, spE = sem("e_ld"), sem("e_v"), sem("e_s"), sem("e_sp")
    with contextlib.ExitStack() as ph:
        def psb(name, shape, dt):
            return ph.enter_context(nc.sbuf_tensor(name, shape, dt))
        at_tm = [psb(f"at_tm{i}", [128, H], BF16) for i in range(2)]
        xr_t = [psb(f"xr_t{i}", [128, H], F32) for i in range(2)]
        sq_e = [psb(f"sq_e{i}", [128, H], F32) for i in range(2)]
        x2_t = [psb(f"x2_t{i}", [128, H], F32) for i in range(2)]
        xn2_t = [psb(f"xn2_t{i}", [128, H], BF16) for i in range(2)]
        st_e = psb("st_e", [128, 8], F32)

        with nc.Block() as blk:
            @blk.sync
            def _(e):
                for tc in range(NCH):
                    if tc >= 2:
                        e.wait_ge(vE.h, 4 * (tc - 2) + 4)
                        e.wait_ge(sE.h, 4 * (tc - 2) + 3)
                    e.dma_start(at_tm[tc % 2][:, :],
                                at_sp[0:H, tc * 128:(tc + 1) * 128],
                                transpose=True).then_inc(ldE.h, 16)
                    e.dma_start(xr_t[tc % 2][:, :],
                                x_h[tc * 128:(tc + 1) * 128, :]).then_inc(ldE.h, 16)
            ldE.n += 32 * NCH

            @blk.scalar
            def _(e):
                for tc in range(NCH):
                    e.wait_ge(ldE.h, 32 * (tc + 1))
                    if tc >= 2:
                        e.wait_ge(vE.h, 4 * (tc - 2) + 3)
                    e.activation(sq_e[tc % 2][:, :], at_tm[tc % 2][:, :],
                                 AF.Square).then_inc(sE.h, 1)
                    e.wait_ge(vE.h, 4 * tc + 1)
                    e.activation(st_e[:, 6:7], st_e[:, 4:5], AF.Ln, scale=-1.0,
                                 bias=eps_c[:, 0:1])
                    e.activation(st_e[:, 7:8], st_e[:, 6:7], AF.Exp,
                                 scale=-0.5).then_inc(sE.h, 1)
                    e.wait_ge(vE.h, 4 * tc + 2)
                    e.activation(sq_e[tc % 2][:, :], x2_t[tc % 2][:, :],
                                 AF.Square).then_inc(sE.h, 1)
                    e.wait_ge(vE.h, 4 * tc + 3)
                    e.activation(st_e[:, 6:7], st_e[:, 4:5], AF.Ln, scale=-1.0,
                                 bias=eps_c[:, 0:1])
                    e.activation(st_e[:, 7:8], st_e[:, 6:7], AF.Exp,
                                 scale=-0.5).then_inc(sE.h, 1)
            sE.n += 4 * NCH

            @blk.vector
            def _(e):
                for tc in range(NCH):
                    e.wait_ge(ldE.h, 32 * (tc + 1))
                    e.reduce_sum(st_e[:, 0:1], at_tm[tc % 2][:, :], axis=AX.X)
                    e.wait_ge(sE.h, 4 * tc + 1)
                    e.reduce_sum(st_e[:, 1:2], sq_e[tc % 2][:, :], axis=AX.X)
                    e.tensor_scalar(st_e[:, 2:3], st_e[:, 0:1], 1.0 / H, None,
                                    OP.mult)
                    e.tensor_scalar(st_e[:, 3:4], st_e[:, 1:2], 1.0 / H, None,
                                    OP.mult)
                    e.tensor_scalar(st_e[:, 0:1], st_e[:, 0:1], 0.0, None, OP.mult)
                    e.scalar_tensor_tensor(st_e[:, 4:5], st_e[:, 2:3],
                                           st_e[:, 2:3], st_e[:, 3:4],
                                           OP.mult, OP.subtract).then_inc(vE.h, 1)
                    e.wait_ge(sE.h, 4 * tc + 2)
                    if tc >= 2:
                        e.wait_ge(spE.h, 32 * (tc - 2) + 16)
                    e.tensor_scalar(x2_t[tc % 2][:, :], at_tm[tc % 2][:, :],
                                    st_e[:, 2:3], st_e[:, 7:8], OP.subtract,
                                    OP.mult)
                    e.tensor_tensor(x2_t[tc % 2][:, :], x2_t[tc % 2][:, :],
                                    xr_t[tc % 2][:, :], OP.add).then_inc(vE.h, 1)
                    e.wait_ge(sE.h, 4 * tc + 3)
                    e.reduce_sum(st_e[:, 0:1], x2_t[tc % 2][:, :], axis=AX.X)
                    e.reduce_sum(st_e[:, 1:2], sq_e[tc % 2][:, :], axis=AX.X)
                    e.tensor_scalar(st_e[:, 2:3], st_e[:, 0:1], 1.0 / H, None,
                                    OP.mult)
                    e.tensor_scalar(st_e[:, 3:4], st_e[:, 1:2], 1.0 / H, None,
                                    OP.mult)
                    e.tensor_scalar(st_e[:, 0:1], st_e[:, 0:1], 0.0, None, OP.mult)
                    e.scalar_tensor_tensor(st_e[:, 4:5], st_e[:, 2:3],
                                           st_e[:, 2:3], st_e[:, 3:4],
                                           OP.mult, OP.subtract).then_inc(vE.h, 1)
                    e.wait_ge(sE.h, 4 * tc + 4)
                    if tc >= 2:
                        e.wait_ge(spE.h, 32 * (tc - 2) + 32)
                    e.tensor_scalar(xn2_t[tc % 2][:, :], x2_t[tc % 2][:, :],
                                    st_e[:, 2:3], st_e[:, 7:8], OP.subtract,
                                    OP.mult).then_inc(vE.h, 1)
            vE.n += 4 * NCH

            @blk.gpsimd
            def _(e):
                for tc in range(NCH):
                    e.wait_ge(vE.h, 4 * tc + 2)
                    e.dma_start(x2_sp[tc * 128:(tc + 1) * 128, :],
                                x2_t[tc % 2][:, :]).then_inc(spE.h, 16)
                    e.wait_ge(vE.h, 4 * tc + 4)
                    e.dma_start(xn2_sp[tc * 128:(tc + 1) * 128, :],
                                xn2_t[tc % 2][:, :]).then_inc(spE.h, 16)
                e.wait_ge(spE.h, 32 * NCH)
            spE.n += 32 * NCH

        nc.all_engine_barrier()

    ldXT, ldWf = sem("d_ldxt"), sem("d_ldwf")
    tdD, evD, spD, vD = sem("d_td"), sem("d_ev"), sem("d_sp"), sem("d_v")
    with contextlib.ExitStack() as ph:
        def psb(name, shape, dt):
            return ph.enter_context(nc.sbuf_tensor(name, shape, dt))
        xk2 = psb("xk2", [128, NKB * S], BF16)
        wf = [psb(f"wf{i}", [128, NKB * 128], BF16) for i in range(2)]
        la_s = psb("la_s", [128, NKB * 48], BF16)
        lb_s = psb("lb_s", [48, H], BF16)
        bias_s = psb("bias_s", [128, 32], F32)
        u48 = psb("u48", [48, S], BF16)
        ff_t = [psb(f"ff_t{i}", [128, S], BF16) for i in range(2)]

        with contextlib.ExitStack() as ph2:
            psU = ph2.enter_context(nc.psum_tensor("psU", [48, S], F32))
            with nc.Block() as blk:
                @blk.sync
                def _(e):
                    e.dma_start(bias_s[:, :], biast_h[:, :]).then_inc(ldWf.h, 16)
                    for k in range(NKB):
                        e.dma_start(xk2[:, k * S:(k + 1) * S],
                                    xn2_sp[0:S, k * 128:(k + 1) * 128],
                                    transpose=True).then_inc(ldXT.h, 16)
                ldXT.n += 16 * NKB

                @blk.gpsimd
                def _(e):
                    e.dma_start(la_s[:, :].rearrange("p (k r) -> p k r", k=NKB),
                                lat_h[:, :].rearrange("(k p) r -> p k r",
                                                      p=128)).then_inc(ldWf.h, 16)
                    e.dma_start(lb_s[:, :], lbt_h[:, :]).then_inc(ldWf.h, 16)
                ldWf.n += 16 * 3

                @blk.tensor
                def _(e):
                    e.wait_ge(ldWf.h, 16 * 3)
                    mi = None
                    for k in range(NKB):
                        e.wait_ge(ldXT.h, 16 * (k + 1))
                        for n in range(4):
                            mi = e.matmul(
                                psU[0:48, n * 512:(n + 1) * 512],
                                la_s[:, k * 48:(k + 1) * 48],
                                xk2[:, k * S + n * 512:k * S + (n + 1) * 512],
                                start=(k == 0), stop=(k == NKB - 1))
                    mi.then_inc(tdD.h, 1)
                tdD.n += 1

                @blk.vector
                def _(e):
                    e.wait_ge(tdD.h, 1)
                    e.tensor_scalar(u48[0:48, :], psU[0:48, :], m48[0:48, 0:1],
                                    None, OP.mult).then_inc(vD.h, 1)
                vD.n += 1
            nc.all_engine_barrier()

        psD = [ph.enter_context(nc.psum_tensor(f"psDD{i}", [128, S], F32))
               for i in range(2)]
        b_ldwf = ldWf.n
        with nc.Block() as blk:
            @blk.gpsimd
            def _(e):
                for ob in range(32):
                    if ob >= 2:
                        e.wait_ge(tdD.h, 1 + ob - 1)
                    e.dma_start(
                        wf[ob % 2][:, :].rearrange("p (k c) -> p k c", k=NKB),
                        wft_h[:, ob * 128:(ob + 1) * 128].rearrange(
                            "(k p) c -> p k c", p=128),
                    ).then_inc(ldWf.h, 16)
            ldWf.n += 16 * 32

            @blk.tensor
            def _(e):
                for ob in range(32):
                    e.wait_ge(ldWf.h, b_ldwf + 16 * (ob + 1))
                    if ob >= 2:
                        e.wait_ge(evD.h, ob - 1)
                    for k in range(NKB):
                        for n in range(4):
                            e.matmul(psD[ob % 2][:, n * 512:(n + 1) * 512],
                                     wf[ob % 2][:, k * 128:(k + 1) * 128],
                                     xk2[:, k * S + n * 512:k * S + (n + 1) * 512],
                                     start=(k == 0), stop=False)
                    mi = None
                    for n in range(4):
                        mi = e.matmul(psD[ob % 2][:, n * 512:(n + 1) * 512],
                                      lb_s[0:48, ob * 128:(ob + 1) * 128],
                                      u48[0:48, n * 512:(n + 1) * 512],
                                      start=False, stop=True)
                    mi.then_inc(tdD.h, 1)
            tdD.n += 32

            @blk.vector
            def _(e):
                for ob in range(32):
                    e.wait_ge(tdD.h, 1 + ob + 1)
                    if ob >= 2:
                        e.wait_ge(spD.h, 16 * (ob - 1))
                    e.tensor_scalar(ff_t[ob % 2][:, :], psD[ob % 2][:, :],
                                    bias_s[:, ob:ob + 1], None,
                                    OP.add).then_inc(evD.h, 1)
            evD.n += 32

            @blk.scalar
            def _(e):
                for ob in range(32):
                    e.wait_ge(evD.h, ob + 1)
                    e.dma_start(ff_sp[ob * 128:(ob + 1) * 128, :],
                                ff_t[ob % 2][:, :]).then_inc(spD.h, 16)
                e.wait_ge(spD.h, 16 * 32)
            spD.n += 16 * 32
        nc.all_engine_barrier()

    ldF, vF, sF, stO = sem("f_ld"), sem("f_v"), sem("f_s"), sem("f_out")
    with contextlib.ExitStack() as ph:
        def psb(name, shape, dt):
            return ph.enter_context(nc.sbuf_tensor(name, shape, dt))
        ff_tm = [psb(f"ff_tm{i}", [128, H], BF16) for i in range(2)]
        x2r_t = [psb(f"x2r_t{i}", [128, H], F32) for i in range(2)]
        sq_f = [psb(f"sq_f{i}", [128, H], F32) for i in range(2)]
        out_t = [psb(f"out_t{i}", [128, H], F32) for i in range(2)]
        st_f = psb("st_f", [128, 8], F32)

        with nc.Block() as blk:
            @blk.sync
            def _(e):
                for tc in range(NCH):
                    if tc >= 2:
                        e.wait_ge(vF.h, 3 * (tc - 2) + 3)
                        e.wait_ge(sF.h, 2 * (tc - 2) + 1)
                    e.dma_start(ff_tm[tc % 2][:, :],
                                ff_sp[0:H, tc * 128:(tc + 1) * 128],
                                transpose=True).then_inc(ldF.h, 16)
                    e.dma_start(x2r_t[tc % 2][:, :],
                                x2_sp[tc * 128:(tc + 1) * 128, :]).then_inc(
                        ldF.h, 16)
            ldF.n += 32 * NCH

            @blk.scalar
            def _(e):
                for tc in range(NCH):
                    e.wait_ge(ldF.h, 32 * (tc + 1))
                    if tc >= 2:
                        e.wait_ge(vF.h, 3 * (tc - 2) + 1)
                    e.activation(sq_f[tc % 2][:, :], ff_tm[tc % 2][:, :],
                                 AF.Square).then_inc(sF.h, 1)
                    e.wait_ge(vF.h, 3 * tc + 1)
                    e.activation(st_f[:, 6:7], st_f[:, 4:5], AF.Ln, scale=-1.0,
                                 bias=eps_c[:, 0:1])
                    e.activation(st_f[:, 7:8], st_f[:, 6:7], AF.Exp,
                                 scale=-0.5).then_inc(sF.h, 1)
            sF.n += 2 * NCH

            @blk.vector
            def _(e):
                for tc in range(NCH):
                    e.wait_ge(ldF.h, 32 * (tc + 1))
                    e.reduce_sum(st_f[:, 0:1], ff_tm[tc % 2][:, :], axis=AX.X)
                    e.wait_ge(sF.h, 2 * tc + 1)
                    e.reduce_sum(st_f[:, 1:2], sq_f[tc % 2][:, :], axis=AX.X)
                    e.tensor_scalar(st_f[:, 2:3], st_f[:, 0:1], 1.0 / H, None,
                                    OP.mult)
                    e.tensor_scalar(st_f[:, 3:4], st_f[:, 1:2], 1.0 / H, None,
                                    OP.mult)
                    e.tensor_scalar(st_f[:, 0:1], st_f[:, 0:1], 0.0, None, OP.mult)
                    e.scalar_tensor_tensor(st_f[:, 4:5], st_f[:, 2:3],
                                           st_f[:, 2:3], st_f[:, 3:4],
                                           OP.mult, OP.subtract).then_inc(vF.h, 1)
                    e.wait_ge(sF.h, 2 * tc + 2)
                    if tc >= 2:
                        e.wait_ge(stO.h, 16 * (tc - 1))
                    e.tensor_scalar(out_t[tc % 2][:, :], ff_tm[tc % 2][:, :],
                                    st_f[:, 2:3], st_f[:, 7:8], OP.subtract,
                                    OP.mult).then_inc(vF.h, 1)
                    e.tensor_tensor(out_t[tc % 2][:, :], out_t[tc % 2][:, :],
                                    x2r_t[tc % 2][:, :], OP.add).then_inc(vF.h, 1)
                e.wait_ge(stO.h, 16 * NCH)
            vF.n += 3 * NCH

            @blk.gpsimd
            def _(e):
                for tc in range(NCH):
                    e.wait_ge(vF.h, 3 * tc + 3)
                    e.dma_start(out_h[tc * 128:(tc + 1) * 128, :],
                                out_t[tc % 2][:, :]).then_inc(stO.h, 16)
                e.wait_ge(stO.h, 16 * NCH)
            stO.n += 16 * NCH

            @blk.scalar
            def _(e):
                e.wait_ge(stO.h, 16 * NCH)

    es.close()
    return nc


def make_in_maps(ins, cores=range(8)):
    x = np.asarray(ins["L0_attn_hidden_states"])
    W_in = np.ascontiguousarray(np.asarray(ins["L0_attn_W_in"], dtype=np.float32))
    conv_w = np.asarray(ins["L0_attn_conv_w"], dtype=np.float32)
    convw_t = np.ascontiguousarray(conv_w.reshape(NXB, 128, KCONV)
                                   .transpose(1, 0, 2).reshape(128, NXB * KCONV))
    dtb = np.ascontiguousarray(np.asarray(ins["L0_attn_dt_bias"],
                                          dtype=np.float32).reshape(1, NH))
    alog = np.ascontiguousarray(np.asarray(ins["L0_attn_A_log"],
                                           dtype=np.float32).reshape(1, NH))
    dpar = np.ascontiguousarray(np.asarray(ins["L0_attn_D_param"],
                                           dtype=np.float32).reshape(1, NH))
    nwt = np.ascontiguousarray(np.asarray(ins["L0_attn_norm_weight"],
                                          dtype=np.float32).reshape(NZB, 128).T)
    w_out = np.ascontiguousarray(np.asarray(ins["L0_attn_W_out"],
                                            dtype=np.float32))
    wf_t = np.ascontiguousarray(np.asarray(ins["L0_ffn_weight"],
                                           dtype=np.float32).T)
    bias_t = np.ascontiguousarray(np.asarray(ins["L0_ffn_bias"],
                                             dtype=np.float32).reshape(32, 128).T)
    la_t = np.ascontiguousarray(np.asarray(ins["L0_ffn_lora_A"], dtype=np.float32)
                                .transpose(2, 0, 1).reshape(H, 48))
    lb_t = np.ascontiguousarray(np.asarray(ins["L0_ffn_lora_B"], dtype=np.float32)
                                .transpose(0, 2, 1).reshape(48, H))
    lsi = np.ascontiguousarray(
        np.asarray(ins["L0_ffn_lora_start_indices"]).astype(np.int32).reshape(1, 4))
    maps = []
    for b in cores:
        maps.append({
            "x": np.ascontiguousarray(x[b]).astype(np.float32),
            "w_in": W_in, "conv_w_t": convw_t, "dtb": dtb, "alog": alog,
            "dpar": dpar, "nwt": nwt, "w_out": w_out, "wf_t": wf_t,
            "bias_t": bias_t, "la_t": la_t, "lb_t": lb_t, "lsi": lsi,
            "bval": np.array([[float(b)]], dtype=np.float32),
        })
    return maps


_NC_CACHE = {}


def kernel(**inputs):
    ins = inputs
    if "prog" not in _NC_CACHE:
        _NC_CACHE["prog"] = build(debug=False)
    nc = _NC_CACHE["prog"]
    maps = make_in_maps(ins, cores=range(8))
    from concourse.bass_utils import run_bass_kernel_spmd
    res = run_bass_kernel_spmd(nc, maps, core_ids=list(range(8)))
    out = np.stack([np.asarray(res.results[b]["out"]) for b in range(8)], 0)
    return out.astype(np.float32)
